# version 01
# baseline (speedup 1.0000x reference)
from contextlib import ExitStack
import numpy as np
import ml_dtypes
import concourse.bass as bass
import concourse.mybir as mybir
from concourse.bass_utils import run_bass_kernel_spmd

F32 = mybir.dt.float32
BF16 = mybir.dt.bfloat16
AF = mybir.ActivationFunctionType
ALU = mybir.AluOpType
AX = mybir.AxisListType
BF16NP = ml_dtypes.bfloat16

TOK = 2048
NT = TOK // 128
D = 1024
NH = 16
DH = 64
L = 8192
REL_CON = 4731
REL_TT = 4732


class Ctr:

    def __init__(self, nc, name):
        self.sem = nc.alloc_semaphore(name)
        self.n = 0

    def inc(self, inst, k=1):
        inst.then_inc(self.sem, k)
        self.n += k
        return inst

    def wait(self, eng, n=None):
        eng.wait_ge(self.sem, self.n if n is None else n)


def w_moving(W):
    dout = W.shape[1]
    return np.ascontiguousarray(
        W.reshape(8, 128, dout).transpose(1, 0, 2).reshape(128, 8 * dout)
    ).astype(np.float32)


def w_moving_g(W, kchunks):
    dout = W.shape[1]
    return np.ascontiguousarray(
        W.reshape(kchunks, 128, dout).transpose(1, 0, 2).reshape(128, kchunks * dout)
    ).astype(np.float32)


def b_fm(b):
    return np.ascontiguousarray(np.asarray(b).reshape(8, 128).T).astype(np.float32)


def shard_x(x):
    out = []
    for c in range(8):
        b, t0 = divmod(c, 4)
        out.append(np.ascontiguousarray(x[b, t0 * TOK:(t0 + 1) * TOK]))
    return out


def load_x_tokmaj(ctr, eng, dst_sb, src_hbm, ntiles=NT, d=D):
    for i in range(ntiles):
        ctr.inc(eng.dma_start(out=dst_sb[:, i * d:(i + 1) * d],
                              in_=src_hbm[i * 128:(i + 1) * 128, :]), 16)


def store_x_tokmaj(ctr, eng, dst_hbm, src_sb, ntiles=NT, d=D):
    for i in range(ntiles):
        ctr.inc(eng.dma_start(out=dst_hbm[i * 128:(i + 1) * 128, :],
                              in_=src_sb[:, i * d:(i + 1) * d]), 16)


def build_LA():
    nc = bass.Bass("TRN2", target_bir_lowering=False, detect_race_conditions=False)
    x = nc.dram_tensor("x", [TOK, D], F32, kind="ExternalInput").ap()
    wq = nc.dram_tensor("wq", [128, 8 * D], F32, kind="ExternalInput").ap()
    wk = nc.dram_tensor("wk", [128, 8 * D], F32, kind="ExternalInput").ap()
    wv = nc.dram_tensor("wv", [128, 8 * D], F32, kind="ExternalInput").ap()
    bq = nc.dram_tensor("bq", [128, 8], F32, kind="ExternalInput").ap()
    bk = nc.dram_tensor("bk", [128, 8], F32, kind="ExternalInput").ap()
    bvr = nc.dram_tensor("bvr", [1, D], F32, kind="ExternalInput").ap()
    ones_h = nc.dram_tensor("ones", [1, 128], F32, kind="ExternalInput").ap()
    qt_o = nc.dram_tensor("qt_o", [D, TOK], BF16, kind="ExternalOutput").ap()
    kt_o = nc.dram_tensor("kt_o", [D, TOK], BF16, kind="ExternalOutput").ap()
    v_o = nc.dram_tensor("v_o", [TOK, D], BF16, kind="ExternalOutput").ap()

    with (
        nc.sbuf_tensor("xb", [128, NT * D], BF16) as xb,
        nc.sbuf_tensor("xT", [128, 8 * TOK], BF16) as xT,
        nc.sbuf_tensor("wqs", [128, 8 * D], BF16) as wqs,
        nc.sbuf_tensor("wks", [128, 8 * D], BF16) as wks,
        nc.sbuf_tensor("wvs", [128, 8 * D], BF16) as wvs,
        nc.sbuf_tensor("bqs", [128, 8], F32) as bqs,
        nc.sbuf_tensor("bks", [128, 8], F32) as bks,
        nc.sbuf_tensor("bvrow", [1, D], BF16) as bvrow,
        nc.sbuf_tensor("ones1", [1, 128], BF16) as ones1,
        nc.sbuf_tensor("qt_s", [128, 8 * TOK], BF16) as qt_s,
        nc.sbuf_tensor("kt_s", [128, 8 * TOK], BF16) as kt_s,
        nc.psum_tensor("ps0", [128, 512], F32) as ps0,
        nc.psum_tensor("ps1", [128, 512], F32) as ps1,
        nc.psum_tensor("ps2", [128, 512], F32) as ps2,
        nc.psum_tensor("ps3", [128, 512], F32) as ps3,
        nc.Block() as block,
    ):
        psums = [ps0, ps1, ps2, ps3]
        ld = Ctr(nc, "ld")
        ldg = Ctr(nc, "ldg")
        xc = Ctr(nc, "xc")
        td = Ctr(nc, "td")
        mm = Ctr(nc, "mm")
        evq = Ctr(nc, "evq")
        evv = Ctr(nc, "evv")
        st = Ctr(nc, "st")

        @block.sync
        def _(sync):
            ld.inc(sync.dma_start(out=bqs[:], in_=bq[:, :]), 16)
            ld.inc(sync.dma_start(out=bks[:], in_=bk[:, :]), 16)

        @block.gpsimd
        def _(gp):
            ldg.inc(gp.dma_start(out=wqs[:], in_=wq[:, :]), 16)
            ldg.inc(gp.dma_start(out=wks[:], in_=wk[:, :]), 16)
            ldg.inc(gp.dma_start(out=wvs[:], in_=wv[:, :]), 16)
            ldg.inc(gp.dma_start(out=ones1[:], in_=ones_h[:, :]), 16)
            ldg.inc(gp.dma_start(out=bvrow[:], in_=bvr[:, :]), 16)
            for i in range(NT):
                xc.inc(gp.dma_start(out=xb[:, i * D:(i + 1) * D],
                                    in_=x[i * 128:(i + 1) * 128, :]), 16)

        @block.scalar
        def _(s):
            s.wait_ge(xc.sem, 16 * NT)
            for i in range(NT):
                for kc in range(8):
                    td.inc(s.dma_start_transpose(
                        xT[:, kc * TOK + i * 128: kc * TOK + (i + 1) * 128],
                        xb[:, i * D + kc * 128: i * D + (kc + 1) * 128]), 16)

        nqk = 64
        nv = 32
        @block.tensor
        def _(t):
            t.wait_ge(td.sem, td.n)
            ld.wait(t)
            ldg.wait(t)
            g = 0
            for name, wsb in (("q", wqs), ("k", wks)):
                for oc in range(8):
                    for tc in range(4):
                        if g >= 4:
                            t.wait_ge(evq.sem, g - 3)
                        ps = psums[g % 4]
                        for kc in range(8):
                            inst = t.matmul(ps[:, 0:512],
                                     wsb[:, kc * D + oc * 128: kc * D + (oc + 1) * 128],
                                     xT[:, kc * TOK + tc * 512: kc * TOK + (tc + 1) * 512],
                                     start=(kc == 0), stop=(kc == 7))
                        mm.inc(inst)
                        g += 1
            t.wait_ge(evq.sem, 64)
            for tt in range(NT):
                for dc in range(2):
                    gv = tt * 2 + dc
                    if gv >= 4:
                        t.wait_ge(evv.sem, gv - 3)
                    ps = psums[g % 4]
                    for kc in range(8):
                        t.matmul(ps[:, 0:512],
                                 xT[:, kc * TOK + tt * 128: kc * TOK + (tt + 1) * 128],
                                 wvs[:, kc * D + dc * 512: kc * D + (dc + 1) * 512],
                                 start=(kc == 0), stop=False)
                    inst = t.matmul(ps[:, 0:512], ones1[:, :],
                             bvrow[:, dc * 512:(dc + 1) * 512],
                             start=False, stop=True)
                    mm.inc(inst)
                    g += 1

        @block.scalar
        def _(s):
            g = 0
            for name, bsb in (("q", bqs), ("k", bks)):
                dst = qt_s if name == "q" else kt_s
                for oc in range(8):
                    for tc in range(4):
                        s.wait_ge(mm.sem, g + 1)
                        ps = psums[g % 4]
                        evq.inc(s.activation(
                            out=dst[:, oc * TOK + tc * 512: oc * TOK + (tc + 1) * 512],
                            in_=ps[:, 0:512], func=AF.Identity,
                            bias=bsb[:, oc:oc + 1]))
                        g += 1

        @block.vector
        def _(v):
            v.wait_ge(td.sem, 16 * 8 * NT)
            for tt in range(NT):
                for dc in range(2):
                    g = 64 + tt * 2 + dc
                    v.wait_ge(mm.sem, g + 1)
                    ps = psums[g % 4]
                    evv.inc(v.tensor_copy(
                        xb[:, tt * D + dc * 512: tt * D + (dc + 1) * 512],
                        ps[:, 0:512]))

        @block.sync
        def _(sync):
            sync.wait_ge(evq.sem, evq.n)
            sync.wait_ge(evv.sem, evv.n)
            for oc in range(8):
                st.inc(sync.dma_start(
                    out=qt_o[oc * 128:(oc + 1) * 128, :],
                    in_=qt_s[:, oc * TOK:(oc + 1) * TOK]), 16)
                st.inc(sync.dma_start(
                    out=kt_o[oc * 128:(oc + 1) * 128, :],
                    in_=kt_s[:, oc * TOK:(oc + 1) * TOK]), 16)
            store_x_tokmaj(st, sync, v_o, xb)
            st.wait(sync)
    return nc


def host_LA(inputs, xs):
    nc = build_LA()
    ones = np.ones((1, 128), dtype=np.float32)
    base = {
        "wq": w_moving(np.asarray(inputs["L0_attn_Wq"])),
        "wk": w_moving(np.asarray(inputs["L0_attn_Wk"])),
        "wv": w_moving(np.asarray(inputs["L0_attn_Wv"])),
        "bq": b_fm(inputs["L0_attn_bq"]),
        "bk": b_fm(inputs["L0_attn_bk"]),
        "bvr": np.asarray(inputs["L0_attn_bv"], np.float32).reshape(1, D),
        "ones": ones,
    }
    in_maps = [dict(base, x=xs[c]) for c in range(8)]
    res = run_bass_kernel_spmd(nc, in_maps, core_ids=list(range(8)))
    return res.results



def build_LB(core_id):
    qbase = (core_id % 4) * TOK
    nc = bass.Bass("TRN2", target_bir_lowering=False, detect_race_conditions=False)
    x = nc.dram_tensor("x", [TOK, D], F32, kind="ExternalInput").ap()
    qt = nc.dram_tensor("qt", [D, TOK], BF16, kind="ExternalInput").ap()
    ktb = nc.dram_tensor("ktb", [D, L], BF16, kind="ExternalInput").ap()
    vb = nc.dram_tensor("vb", [L, D], BF16, kind="ExternalInput").ap()
    relp = nc.dram_tensor("relp", [16, REL_TT], F32, kind="ExternalInput").ap()
    ident_h = nc.dram_tensor("ident", [128, 128], F32, kind="ExternalInput").ap()
    x1_o = nc.dram_tensor("x1_o", [TOK, D], F32, kind="ExternalOutput").ap()
    dbg_o = nc.dram_tensor("dbg_o", [TOK, D], F32, kind="ExternalOutput").ap()

    NKT = L // 128

    def tailp(qbp, qb, kt):
        q0 = qbase + qbp * 1024 + qb * 512
        return kt * 128 >= q0 + 3971

    steps = [(kt, qb) for kt in range(NKT) for qb in range(2)]
    NS = len(steps)

    pt_prereq = []
    exp_cnt = 0
    tad_cnt = 0
    tidx_total = 0
    for pair in range(8):
        for qbp in range(2):
            for (kt, qb) in steps:
                for hh in range(2):
                    exp_cnt += 1
                    if tailp(qbp, qb, kt):
                        tad_cnt += 1
                        pt_prereq.append(("tad", tad_cnt))
                        tidx_total += 1
                    else:
                        pt_prereq.append(("expd", exp_cnt))

    with ExitStack() as ctx:
        sb = lambda name, shape, dt: ctx.enter_context(nc.sbuf_tensor(name, shape, dt))
        pt = lambda name, shape: ctx.enter_context(nc.psum_tensor(name, shape, F32))
        qt_s = sb("qt_s", [128, 8 * TOK], BF16)
        osb = sb("osb", [128, NT * D], F32)
        rels = sb("rels", [16, REL_TT], F32)
        wt = sb("wt", [16, REL_TT], BF16)
        negbh = sb("negbh", [16, 1], F32)
        ident = sb("ident_s", [128, 128], F32)
        kfl = sb("kfl", [128, 4 * 128], BF16)
        vfl = sb("vfl", [128, 4 * 130], BF16)
        ptb = sb("ptb", [128, 8 * 512], BF16)
        ttl = sb("ttl", [128, 2 * 512], BF16)
        oTs = sb("oT", [65, 2 * 512], F32)
        rs = sb("rs", [128, 16], F32)
        xl = sb("xl", [128, 2 * D], F32)
        yl = sb("yl", [128, 2 * D], F32)
        sqs = sb("sqs", [128, D], F32)
        mst = sb("mst", [128, 8], F32)
        ps_st0 = pt("ps_st0", [128, 512])
        ps_st1 = pt("ps_st1", [128, 512])
        ps_av0 = pt("ps_av0", [65, 512])
        ps_av1 = pt("ps_av1", [65, 512])
        ps_av2 = pt("ps_av2", [65, 512])
        ps_av3 = pt("ps_av3", [65, 512])
        ps_tr0 = pt("ps_tr0", [128, 65])
        ps_tr1 = pt("ps_tr1", [128, 65])
        block = ctx.enter_context(nc.Block())
        ps_avs = [ps_av0, ps_av1, ps_av2, ps_av3]
        ps_trs = [ps_tr0, ps_tr1]
        ps_sts = [ps_st0, ps_st1]
        ld = Ctr(nc, "ld")
        vin = Ctr(nc, "vin")
        wtd = Ctr(nc, "wtd")
        kv = [Ctr(nc, f"kv{i}") for i in range(4)]
        std = Ctr(nc, "std")
        expd = Ctr(nc, "expd")
        tld = [Ctr(nc, f"tld{i}") for i in range(2)]
        tad = Ctr(nc, "tad")
        avd = Ctr(nc, "avd")
        oev = Ctr(nc, "oev")
        trd = Ctr(nc, "trd")
        osc = Ctr(nc, "osc")
        rec = Ctr(nc, "rec")
        xld = [Ctr(nc, f"xld{i}") for i in range(2)]
        lnv = Ctr(nc, "lnv")
        lns = Ctr(nc, "lns")
        st = Ctr(nc, "st")

        @block.sync
        def _(sync):
            for oc in range(8):
                ld.inc(sync.dma_start(
                    out=qt_s[:, oc * TOK:(oc + 1) * TOK],
                    in_=qt[oc * 128:(oc + 1) * 128, :]), 16)
            ld.inc(sync.dma_start(out=rels[:], in_=relp[:, :]), 16)
            ld.inc(sync.dma_start(out=ident[:], in_=ident_h[:, :]), 16)

        @block.vector
        def _(v):
            ld.wait(v)
            for s in range(4):
                v.memset(vfl[:, s * 130 + 64: s * 130 + 65], 1.0)
                v.memset(vfl[:, s * 130 + 129: s * 130 + 130], 1.0)
            vin.inc(v.tensor_scalar_mul(negbh[:], rels[:, 637:638], -1.0))

        @block.scalar
        def _(s):
            s.wait_ge(vin.sem, 1)
            wtd.inc(s.activation(out=wt[:], in_=rels[:], func=AF.Exp,
                                 bias=negbh[:]))

        @block.sync
        def _(sync):
            sync.wait_ge(wtd.sem, 1)
            sync.wait_ge(vin.sem, 1)
            g = 0
            tidx = 0
            for pair in range(8):
                for qbp in range(2):
                    for kt in range(NKT):
                        slot = g % 4
                        if g >= 4:
                            sync.wait_ge(std.sem, 4 * (g - 3))
                            sync.wait_ge(avd.sem, 4 * (g - 3))
                        kv[slot].inc(sync.dma_start(
                            out=kfl[:, slot * 128:(slot + 1) * 128],
                            in_=ktb[pair * 128:(pair + 1) * 128,
                                    kt * 128:(kt + 1) * 128]), 16)
                        dstv = bass.AP(tensor=vfl[:].tensor,
                                       offset=slot * 130,
                                       ap=[[4 * 130, 128], [65, 2], [1, 64]])
                        srcv = bass.AP(tensor=vb.tensor,
                                       offset=kt * 128 * D + pair * 128,
                                       ap=[[D, 128], [64, 2], [1, 64]])
                        kv[slot].inc(sync.dma_start(out=dstv, in_=srcv), 16)
                        for qb in range(2):
                            if tailp(qbp, qb, kt):
                                for hh in range(2):
                                    h = pair * 2 + hh
                                    ts = tidx % 2
                                    if tidx >= 2:
                                        sync.wait_ge(tad.sem, tidx - 1)
                                    q0 = qbase + qbp * 1024 + qb * 512
                                    base = kt * 128 - q0 - 3971
                                    src = bass.AP(
                                        tensor=wt[:].tensor,
                                        offset=h * REL_TT + base,
                                        ap=[[REL_TT, 1], [1, 128], [1, 512]])
                                    tld[ts].inc(sync.dma_start(
                                        out=ttl[:, ts * 512:(ts + 1) * 512],
                                        in_=src), 16)
                                    tidx += 1
                        g += 1

        @block.tensor
        def _(t):
            ld.wait(t)
            pt_idx = 0
            av_idx = 0
            oev_cnt = 0
            osc_cnt = 0
            g = 0

            def emit_st(pair, qbp, kt, qb, g):
                nonlocal pt_idx
                slot = g % 4
                t.wait_ge(kv[slot].sem, 32 * (g // 4 + 1))
                for hh in range(2):
                    prev = pt_idx + hh - 2
                    if prev >= 0:
                        t.wait_ge(expd.sem, prev + 1)
                    std.inc(t.matmul(
                        ps_sts[hh][:, 0:512],
                        kfl[hh * 64:(hh + 1) * 64, slot * 128:(slot + 1) * 128],
                        qt_s[hh * 64:(hh + 1) * 64,
                             pair * TOK + qbp * 1024 + qb * 512:
                             pair * TOK + qbp * 1024 + (qb + 1) * 512],
                        start=True, stop=True,
                        tile_position=(hh * 64, 0)))
                pt_idx += 2

            def emit_av(pair, qbp, kt, qb):
                nonlocal av_idx
                slot = (av_idx // 4) % 4
                for hh in range(2):
                    kind, cnt = pt_prereq[av_idx]
                    t.wait_ge((tad if kind == "tad" else expd).sem, cnt)
                    ptslot = av_idx % 8
                    avd.inc(t.matmul(
                        ps_avs[qb * 2 + hh][0:65, 0:512],
                        vfl[:, slot * 130 + hh * 65: slot * 130 + (hh + 1) * 65],
                        ptb[:, ptslot * 512:(ptslot + 1) * 512],
                        start=(kt == 0), stop=(kt == NKT - 1)))
                    av_idx += 1

            for pair in range(8):
                for qbp in range(2):
                    if oev_cnt > 0:
                        t.wait_ge(oev.sem, oev_cnt)
                    prev_step = None
                    for (kt, qb) in steps:
                        emit_st(pair, qbp, kt, qb, g + (0 if qb == 0 else 0))
                        if qb == 1:
                            g += 1
                        if prev_step is not None:
                            emit_av(pair, qbp, *prev_step)
                        prev_step = (kt, qb)
                    emit_av(pair, qbp, *prev_step)
                    for qb in range(2):
                        for hh in range(2):
                            idx = qb * 2 + hh
                            t.wait_ge(oev.sem, oev_cnt + idx + 1)
                            for j in range(4):
                                jj = (idx * 4 + j)
                                if jj >= 2:
                                    t.wait_ge(osc.sem, osc_cnt + jj - 1)
                                trd.inc(t.transpose(
                                    ps_trs[jj % 2][:, 0:65],
                                    oTs[0:65, (idx % 2) * 512 + j * 128:
                                        (idx % 2) * 512 + (j + 1) * 128],
                                    ident[0:65, 0:65]))
                    oev_cnt += 4
                    osc_cnt += 16

        @block.scalar
        def _(s):
            pt_idx = 0
            for pair in range(8):
                for qbp in range(2):
                    for (kt, qb) in steps:
                        for hh in range(2):
                            s.wait_ge(std.sem, pt_idx + 1)
                            if pt_idx >= 8:
                                s.wait_ge(avd.sem, pt_idx - 7)
                            slot = pt_idx % 8
                            expd.inc(s.activation(
                                out=ptb[:, slot * 512:(slot + 1) * 512],
                                in_=ps_sts[hh][:, 0:512],
                                func=AF.Exp, scale=0.125))
                            pt_idx += 1

        @block.vector
        def _(v):
            pt_idx = 0
            tidx = 0
            trd_cnt = 0
            for pi in range(16):
                pair, qbp = divmod(pi, 2)
                for (kt, qb) in steps:
                    for hh in range(2):
                        if tailp(qbp, qb, kt):
                            ts = tidx % 2
                            v.wait_ge(expd.sem, pt_idx + 1)
                            v.wait_ge(tld[ts].sem, 16 * (tidx // 2 + 1))
                            slot = pt_idx % 8
                            rev = bass.AP(tensor=ttl[:].tensor,
                                          offset=ts * 512 + 511,
                                          ap=[[2 * 512, 128], [-1, 512]])
                            tad.inc(v.tensor_tensor(
                                ptb[:, slot * 512:(slot + 1) * 512],
                                ptb[:, slot * 512:(slot + 1) * 512],
                                rev, ALU.mult))
                            tidx += 1
                        pt_idx += 1
                av_total = (pi + 1) * 2 * NS
                for qb in range(2):
                    for hh in range(2):
                        idx = qb * 2 + hh
                        v.wait_ge(avd.sem, av_total)
                        oev.inc(v.tensor_copy(
                            oTs[0:65, (idx % 2) * 512:(idx % 2 + 1) * 512],
                            ps_avs[idx][0:65, 0:512]))
                        for j in range(4):
                            trd_cnt += 1
                            v.wait_ge(trd.sem, trd_cnt)
                            ps = ps_trs[(idx * 4 + j) % 2]
                            rc_ = osc.n % 16
                            if osc.n >= 16:
                                v.wait_ge(osc.sem, osc.n - 15)
                            rec.inc(v.reciprocal(rs[:, rc_:rc_ + 1], ps[:, 64:65]))
                            v.wait_ge(rec.sem, rec.n)
                            ti = qbp * 8 + qb * 4 + j
                            hcol = (pair * 2 + hh) * 64
                            osc.inc(v.tensor_scalar_mul(
                                osb[:, ti * D + hcol: ti * D + hcol + 64],
                                ps[:, 0:64], rs[:, rc_:rc_ + 1]))

        @block.sync
        def _(sync):
            sync.wait_ge(osc.sem, osc.n)
            for i in range(NT):
                sl = i % 2
                if i >= 2:
                    sync.wait_ge(lns.sem, (i - 2) * 3 + 3)
                    st.inc(sync.dma_start(
                        out=x1_o[(i - 2) * 128:(i - 1) * 128, :],
                        in_=yl[:, sl * D:(sl + 1) * D]), 16)
                    sync.wait_ge(lnv.sem, (i - 2) * 11 + 11)
                xld[sl].inc(sync.dma_start(
                    out=xl[:, sl * D:(sl + 1) * D],
                    in_=x[i * 128:(i + 1) * 128, :]), 16)
            for i in (NT - 2, NT - 1):
                sync.wait_ge(lns.sem, i * 3 + 3)
                st.inc(sync.dma_start(
                    out=x1_o[i * 128:(i + 1) * 128, :],
                    in_=yl[:, (i % 2) * D:(i % 2 + 1) * D]), 16)
            store_x_tokmaj(st, sync, dbg_o, osb)
            st.wait(sync)

        @block.vector
        def _(v):
            def ch(inst):
                lnv.inc(inst)
                v.wait_ge(lnv.sem, lnv.n)
                return inst
            for i in range(NT):
                sl = i % 2
                v.wait_ge(xld[sl].sem, 16 * (i // 2 + 1))
                if i >= 2:
                    v.wait_ge(st.sem, 16 * (i - 1))
                ch(v.tensor_add(yl[:, sl * D:(sl + 1) * D],
                                xl[:, sl * D:(sl + 1) * D],
                                osb[:, i * D:(i + 1) * D]))
                ch(v.tensor_reduce(mst[:, 0:1], yl[:, sl * D:(sl + 1) * D],
                                   axis=AX.X, op=ALU.add))
                ch(v.memset(mst[:, 2:3], 0.0))
                ch(v.tensor_scalar_mul(mst[:, 1:2], mst[:, 0:1],
                                       1.0 / D))
                v.wait_ge(lns.sem, i * 3 + 1)
                ch(v.tensor_scalar_mul(mst[:, 3:4], mst[:, 2:3], 1.0 / D))
                ch(v.tensor_tensor(mst[:, 4:5], mst[:, 1:2], mst[:, 1:2],
                                   ALU.mult))
                ch(v.tensor_sub(mst[:, 4:5], mst[:, 3:4], mst[:, 4:5]))
                ch(v.tensor_scalar_add(mst[:, 4:5], mst[:, 4:5], 1e-5))
                v.wait_ge(lns.sem, i * 3 + 2)
                ch(v.reciprocal(mst[:, 6:7], mst[:, 5:6]))
                ch(v.tensor_tensor(mst[:, 7:8], mst[:, 1:2], mst[:, 6:7],
                                   ALU.mult))
                ch(v.tensor_scalar_mul(mst[:, 7:8], mst[:, 7:8], -1.0))

        @block.scalar
        def _(s):
            for i in range(NT):
                sl = i % 2
                s.wait_ge(lnv.sem, i * 11 + 4)
                lns.inc(s.activation(out=sqs[:], in_=yl[:, sl * D:(sl + 1) * D],
                                     func=AF.Square, accum_out=mst[:, 2:3]))
                s.wait_ge(lnv.sem, i * 11 + 8)
                lns.inc(s.activation(out=mst[:, 5:6], in_=mst[:, 4:5],
                                     func=AF.Sqrt))
                s.wait_ge(lnv.sem, i * 11 + 11)
                lns.inc(s.activation(out=yl[:, sl * D:(sl + 1) * D],
                                     in_=yl[:, sl * D:(sl + 1) * D],
                                     func=AF.Identity,
                                     bias=mst[:, 7:8], scale=mst[:, 6:7]))

    return nc



def build_LC(gwb, swb):
    nc = bass.Bass("TRN2", target_bir_lowering=False,
                   detect_race_conditions=False, dynamic_dma_scratch_size=4096)
    x1h = nc.dram_tensor("x1h", [TOK, D], F32, kind="ExternalInput").ap()
    plh = nc.dram_tensor("plh", [1, D], F32, kind="ExternalInput").ap()
    wrh = nc.dram_tensor("wrh", [128, 8 * D], F32, kind="ExternalInput").ap()
    wkh = nc.dram_tensor("wkh", [128, 8 * D], F32, kind="ExternalInput").ap()
    wvh = nc.dram_tensor("wvh", [128, 8 * D], F32, kind="ExternalInput").ap()
    murh = nc.dram_tensor("murh", [128, D], F32, kind="ExternalInput").ap()
    mukh = nc.dram_tensor("mukh", [128, D], F32, kind="ExternalInput").ap()
    q1wh = nc.dram_tensor("q1wh", [128, 8 * D], F32, kind="ExternalInput").ap()
    k1wh = nc.dram_tensor("k1wh", [128, 8 * D], F32, kind="ExternalInput").ap()
    k2wh = nc.dram_tensor("k2wh", [128, 8 * D], F32, kind="ExternalInput").ap()
    q1bh = nc.dram_tensor("q1bh", [1, D], F32, kind="ExternalInput").ap()
    k1bh = nc.dram_tensor("k1bh", [1, D], F32, kind="ExternalInput").ap()
    k2bh = nc.dram_tensor("k2bh", [1, D], F32, kind="ExternalInput").ap()
    wgbdh = nc.dram_tensor("wgbdh", [128, 8 * 16], F32, kind="ExternalInput").ap()
    wsbdh = nc.dram_tensor("wsbdh", [128, 8 * 16], F32, kind="ExternalInput").ap()
    onesh = nc.dram_tensor("onesh", [1, 128], F32, kind="ExternalInput").ap()
    identh = nc.dram_tensor("identh", [128, 128], F32, kind="ExternalInput").ap()
    gswbh = nc.dram_tensor("gswbh", [128, 2], F32, kind="ExternalInput").ap()
    x2_o = nc.dram_tensor("x2_o", [TOK, D], F32, kind="ExternalOutput").ap()
    q1_o = nc.dram_tensor("q1_o", [TOK, D], BF16, kind="ExternalOutput").ap()
    k1_o = nc.dram_tensor("k1_o", [TOK, D], BF16, kind="ExternalOutput").ap()
    k2_o = nc.dram_tensor("k2_o", [TOK, D], BF16, kind="ExternalOutput").ap()
    rs_o = nc.dram_tensor("rs_o", [TOK, 16], F32, kind="ExternalOutput").ap()
    lgt_o = nc.dram_tensor("lgt_o", [16, TOK], F32, kind="ExternalOutput").ap()
    gsum_o = nc.dram_tensor("gsum_o", [16, 1], F32, kind="ExternalOutput").ap()

    with ExitStack() as ctx:
        sb = lambda name, shape, dt: ctx.enter_context(nc.sbuf_tensor(name, shape, dt))
        pt = lambda name, shape: ctx.enter_context(nc.psum_tensor(name, shape, F32))
        wr = sb("wr", [128, 8 * D], BF16)
        wk = sb("wk", [128, 8 * D], BF16)
        wv = sb("wv", [128, 8 * D], BF16)
        q1w = sb("q1w", [128, 8 * D], BF16)
        k1w = sb("k1w", [128, 8 * D], BF16)
        k2w = sb("k2w", [128, 8 * D], BF16)
        wgbd = sb("wgbd", [128, 8 * 16], BF16)
        wsbd = sb("wsbd", [128, 8 * 16], BF16)
        mur = sb("mur", [128, D], F32)
        muk = sb("muk", [128, D], F32)
        q1b = sb("q1b", [1, D], BF16)
        k1br = sb("k1br", [1, D], BF16)
        k2br = sb("k2br", [1, D], BF16)
        ones1 = sb("ones1", [1, 128], BF16)
        ident = sb("ident_s", [128, 128], F32)
        gswb = sb("gswb", [128, 2], F32)
        xs_ = sb("xs_", [128, 2 * D], F32)
        xp_ = sb("xp_", [128, 2 * D], F32)
        db_ = sb("db_", [128, 2 * D], F32)
        tp_ = sb("tp_", [128, 2 * D], F32)
        xrb = sb("xrb", [128, 2 * D], BF16)
        xkb = sb("xkb", [128, 2 * D], BF16)
        kab = sb("kab", [128, 2 * D], BF16)
        rf_ = sb("rf_", [128, 2 * D], F32)
        vo_ = sb("vo_", [128, 2 * D], F32)
        yl_ = sb("yl_", [128, 2 * D], F32)
        x2b = sb("x2b", [128, 2 * D], BF16)
        q1s = sb("q1s", [128, 2 * D], BF16)
        k1s = sb("k1s", [128, 2 * D], BF16)
        k2s = sb("k2s", [128, 2 * D], BF16)
        gf_ = sb("gf_", [128, 2 * 16], F32)
        lgf = sb("lgf", [128, 2 * 16], F32)
        rsl = sb("rsl", [128, 2 * 16], F32)
        xrT = sb("xrT", [128, 2 * D], BF16)
        xkT = sb("xkT", [128, 2 * D], BF16)
        kaT = sb("kaT", [128, 2 * D], BF16)
        x2T = sb("x2T", [128, 2 * D], BF16)
        k1T = sb("k1T", [128, 2 * D], BF16)
        mst = sb("mst", [128, 8], F32)
        lgS = sb("lgS", [16, 2 * 128], F32)
        gacc = sb("gacc", [16, 2], F32)
        psA = pt("psA", [128, 512])
        psB = pt("psB", [128, 512])
        psC = pt("psC", [128, 512])
        psD = pt("psD", [128, 512])
        psG = pt("psG", [128, 16])
        psR = pt("psR", [128, 16])
        psT = pt("psT", [16, 2 * 128])
        block = ctx.enter_context(nc.Block())

        ld = Ctr(nc, "ld")
        ldg = Ctr(nc, "ldg")
        xin = [Ctr(nc, "xin0"), Ctr(nc, "xin1")]
        txr = [Ctr(nc, "txr0"), Ctr(nc, "txr1")]
        txk = [Ctr(nc, "txk0"), Ctr(nc, "txk1")]
        tka = [Ctr(nc, "tka0"), Ctr(nc, "tka1")]
        tx2 = [Ctr(nc, "tx20"), Ctr(nc, "tx21")]
        tk1 = [Ctr(nc, "tk10"), Ctr(nc, "tk11")]
        lct = Ctr(nc, "lct")
        lcp = Ctr(nc, "lcp")
        stl = Ctr(nc, "stl")
        lcs = Ctr(nc, "lcs")
        lcv = Ctr(nc, "lcv")
        st = Ctr(nc, "st")

        NVT = 28
        NSC = 11
        NTT = 15
        VD, VMR, VXR, VMK, VXK = 1, 2, 3, 4, 5
        VKR0, VKR1, VVO0, VVO1, VY = 6, 7, 8, 9, 10
        VX2B = 21
        VQ10, VQ11, VK10, VK11 = 22, 23, 24, 25
        VGCL = 26
        SR0, SR1, SKA = 1, 2, 3
        SSQ, SSQRT, SFIN = 4, 5, 6
        SK20, SK21, SG, SLN, SRS = 7, 8, 9, 10, 11
        TR0, TR1, TK0, TK1 = 1, 2, 3, 4
        TV0, TV1, TQ0, TQ1 = 5, 6, 7, 8
        TKA1, TKB1, TK2A, TK2B = 9, 10, 11, 12
        TG, TRS, TLG = 13, 14, 15

        XIN_PER = 3 * 16

        @block.sync
        def _(sync):
            ld.inc(sync.dma_start(out=mur[:], in_=murh[:, :]), 16)
            ld.inc(sync.dma_start(out=muk[:], in_=mukh[:, :]), 16)
            ld.inc(sync.dma_start(out=ident[:], in_=identh[:, :]), 16)
            ld.inc(sync.dma_start(out=gswb[:], in_=gswbh[:, :]), 16)

            def stores(i):
                s_ = i % 2
                sync.wait_ge(lcs.sem, i * NSC + SFIN)
                st.inc(sync.dma_start(out=x2_o[i * 128:(i + 1) * 128, :],
                                      in_=yl_[:, s_ * D:(s_ + 1) * D]), 16)
                sync.wait_ge(lcv.sem, 1 + i * NVT + 23)
                st.inc(sync.dma_start(out=q1_o[i * 128:(i + 1) * 128, :],
                                      in_=q1s[:, s_ * D:(s_ + 1) * D]), 16)
                sync.wait_ge(lcv.sem, 1 + i * NVT + 25)
                st.inc(sync.dma_start(out=k1_o[i * 128:(i + 1) * 128, :],
                                      in_=k1s[:, s_ * D:(s_ + 1) * D]), 16)
                sync.wait_ge(lcs.sem, i * NSC + SK21)
                st.inc(sync.dma_start(out=k2_o[i * 128:(i + 1) * 128, :],
                                      in_=k2s[:, s_ * D:(s_ + 1) * D]), 16)
                sync.wait_ge(lcs.sem, i * NSC + SRS)
                st.inc(sync.dma_start(out=rs_o[i * 128:(i + 1) * 128, :],
                                      in_=rsl[:, s_ * 16:(s_ + 1) * 16]), 16)
                sync.wait_ge(lcp.sem, i + 1)
                stl.inc(sync.dma_start(out=lgt_o[:, i * 128:(i + 1) * 128],
                                       in_=lgS[:, s_ * 128:(s_ + 1) * 128]), 16)

            for i in range(NT):
                s_ = i % 2
                off = lambda k: 1 + i * NVT + k
                offs = lambda k: i * NSC + k
                offt_prev = lambda k: (i - 2) * NTT + k
                if i >= 2:
                    stores(i - 2)
                    sync.wait_ge(lcv.sem, 1 + (i - 2) * NVT + VY)
                xin[s_].inc(sync.dma_start(
                    out=xs_[:, s_ * D:(s_ + 1) * D],
                    in_=x1h[i * 128:(i + 1) * 128, :]), 16)
                if i == 0:
                    xin[s_].inc(sync.dma_start(
                        out=xp_[0:1, s_ * D:(s_ + 1) * D], in_=plh[0:1, :]), 16)
                    xin[s_].inc(sync.dma_start(
                        out=xp_[1:128, s_ * D:(s_ + 1) * D],
                        in_=x1h[0:127, :]), 16)
                else:
                    xin[s_].inc(sync.dma_start(
                        out=xp_[1:128, s_ * D:(s_ + 1) * D],
                        in_=x1h[i * 128:(i + 1) * 128 - 1, :]), 16)
                    xin[s_].inc(sync.dma_start(
                        out=xp_[0:1, s_ * D:(s_ + 1) * D],
                        in_=x1h[i * 128 - 1:i * 128, :]), 16)
                if i >= 2:
                    sync.wait_ge(lct.sem, offt_prev(TR1))
                sync.wait_ge(lcv.sem, off(VXR))
                for kc in range(8):
                    txr[s_].inc(sync.dma_start_transpose(
                        xrT[:, s_ * D + kc * 128: s_ * D + (kc + 1) * 128],
                        xrb[:, s_ * D + kc * 128: s_ * D + (kc + 1) * 128]), 16)
                if i >= 2:
                    sync.wait_ge(lct.sem, offt_prev(TK1))
                sync.wait_ge(lcv.sem, off(VXK))
                for kc in range(8):
                    txk[s_].inc(sync.dma_start_transpose(
                        xkT[:, s_ * D + kc * 128: s_ * D + (kc + 1) * 128],
                        xkb[:, s_ * D + kc * 128: s_ * D + (kc + 1) * 128]), 16)
                if i >= 2:
                    sync.wait_ge(lct.sem, offt_prev(TV1))
                sync.wait_ge(lcs.sem, offs(SKA))
                for kc in range(8):
                    tka[s_].inc(sync.dma_start_transpose(
                        kaT[:, s_ * D + kc * 128: s_ * D + (kc + 1) * 128],
                        kab[:, s_ * D + kc * 128: s_ * D + (kc + 1) * 128]), 16)
                if i >= 2:
                    sync.wait_ge(lct.sem, offt_prev(TG))
                sync.wait_ge(lcv.sem, off(VX2B))
                for kc in range(8):
                    tx2[s_].inc(sync.dma_start_transpose(
                        x2T[:, s_ * D + kc * 128: s_ * D + (kc + 1) * 128],
                        x2b[:, s_ * D + kc * 128: s_ * D + (kc + 1) * 128]), 16)
                if i >= 2:
                    sync.wait_ge(lct.sem, offt_prev(TRS))
                sync.wait_ge(lcv.sem, off(VK11))
                for kc in range(8):
                    tk1[s_].inc(sync.dma_start_transpose(
                        k1T[:, s_ * D + kc * 128: s_ * D + (kc + 1) * 128],
                        k1s[:, s_ * D + kc * 128: s_ * D + (kc + 1) * 128]), 16)
            stores(NT - 2)
            stores(NT - 1)
            sync.wait_ge(lcv.sem, 1 + NT * NVT)
            st.inc(sync.dma_start(out=gsum_o[:, :], in_=gacc[:, 0:1]), 16)
            st.wait(sync)
            stl.wait(sync)

        @block.gpsimd
        def _(gp):
            for dst, src in ((wr, wrh), (wk, wkh), (wv, wvh), (q1w, q1wh),
                             (k1w, k1wh), (k2w, k2wh), (wgbd, wgbdh),
                             (wsbd, wsbdh), (q1b, q1bh), (k1br, k1bh),
                             (k2br, k2bh), (ones1, onesh)):
                ldg.inc(gp.dma_start(out=dst[:], in_=src[:, :]), 16)

        @block.tensor
        def _(t):
            ld.wait(t)
            ldg.wait(t)
            for i in range(NT):
                s_ = i % 2
                lap = i // 2 + 1

                def proj(ps, xTs, w_sb, dc, n=512, bias=None):
                    wd = w_sb.shape[1] // 8
                    last = None
                    for kc in range(8):
                        last = t.matmul(
                            ps[0:128, 0:n],
                            xTs[:, s_ * D + kc * 128: s_ * D + (kc + 1) * 128],
                            w_sb[:, kc * wd + dc * 512: kc * wd + dc * 512 + n],
                            start=(kc == 0),
                            stop=(kc == 7 and bias is None))
                    if bias is not None:
                        last = t.matmul(ps[0:128, 0:n], ones1[:, :],
                                        bias[:, dc * 512: dc * 512 + n],
                                        start=False, stop=True)
                    lct.inc(last)

                t.wait_ge(txr[s_].sem, 128 * lap)
                if i >= 1:
                    t.wait_ge(lcs.sem, (i - 1) * NSC + SR1)
                proj(psA, xrT, wr, 0)
                proj(psB, xrT, wr, 1)
                t.wait_ge(txk[s_].sem, 128 * lap)
                if i >= 1:
                    t.wait_ge(lcv.sem, 1 + (i - 1) * NVT + VKR1)
                proj(psC, xkT, wk, 0)
                proj(psD, xkT, wk, 1)
                t.wait_ge(tka[s_].sem, 128 * lap)
                t.wait_ge(lcs.sem, i * NSC + SR1)
                proj(psA, kaT, wv, 0)
                proj(psB, kaT, wv, 1)
                t.wait_ge(tx2[s_].sem, 128 * lap)
                t.wait_ge(lcv.sem, 1 + i * NVT + VKR1)
                proj(psC, x2T, q1w, 0, bias=q1b)
                proj(psD, x2T, q1w, 1, bias=q1b)
                t.wait_ge(lcv.sem, 1 + i * NVT + VVO1)
                proj(psA, x2T, k1w, 0, bias=k1br)
                proj(psB, x2T, k1w, 1, bias=k1br)
                t.wait_ge(lcv.sem, 1 + i * NVT + VQ11)
                proj(psC, x2T, k2w, 0, bias=k2br)
                proj(psD, x2T, k2w, 1, bias=k2br)
                if i >= 1:
                    t.wait_ge(lcs.sem, (i - 1) * NSC + SG)
                proj(psG, x2T, wgbd, 0, n=16)
                t.wait_ge(tk1[s_].sem, 128 * lap)
                if i >= 1:
                    t.wait_ge(lcs.sem, (i - 1) * NSC + SRS)
                proj(psR, k1T, wsbd, 0, n=16)
                t.wait_ge(lcs.sem, i * NSC + SLN)
                if i >= 2:
                    t.wait_ge(lcp.sem, i - 1)
                lct.inc(t.transpose(psT[0:16, s_ * 128:(s_ + 1) * 128],
                                    lgf[:, s_ * 16:(s_ + 1) * 16],
                                    ident[:, :]))

        @block.scalar
        def _(s):
            for i in range(NT):
                s_ = i % 2
                offv = lambda k: 1 + i * NVT + k
                offt = lambda k: i * NTT + k
                if i >= 2:
                    s.wait_ge(st.sem, 80 * (i - 1))
                s.wait_ge(lct.sem, offt(TR1))
                lcs.inc(s.activation(out=rf_[:, s_ * D: s_ * D + 512],
                                     in_=psA[:, 0:512], func=AF.Sigmoid))
                lcs.inc(s.activation(out=rf_[:, s_ * D + 512: s_ * D + 1024],
                                     in_=psB[:, 0:512], func=AF.Sigmoid))
                s.wait_ge(lcv.sem, offv(VKR1))
                lcs.inc(s.activation(out=kab[:, s_ * D:(s_ + 1) * D],
                                     in_=kab[:, s_ * D:(s_ + 1) * D],
                                     func=AF.Square))
                s.wait_ge(lcv.sem, offv(VY + 3))
                lcs.inc(s.activation(out=vo_[:, s_ * D:(s_ + 1) * D],
                                     in_=yl_[:, s_ * D:(s_ + 1) * D],
                                     func=AF.Square, accum_out=mst[:, 2:3]))
                s.wait_ge(lcv.sem, offv(VY + 7))
                lcs.inc(s.activation(out=mst[:, 5:6], in_=mst[:, 4:5],
                                     func=AF.Sqrt))
                s.wait_ge(lcv.sem, offv(VY + 10))
                lcs.inc(s.activation(out=yl_[:, s_ * D:(s_ + 1) * D],
                                     in_=yl_[:, s_ * D:(s_ + 1) * D],
                                     func=AF.Identity,
                                     bias=mst[:, 7:8], scale=mst[:, 6:7]))
                s.wait_ge(lct.sem, offt(TK2B))
                lcs.inc(s.activation(out=k2s[:, s_ * D: s_ * D + 512],
                                     in_=psC[:, 0:512], func=AF.Sigmoid,
                                     scale=0.000625))
                lcs.inc(s.activation(out=k2s[:, s_ * D + 512: s_ * D + 1024],
                                     in_=psD[:, 0:512], func=AF.Sigmoid,
                                     scale=0.000625))
                s.wait_ge(lct.sem, offt(TG))
                lcs.inc(s.activation(out=gf_[:, s_ * 16:(s_ + 1) * 16],
                                     in_=psG[:, 0:16], func=AF.Sigmoid,
                                     bias=gswb[:, 0:1]))
                s.wait_ge(lcv.sem, offv(VGCL))
                if i >= 2:
                    s.wait_ge(lct.sem, (i - 2) * NTT + TLG)
                lcs.inc(s.activation(out=lgf[:, s_ * 16:(s_ + 1) * 16],
                                     in_=gf_[:, s_ * 16:(s_ + 1) * 16],
                                     func=AF.Ln))
                s.wait_ge(lct.sem, offt(TRS))
                lcs.inc(s.activation(out=rsl[:, s_ * 16:(s_ + 1) * 16],
                                     in_=psR[:, 0:16], func=AF.Silu,
                                     bias=gswb[:, 1:2]))
                s.wait_ge(lct.sem, offt(TLG))
                if i >= 2:
                    s.wait_ge(stl.sem, 16 * (i - 1))
                lcp.inc(s.copy(lgS[:, s_ * 128:(s_ + 1) * 128],
                               psT[0:16, s_ * 128:(s_ + 1) * 128]))

        @block.vector
        def _(v):
            def ch(inst):
                lcv.inc(inst)
                v.wait_ge(lcv.sem, lcv.n)
                return inst
            ch(v.memset(gacc[:], 0.0))
            for i in range(NT):
                s_ = i % 2
                offs = lambda k: i * NSC + k
                offt = lambda k: i * NTT + k
                sl = slice(s_ * D, (s_ + 1) * D)
                sl16 = slice(s_ * 16, (s_ + 1) * 16)
                v.wait_ge(xin[s_].sem, XIN_PER // 16 * 16 * (i // 2 + 1))
                if i >= 2:
                    v.wait_ge(st.sem, 80 * (i - 1))
                ch(v.tensor_sub(db_[:, sl], xs_[:, sl], xp_[:, sl]))
                ch(v.tensor_mul(tp_[:, sl], mur[:, :], db_[:, sl]))
                ch(v.tensor_add(xrb[:, sl], xp_[:, sl], tp_[:, sl]))
                ch(v.tensor_mul(tp_[:, sl], muk[:, :], db_[:, sl]))
                ch(v.tensor_add(xkb[:, sl], xp_[:, sl], tp_[:, sl]))
                v.wait_ge(lct.sem, offt(TK0))
                ch(v.tensor_scalar_max(kab[:, s_ * D: s_ * D + 512],
                                       psC[:, 0:512], 0.0))
                v.wait_ge(lct.sem, offt(TK1))
                ch(v.tensor_scalar_max(kab[:, s_ * D + 512: s_ * D + 1024],
                                       psD[:, 0:512], 0.0))
                v.wait_ge(lct.sem, offt(TV0))
                v.wait_ge(lcs.sem, offs(SR0))
                ch(v.tensor_mul(vo_[:, s_ * D: s_ * D + 512],
                                rf_[:, s_ * D: s_ * D + 512], psA[:, 0:512]))
                v.wait_ge(lct.sem, offt(TV1))
                v.wait_ge(lcs.sem, offs(SR1))
                ch(v.tensor_mul(vo_[:, s_ * D + 512: s_ * D + 1024],
                                rf_[:, s_ * D + 512: s_ * D + 1024],
                                psB[:, 0:512]))
                ch(v.tensor_add(yl_[:, sl], xs_[:, sl], vo_[:, sl]))
                ch(v.tensor_reduce(mst[:, 0:1], yl_[:, sl], axis=AX.X,
                                   op=ALU.add))
                ch(v.memset(mst[:, 2:3], 0.0))
                ch(v.tensor_scalar_mul(mst[:, 1:2], mst[:, 0:1], 1.0 / D))
                v.wait_ge(lcs.sem, offs(SSQ))
                ch(v.tensor_scalar_mul(mst[:, 3:4], mst[:, 2:3], 1.0 / D))
                ch(v.tensor_tensor(mst[:, 4:5], mst[:, 1:2], mst[:, 1:2],
                                   ALU.mult))
                ch(v.tensor_sub(mst[:, 4:5], mst[:, 3:4], mst[:, 4:5]))
                ch(v.tensor_scalar_add(mst[:, 4:5], mst[:, 4:5], 1e-5))
                v.wait_ge(lcs.sem, offs(SSQRT))
                ch(v.reciprocal(mst[:, 6:7], mst[:, 5:6]))
                ch(v.tensor_tensor(mst[:, 7:8], mst[:, 1:2], mst[:, 6:7],
                                   ALU.mult))
                ch(v.tensor_scalar_mul(mst[:, 7:8], mst[:, 7:8], -1.0))
                v.wait_ge(lcs.sem, offs(SFIN))
                ch(v.tensor_copy(x2b[:, sl], yl_[:, sl]))
                v.wait_ge(lct.sem, offt(TQ0))
                ch(v.tensor_copy(q1s[:, s_ * D: s_ * D + 512], psC[:, 0:512]))
                v.wait_ge(lct.sem, offt(TQ1))
                ch(v.tensor_copy(q1s[:, s_ * D + 512: s_ * D + 1024],
                                 psD[:, 0:512]))
                v.wait_ge(lct.sem, offt(TKB1))
                ch(v.tensor_copy(k1s[:, s_ * D: s_ * D + 512], psA[:, 0:512]))
                ch(v.tensor_copy(k1s[:, s_ * D + 512: s_ * D + 1024],
                                 psB[:, 0:512]))
                v.wait_ge(lcs.sem, offs(SG))
                ch(v.tensor_scalar_max(gf_[:, sl16], gf_[:, sl16], 1e-6))
                v.wait_ge(lcp.sem, i + 1)
                ch(v.tensor_reduce(gacc[:, 1:2], lgS[:, s_ * 128:(s_ + 1) * 128],
                                   axis=AX.X, op=ALU.add))
                ch(v.tensor_add(gacc[:, 0:1], gacc[:, 0:1], gacc[:, 1:2]))

        @block.scalar
        def _(s):
            for i in range(NT):
                s_ = i % 2
                s.wait_ge(lct.sem, i * NTT + TLG)
                if i >= 2:
                    s.wait_ge(stl.sem, 16 * (i - 1))
                lcp.inc(s.copy(lgS[:, s_ * 128:(s_ + 1) * 128],
                               psT[0:16, s_ * 128:(s_ + 1) * 128]))

    return nc

def build_LD():
    nc = bass.Bass("TRN2", target_bir_lowering=False,
                   detect_race_conditions=False, dynamic_dma_scratch_size=4096)
    x2h = nc.dram_tensor("x2h", [TOK, D], F32, kind="ExternalInput").ap()
    q1h = nc.dram_tensor("q1h", [TOK, D], BF16, kind="ExternalInput").ap()
    k1h = nc.dram_tensor("k1h", [TOK, D], BF16, kind="ExternalInput").ap()
    lgth = nc.dram_tensor("lgth", [16, TOK], F32, kind="ExternalInput").ap()
    rsh = nc.dram_tensor("rsh", [TOK, 16], F32, kind="ExternalInput").ap()
    gsh = nc.dram_tensor("gsh", [16, 8], F32, kind="ExternalInput").ap()
    pmh = nc.dram_tensor("pmh", [16, 8], F32, kind="ExternalInput").ap()
    maskh = nc.dram_tensor("maskh", [128, 128], F32, kind="ExternalInput").ap()
    identh = nc.dram_tensor("identh", [128, 128], F32, kind="ExternalInput").ap()
    o1_o = nc.dram_tensor("o1_o", [TOK, D], F32, kind="ExternalOutput").ap()
    sa_o = nc.dram_tensor("sa_o", [NT, 128, 512], F32, kind="ExternalOutput").ap()
    gc_o = nc.dram_tensor("gc_o", [TOK, 16], F32, kind="ExternalOutput").ap()

    with ExitStack() as ctx:
        sb = lambda n_, sh, dt: ctx.enter_context(nc.sbuf_tensor(n_, sh, dt))
        pt = lambda n_, sh: ctx.enter_context(nc.psum_tensor(n_, sh, F32))
        lgt = sb("lgt", [16, TOK], F32)
        cums = sb("cums", [16, TOK], F32)
        gch = sb("gch", [16, TOK], F32)
        gs8 = sb("gs8", [16, 8], F32)
        pm8 = sb("pm8", [16, 8], F32)
        cumin = sb("cumin", [16, 2], F32)
        gct = sb("gct", [128, NT * 16], F32)
        wt_ = sb("wt_", [128, NT * 16], F32)
        rs_ = sb("rs_", [128, NT * 16], F32)
        mask = sb("mask", [128, 128], F32)
        ident = sb("ident_s", [128, 128], F32)
        x2f = sb("x2f", [128, 2 * D], F32)
        x2b = sb("x2b", [128, 2 * D], BF16)
        q1t = sb("q1t", [128, 2 * D], BF16)
        k1t = sb("k1t", [128, 2 * D], BF16)
        qpb = sb("qpb", [128, 2 * D], BF16)
        kpb = sb("kpb", [128, 2 * D], BF16)
        qpT = sb("qpT", [128, 2 * D], BF16)
        kpT = sb("kpT", [128, 2 * D], BF16)
        a1m = sb("a1m", [128, 2 * 256], BF16)
        o1t = sb("o1t", [128, 2 * D], F32)
        ssn = sb("ssn", [128, 2 * 512], F32)
        psS1a = pt("psS1a", [128, 128])
        psS1b = pt("psS1b", [128, 128])
        psS1c = pt("psS1c", [128, 128])
        psS1d = pt("psS1d", [128, 128])
        psOa = pt("psOa", [128, 64])
        psOb = pt("psOb", [128, 64])
        psS = pt("psS", [128, 512])
        psGc = pt("psGc", [128, 16])
        block = ctx.enter_context(nc.Block())

        ld = Ctr(nc, "ld")
        xin = [Ctr(nc, "xin0"), Ctr(nc, "xin1")]
        ttr = [Ctr(nc, "ttr0"), Ctr(nc, "ttr1")]
        ldv = Ctr(nc, "ldv")
        lds = Ctr(nc, "lds")
        ldt = Ctr(nc, "ldt")
        st = Ctr(nc, "st")

        NVA = 4 + 3 * NT
        NVB = 66
        NTA = NT
        NTB = 48

        @block.sync
        def _(sync):
            ld.inc(sync.dma_start(out=lgt[:], in_=lgth[:, :]), 16)
            ld.inc(sync.dma_start(out=gs8[:], in_=gsh[:, :]), 16)
            ld.inc(sync.dma_start(out=pm8[:], in_=pmh[:, :]), 16)
            ld.inc(sync.dma_start(out=mask[:], in_=maskh[:, :]), 16)
            ld.inc(sync.dma_start(out=ident[:], in_=identh[:, :]), 16)
            for i in range(NT):
                ld.inc(sync.dma_start(out=rs_[:, i * 16:(i + 1) * 16],
                                      in_=rsh[i * 128:(i + 1) * 128, :]), 16)

        @block.vector
        def _(v):
            def ch(inst):
                ldv.inc(inst)
                v.wait_ge(ldv.sem, ldv.n)
                return inst
            ld.wait(v)
            ch(v.tensor_mul(cumin[:, 0:1] if False else gs8[:], gs8[:], pm8[:]))
            ch(v.tensor_reduce(cumin[:, 0:1], gs8[:], axis=AX.X, op=ALU.add))
            ch(v.tensor_tensor_scan(cums[:], lgt[:], lgt[:], cumin[:, 0:1],
                                    op0=ALU.add, op1=ALU.bypass))
            ch(v.tensor_scalar_max(cums[:], cums[:], -30.0))
            v.wait_ge(lds.sem, 1)
            ch(v.tensor_scalar_add(gch[:], gch[:], 1e-6))
            for i in range(NT):
                v.wait_ge(ldt.sem, i + 1)
                ch(v.tensor_copy(gct[:, i * 16:(i + 1) * 16], psGc[:, 0:16]))
                ch(v.reciprocal(wt_[:, i * 16:(i + 1) * 16],
                                gct[:, i * 16:(i + 1) * 16]))
                ch(v.tensor_mul(wt_[:, i * 16:(i + 1) * 16],
                                wt_[:, i * 16:(i + 1) * 16],
                                rs_[:, i * 16:(i + 1) * 16]))
            ch(v.memset(psS[:, 0:512], 0.0))
            for i in range(NT):
                s_ = i % 2
                v.wait_ge(xin[s_].sem, 48 * (i // 2 + 1))
                if i >= 2:
                    v.wait_ge(st.sem, 32 * (i - 1))
                ch(v.tensor_copy(x2b[:, s_ * D:(s_ + 1) * D],
                                 x2f[:, s_ * D:(s_ + 1) * D]))
                for h in range(16):
                    ch(v.tensor_scalar_mul(
                        qpb[:, s_ * D + h * 64: s_ * D + (h + 1) * 64],
                        q1t[:, s_ * D + h * 64: s_ * D + (h + 1) * 64],
                        gct[:, i * 16 + h: i * 16 + h + 1]))
                    ch(v.tensor_scalar_mul(
                        kpb[:, s_ * D + h * 64: s_ * D + (h + 1) * 64],
                        k1t[:, s_ * D + h * 64: s_ * D + (h + 1) * 64],
                        wt_[:, i * 16 + h: i * 16 + h + 1]))
                base_t = NTA + i * NTB

                def evac_pair(hp2):
                    v.wait_ge(ldt.sem, base_t + hp2 * 6 + 4)
                    for hh in range(2):
                        h = hp2 * 2 + hh
                        ps = psOa if hh == 0 else psOb
                        ch(v.tensor_copy(
                            o1t[:, s_ * D + h * 64: s_ * D + (h + 1) * 64],
                            ps[:, 0:64]))
                for hp in range(8):
                    sl_ = hp % 2
                    banks = [psS1a, psS1b] if sl_ == 0 else [psS1c, psS1d]
                    v.wait_ge(ldt.sem, base_t + hp * 6 + 2)
                    for hh in range(2):
                        ch(v.tensor_mul(
                            a1m[:, sl_ * 256 + hh * 128: sl_ * 256 + (hh + 1) * 128],
                            banks[hh][:, 0:128], mask[:, :]))
                    if hp >= 1:
                        evac_pair(hp - 1)
                evac_pair(7)
                v.wait_ge(ldt.sem, base_t + NTB)
                ch(v.tensor_copy(ssn[:, s_ * 512:(s_ + 1) * 512],
                                 psS[:, 0:512]))

        @block.scalar
        def _(s):
            s.wait_ge(ldv.sem, 4)
            lds.inc(s.activation(out=gch[:], in_=cums[:], func=AF.Exp))

        @block.tensor
        def _(t):
            ld.wait(t)
            t.wait_ge(ldv.sem, 5)
            for i in range(NT):
                if i >= 1:
                    t.wait_ge(ldv.sem, 5 + 3 * i)
                ldt.inc(t.transpose(psGc[:, 0:16],
                                    gch[0:16, i * 128:(i + 1) * 128],
                                    ident[0:16, 0:16]))
            A_ = [35, 37, 41, 45, 49, 53, 57, 61]
            E_ = [39, 43, 47, 51, 55, 59, 63, 65]
            t.wait_ge(ldv.sem, 6 + 3 * NT)
            for i in range(NT):
                s_ = i % 2
                base_v = 6 + 3 * NT + i * NVB
                for hp in range(8):
                    sl_ = hp % 2
                    banks = [psS1a, psS1b] if sl_ == 0 else [psS1c, psS1d]
                    t.wait_ge(ttr[s_].sem, 256 * (i // 2 + 1))
                    if hp >= 2:
                        t.wait_ge(ldv.sem, base_v + A_[hp - 2])
                    elif i >= 1:
                        prev_v = 6 + 3 * NT + (i - 1) * NVB
                        t.wait_ge(ldv.sem, prev_v + A_[hp + 6])
                    for hh in range(2):
                        ldt.inc(t.matmul(
                            banks[hh][:, 0:128],
                            kpT[(hh) * 64:(hh + 1) * 64,
                                s_ * D + hp * 128: s_ * D + hp * 128 + 128],
                            qpT[(hh) * 64:(hh + 1) * 64,
                                s_ * D + hp * 128: s_ * D + hp * 128 + 128],
                            start=True, stop=True,
                            tile_position=(hh * 64, 0)))
                    for hh in range(2):
                        h = hp * 2 + hh
                        ps = psOa if hh == 0 else psOb
                        t.wait_ge(ldv.sem, base_v + A_[hp])
                        if hp >= 1:
                            t.wait_ge(ldv.sem, base_v + E_[hp - 1])
                        elif i >= 1:
                            prev_v = 6 + 3 * NT + (i - 1) * NVB
                            t.wait_ge(ldv.sem, prev_v + E_[7])
                        ldt.inc(t.matmul(
                            ps[:, 0:64],
                            a1m[:, sl_ * 256 + hh * 128: sl_ * 256 + (hh + 1) * 128],
                            x2b[:, s_ * D + h * 64: s_ * D + (h + 1) * 64],
                            start=True, stop=True))
                    for hh in range(2):
                        h = hp * 2 + hh
                        if i == 0 and hp == 0 and hh == 0:
                            pass
                        if i >= 2 and hp == 0 and hh == 0:
                            prev_v = 6 + 3 * NT + (i - 1) * NVB
                            t.wait_ge(ldv.sem, prev_v + NVB)
                        ldt.inc(t.matmul(
                            psS[(h % 2) * 64:(h % 2) * 64 + 64,
                                (h // 2) * 64:(h // 2) * 64 + 64],
                            kpb[:, s_ * D + h * 64: s_ * D + (h + 1) * 64],
                            x2b[:, s_ * D + h * 64: s_ * D + (h + 1) * 64],
                            start=False,
                            stop=(i == NT - 1 and h == 15),
                            skip_group_check=True,
                            tile_position=(0, (h % 2) * 64)))

        @block.sync
        def _(sync):
            for i in range(NT):
                s_ = i % 2
                base_v = 6 + 3 * NT + i * NVB
                if i >= 2:
                    prev_v = 6 + 3 * NT + (i - 2) * NVB
                    sync.wait_ge(ldv.sem, prev_v + 65)
                    st.inc(sync.dma_start(
                        out=o1_o[(i - 2) * 128:(i - 1) * 128, :],
                        in_=o1t[:, s_ * D:(s_ + 1) * D]), 16)
                    sync.wait_ge(ldv.sem, prev_v + NVB)
                    st.inc(sync.dma_start(
                        out=sa_o[i - 2, :, :],
                        in_=ssn[:, s_ * 512:(s_ + 1) * 512]), 16)
                    sync.wait_ge(ldv.sem, 5 + 3 * (i - 2) + 1)
                if i >= 2:
                    sync.wait_ge(ldv.sem, prev_v + 33)
                xin[s_].inc(sync.dma_start(out=x2f[:, s_ * D:(s_ + 1) * D],
                                           in_=x2h[i * 128:(i + 1) * 128, :]), 16)
                xin[s_].inc(sync.dma_start(out=q1t[:, s_ * D:(s_ + 1) * D],
                                           in_=q1h[i * 128:(i + 1) * 128, :]), 16)
                xin[s_].inc(sync.dma_start(out=k1t[:, s_ * D:(s_ + 1) * D],
                                           in_=k1h[i * 128:(i + 1) * 128, :]), 16)
                if i >= 2:
                    base_t_prev = NTA + (i - 2) * NTB
                    sync.wait_ge(ldt.sem, base_t_prev + NTB)
                sync.wait_ge(ldv.sem, base_v + 33)
                for kc in range(8):
                    ttr[s_].inc(sync.dma_start_transpose(
                        qpT[:, s_ * D + kc * 128: s_ * D + (kc + 1) * 128],
                        qpb[:, s_ * D + kc * 128: s_ * D + (kc + 1) * 128]), 16)
                for kc in range(8):
                    ttr[s_].inc(sync.dma_start_transpose(
                        kpT[:, s_ * D + kc * 128: s_ * D + (kc + 1) * 128],
                        kpb[:, s_ * D + kc * 128: s_ * D + (kc + 1) * 128]), 16)
            for i in (NT - 2, NT - 1):
                s_ = i % 2
                base_v = 6 + 3 * NT + i * NVB
                sync.wait_ge(ldv.sem, base_v + 65)
                st.inc(sync.dma_start(out=o1_o[i * 128:(i + 1) * 128, :],
                                      in_=o1t[:, s_ * D:(s_ + 1) * D]), 16)
                sync.wait_ge(ldv.sem, base_v + NVB)
                st.inc(sync.dma_start(out=sa_o[i, :, :],
                                      in_=ssn[:, s_ * 512:(s_ + 1) * 512]), 16)
            sync.wait_ge(ldv.sem, 6 + 3 * NT)
            for i in range(NT):
                st.inc(sync.dma_start(out=gc_o[i * 128:(i + 1) * 128, :],
                                      in_=gct[:, i * 16:(i + 1) * 16]), 16)
            st.wait(sync)
    return nc

def build_LE():
    nc = bass.Bass("TRN2", target_bir_lowering=False,
                   detect_race_conditions=False, dynamic_dma_scratch_size=4096)
    x2h = nc.dram_tensor("x2h", [TOK, D], F32, kind="ExternalInput").ap()
    q1h = nc.dram_tensor("q1h", [TOK, D], BF16, kind="ExternalInput").ap()
    k2h = nc.dram_tensor("k2h", [TOK, D], BF16, kind="ExternalInput").ap()
    gch = nc.dram_tensor("gch", [TOK, 16], F32, kind="ExternalInput").ap()
    o1ih = nc.dram_tensor("o1ih", [TOK, D], F32, kind="ExternalInput").ap()
    sah = nc.dram_tensor("sah", [NT, 128, 512], F32, kind="ExternalInput").ap()
    dsh = nc.dram_tensor("dsh", [128, 8 * 512], F32, kind="ExternalInput").ap()
    pmh = nc.dram_tensor("pmh", [128, 8], F32, kind="ExternalInput").ap()
    x2nh = nc.dram_tensor("x2nh", [1, D], F32, kind="ExternalInput").ap()
    emh = nc.dram_tensor("emh", [128, NT], F32, kind="ExternalInput").ap()
    maskh = nc.dram_tensor("maskh", [128, 128], F32, kind="ExternalInput").ap()
    o1_o = nc.dram_tensor("o1_o", [TOK, D], F32, kind="ExternalOutput").ap()
    o2i_o = nc.dram_tensor("o2i_o", [TOK, D], F32, kind="ExternalOutput").ap()
    ma_o = nc.dram_tensor("ma_o", [NT, 128, 512], F32, kind="ExternalOutput").ap()

    with ExitStack() as ctx:
        sb = lambda n_, sh, dt: ctx.enter_context(nc.sbuf_tensor(n_, sh, dt))
        pt = lambda n_, sh: ctx.enter_context(nc.psum_tensor(n_, sh, F32))
        dsall = sb("dsall", [128, 8 * 512], F32)
        pm8 = sb("pm8", [128, 8], F32)
        sacc = sb("sacc", [128, 512], F32)
        gct = sb("gct", [128, NT * 16], F32)
        mask = sb("mask", [128, 128], F32)
        em = sb("em", [128, NT], F32)
        x2f = sb("x2f", [128, 2 * D], F32)
        x2n = sb("x2n", [128, 2 * D], F32)
        x2b = sb("x2b", [128, 2 * D], BF16)
        q1t = sb("q1t", [128, 2 * D], BF16)
        k2t = sb("k2t", [128, 2 * D], BF16)
        o1i = sb("o1i", [128, 2 * D], F32)
        sprev = sb("sprev", [128, 2 * 512], F32)
        stot = sb("stot", [128, 2 * 512], BF16)
        qpb = sb("qpb", [128, 2 * D], BF16)
        qpT = sb("qpT", [128, 2 * D], BF16)
        q2b = sb("q2b", [128, 2 * D], BF16)
        q2T = sb("q2T", [128, 2 * D], BF16)
        k2T = sb("k2T", [128, 2 * D], BF16)
        tq2 = sb("tq2", [128, D], F32)
        eb_ = sb("eb_", [128, 2 * D], BF16)
        a2m = sb("a2m", [128, 2 * 256], BF16)
        o2i = sb("o2i", [128, 2 * D], F32)
        msn = sb("msn", [128, 2 * 512], F32)
        psS1a = pt("psS1a", [128, 128])
        psS1b = pt("psS1b", [128, 128])
        psS1c = pt("psS1c", [128, 128])
        psS1d = pt("psS1d", [128, 128])
        psOa = pt("psOa", [128, 64])
        psOb = pt("psOb", [128, 64])
        psM = pt("psM", [128, 512])
        block = ctx.enter_context(nc.Block())

        ld = Ctr(nc, "ld")
        xin = [Ctr(nc, "xin0"), Ctr(nc, "xin1")]
        ttr = [Ctr(nc, "ttr0"), Ctr(nc, "ttr1")]
        ldv = Ctr(nc, "ldv")
        ldt = Ctr(nc, "ldt")
        st = Ctr(nc, "st")

        NVA = 10
        NVB = 72
        NTB = 64
        A2 = [41, 43, 47, 51, 55, 59, 63, 67]
        E2 = [45, 49, 53, 57, 61, 65, 69, 71]
        xin_marks = {}
        ttr_marks = {}

        @block.sync
        def _(sync):
            ld.inc(sync.dma_start(out=dsall[:], in_=dsh[:, :]), 16)
            ld.inc(sync.dma_start(out=pm8[:], in_=pmh[:, :]), 16)
            ld.inc(sync.dma_start(out=mask[:], in_=maskh[:, :]), 16)
            ld.inc(sync.dma_start(out=em[:], in_=emh[:, :]), 16)
            for i in range(NT):
                ld.inc(sync.dma_start(out=gct[:, i * 16:(i + 1) * 16],
                                      in_=gch[i * 128:(i + 1) * 128, :]), 16)
            for i in range(NT):
                s_ = i % 2
                base_v = NVA + i * NVB
                if i >= 2:
                    prev_v = NVA + (i - 2) * NVB
                    sync.wait_ge(ldv.sem, prev_v + 34)
                    st.inc(sync.dma_start(
                        out=o1_o[(i - 2) * 128:(i - 1) * 128, :],
                        in_=o1i[:, s_ * D:(s_ + 1) * D]), 16)
                    sync.wait_ge(ldv.sem, prev_v + E2[7])
                    st.inc(sync.dma_start(
                        out=o2i_o[(i - 2) * 128:(i - 1) * 128, :],
                        in_=o2i[:, s_ * D:(s_ + 1) * D]), 16)
                    sync.wait_ge(ldv.sem, prev_v + NVB)
                    st.inc(sync.dma_start(
                        out=ma_o[i - 2, :, :],
                        in_=msn[:, s_ * 512:(s_ + 1) * 512]), 16)
                xin[s_].inc(sync.dma_start(out=x2f[:, s_ * D:(s_ + 1) * D],
                                           in_=x2h[i * 128:(i + 1) * 128, :]), 16)
                xin[s_].inc(sync.dma_start(out=q1t[:, s_ * D:(s_ + 1) * D],
                                           in_=q1h[i * 128:(i + 1) * 128, :]), 16)
                xin[s_].inc(sync.dma_start(out=k2t[:, s_ * D:(s_ + 1) * D],
                                           in_=k2h[i * 128:(i + 1) * 128, :]), 16)
                xin[s_].inc(sync.dma_start(out=o1i[:, s_ * D:(s_ + 1) * D],
                                           in_=o1ih[i * 128:(i + 1) * 128, :]), 16)
                if i < NT - 1:
                    xin[s_].inc(sync.dma_start(
                        out=x2n[:, s_ * D:(s_ + 1) * D],
                        in_=x2h[i * 128 + 1:(i + 1) * 128 + 1, :]), 16)
                else:
                    xin[s_].inc(sync.dma_start(
                        out=x2n[0:127, s_ * D:(s_ + 1) * D],
                        in_=x2h[i * 128 + 1:(i + 1) * 128, :]), 16)
                    xin[s_].inc(sync.dma_start(
                        out=x2n[127:128, s_ * D:(s_ + 1) * D],
                        in_=x2nh[0:1, :]), 16)
                if i >= 1:
                    xin[s_].inc(sync.dma_start(
                        out=sprev[:, s_ * 512:(s_ + 1) * 512],
                        in_=sah[i - 1, :, :]), 16)
                xin_marks[i] = xin[s_].n
                if i >= 2:
                    sync.wait_ge(ldt.sem, (i - 1) * NTB)
                sync.wait_ge(ldv.sem, base_v + 17)
                for kc in range(8):
                    ttr[s_].inc(sync.dma_start_transpose(
                        qpT[:, s_ * D + kc * 128: s_ * D + (kc + 1) * 128],
                        qpb[:, s_ * D + kc * 128: s_ * D + (kc + 1) * 128]), 16)
                ttr_marks[(i, 0)] = ttr[s_].n
                sync.wait_ge(ldv.sem, base_v + 39)
                for kc in range(8):
                    ttr[s_].inc(sync.dma_start_transpose(
                        q2T[:, s_ * D + kc * 128: s_ * D + (kc + 1) * 128],
                        q2b[:, s_ * D + kc * 128: s_ * D + (kc + 1) * 128]), 16)
                for kc in range(8):
                    ttr[s_].inc(sync.dma_start_transpose(
                        k2T[:, s_ * D + kc * 128: s_ * D + (kc + 1) * 128],
                        k2t[:, s_ * D + kc * 128: s_ * D + (kc + 1) * 128]), 16)
                ttr_marks[(i, 1)] = ttr[s_].n
            for i in (NT - 2, NT - 1):
                s_ = i % 2
                base_v = NVA + i * NVB
                sync.wait_ge(ldv.sem, base_v + 34)
                st.inc(sync.dma_start(out=o1_o[i * 128:(i + 1) * 128, :],
                                      in_=o1i[:, s_ * D:(s_ + 1) * D]), 16)
                sync.wait_ge(ldv.sem, base_v + E2[7])
                st.inc(sync.dma_start(out=o2i_o[i * 128:(i + 1) * 128, :],
                                      in_=o2i[:, s_ * D:(s_ + 1) * D]), 16)
                sync.wait_ge(ldv.sem, base_v + NVB)
                st.inc(sync.dma_start(out=ma_o[i, :, :],
                                      in_=msn[:, s_ * 512:(s_ + 1) * 512]), 16)
            st.wait(sync)

        @block.tensor
        def _(t):
            ld.wait(t)
            t.wait_ge(ldv.sem, NVA)
            for i in range(NT):
                s_ = i % 2
                base_v = NVA + i * NVB
                base_t = i * NTB
                t.wait_ge(ttr[s_].sem, 16 * ttr_marks[(i, 0)] // 16)
                t.wait_ge(ldv.sem, base_v + 18)
                for hp in range(8):
                    for hh in range(2):
                        h = hp * 2 + hh
                        ps = psOa if hh == 0 else psOb
                        if h >= 2:
                            t.wait_ge(ldv.sem, base_v + 18 + h - 1)
                        elif i >= 1:
                            t.wait_ge(ldv.sem, NVA + (i - 1) * NVB + 18 + 15)
                        ldt.inc(t.matmul(
                            ps[:, 0:64],
                            qpT[(h % 2) * 64:(h % 2) * 64 + 64,
                                s_ * D + (h // 2) * 128: s_ * D + (h // 2) * 128 + 128],
                            stot[(h % 2) * 64:(h % 2) * 64 + 64,
                                 s_ * 512 + (h // 2) * 64: s_ * 512 + (h // 2) * 64 + 64],
                            start=True, stop=True,
                            tile_position=((h % 2) * 64, 0)))
                t.wait_ge(ttr[s_].sem, 16 * ttr_marks[(i, 1)] // 16)
                for hp in range(8):
                    sl_ = hp % 2
                    banks = [psS1a, psS1b] if sl_ == 0 else [psS1c, psS1d]
                    if hp >= 2:
                        t.wait_ge(ldv.sem, base_v + A2[hp - 2])
                    elif i >= 1:
                        t.wait_ge(ldv.sem, NVA + (i - 1) * NVB + A2[hp + 6])
                    for hh in range(2):
                        ldt.inc(t.matmul(
                            banks[hh][:, 0:128],
                            k2T[hh * 64:(hh + 1) * 64,
                                s_ * D + hp * 128: s_ * D + hp * 128 + 128],
                            q2T[hh * 64:(hh + 1) * 64,
                                s_ * D + hp * 128: s_ * D + hp * 128 + 128],
                            start=True, stop=True,
                            tile_position=(hh * 64, 0)))
                    for hh in range(2):
                        h = hp * 2 + hh
                        ps = psOa if hh == 0 else psOb
                        t.wait_ge(ldv.sem, base_v + A2[hp])
                        if hp >= 1:
                            t.wait_ge(ldv.sem, base_v + E2[hp - 1])
                        elif i >= 1:
                            t.wait_ge(ldv.sem, NVA + (i - 1) * NVB + E2[7])
                        ldt.inc(t.matmul(
                            ps[:, 0:64],
                            a2m[:, sl_ * 256 + hh * 128: sl_ * 256 + (hh + 1) * 128],
                            eb_[:, s_ * D + h * 64: s_ * D + (h + 1) * 64],
                            start=True, stop=True))
                    for hh in range(2):
                        h = hp * 2 + hh
                        if i >= 2 and hp == 0 and hh == 0:
                            t.wait_ge(ldv.sem, NVA + (i - 1) * NVB + NVB)
                        ldt.inc(t.matmul(
                            psM[(h % 2) * 64:(h % 2) * 64 + 64,
                                (h // 2) * 64:(h // 2) * 64 + 64],
                            k2t[:, s_ * D + h * 64: s_ * D + (h + 1) * 64],
                            eb_[:, s_ * D + h * 64: s_ * D + (h + 1) * 64],
                            start=False, stop=(i == NT - 1 and h == 15),
                            skip_group_check=True,
                            tile_position=(0, (h % 2) * 64)))

        @block.vector
        def _(v):
            def ch(inst):
                ldv.inc(inst)
                v.wait_ge(ldv.sem, ldv.n)
                return inst
            ld.wait(v)
            ch(v.memset(sacc[:], 0.0))
            for c in range(8):
                ch(v.scalar_tensor_tensor(sacc[:], dsall[:, c * 512:(c + 1) * 512],
                                          pm8[:, c:c + 1], sacc[:],
                                          op0=ALU.mult, op1=ALU.add))
            ch(v.memset(psM[:, 0:512], 0.0))
            for i in range(NT):
                s_ = i % 2
                base_t = i * NTB
                v.wait_ge(xin[s_].sem, 16 * (xin_marks[i] // 16))
                if i >= 2:
                    v.wait_ge(st.sem, 48 * (i - 1))
                ch(v.tensor_copy(x2b[:, s_ * D:(s_ + 1) * D],
                                 x2f[:, s_ * D:(s_ + 1) * D]))
                for h in range(16):
                    ch(v.tensor_scalar_mul(
                        qpb[:, s_ * D + h * 64: s_ * D + (h + 1) * 64],
                        q1t[:, s_ * D + h * 64: s_ * D + (h + 1) * 64],
                        gct[:, i * 16 + h: i * 16 + h + 1]))
                if i == 0:
                    ch(v.tensor_copy(stot[:, 0:512], sacc[:]))
                else:
                    ch(v.tensor_add(stot[:, s_ * 512:(s_ + 1) * 512],
                                    sacc[:], sprev[:, s_ * 512:(s_ + 1) * 512]))
                for hp in range(8):
                    for hh in range(2):
                        h = hp * 2 + hh
                        ps = psOa if hh == 0 else psOb
                        v.wait_ge(ldt.sem, base_t + h + 1)
                        ch(v.tensor_add(
                            o1i[:, s_ * D + h * 64: s_ * D + (h + 1) * 64],
                            o1i[:, s_ * D + h * 64: s_ * D + (h + 1) * 64],
                            ps[:, 0:64]))
                ch(v.tensor_scalar(tq2[:], q1t[:, s_ * D:(s_ + 1) * D],
                                   0.0, 0.125, op0=ALU.min, op1=ALU.mult))
                ch(v.tensor_scalar(q2b[:, s_ * D:(s_ + 1) * D],
                                   q1t[:, s_ * D:(s_ + 1) * D],
                                   0.0, 0.0025, op0=ALU.max, op1=ALU.mult))
                ch(v.tensor_add(q2b[:, s_ * D:(s_ + 1) * D],
                                q2b[:, s_ * D:(s_ + 1) * D], tq2[:]))
                ch(v.tensor_sub(eb_[:, s_ * D:(s_ + 1) * D],
                                x2n[:, s_ * D:(s_ + 1) * D],
                                o1i[:, s_ * D:(s_ + 1) * D]))
                ch(v.tensor_scalar_mul(eb_[:, s_ * D:(s_ + 1) * D],
                                       eb_[:, s_ * D:(s_ + 1) * D],
                                       em[:, i:i + 1]))
                def evac2(hp2):
                    v.wait_ge(ldt.sem, base_t + 16 + hp2 * 6 + 4)
                    for hh in range(2):
                        h = hp2 * 2 + hh
                        ps = psOa if hh == 0 else psOb
                        ch(v.tensor_copy(
                            o2i[:, s_ * D + h * 64: s_ * D + (h + 1) * 64],
                            ps[:, 0:64]))
                for hp in range(8):
                    sl_ = hp % 2
                    banks = [psS1a, psS1b] if sl_ == 0 else [psS1c, psS1d]
                    v.wait_ge(ldt.sem, base_t + 16 + hp * 6 + 2)
                    for hh in range(2):
                        ch(v.tensor_mul(
                            a2m[:, sl_ * 256 + hh * 128: sl_ * 256 + (hh + 1) * 128],
                            banks[hh][:, 0:128], mask[:, :]))
                    if hp >= 1:
                        evac2(hp - 1)
                evac2(7)
                v.wait_ge(ldt.sem, base_t + NTB)
                ch(v.tensor_copy(msn[:, s_ * 512:(s_ + 1) * 512],
                                 psM[:, 0:512]))
    return nc

def build_LF():
    nc = bass.Bass("TRN2", target_bir_lowering=False,
                   detect_race_conditions=False, dynamic_dma_scratch_size=4096)
    x2h = nc.dram_tensor("x2h", [TOK, D], F32, kind="ExternalInput").ap()
    o1h = nc.dram_tensor("o1h", [TOK, D], F32, kind="ExternalInput").ap()
    o2ih = nc.dram_tensor("o2ih", [TOK, D], F32, kind="ExternalInput").ap()
    mah = nc.dram_tensor("mah", [NT, 128, 512], F32, kind="ExternalInput").ap()
    dmh = nc.dram_tensor("dmh", [128, 8 * 512], F32, kind="ExternalInput").ap()
    pmh = nc.dram_tensor("pmh", [128, 8], F32, kind="ExternalInput").ap()
    q1h = nc.dram_tensor("q1h", [TOK, D], BF16, kind="ExternalInput").ap()
    q1lh = nc.dram_tensor("q1lh", [1, D], BF16, kind="ExternalInput").ap()
    flagh = nc.dram_tensor("flagh", [1, 1], F32, kind="ExternalInput").ap()
    cpwh = nc.dram_tensor("cpwh", [128, 8 * D], F32, kind="ExternalInput").ap()
    cpbh = nc.dram_tensor("cpbh", [1, D], F32, kind="ExternalInput").ap()
    onesh = nc.dram_tensor("onesh", [1, 128], F32, kind="ExternalInput").ap()
    identh = nc.dram_tensor("identh", [128, 128], F32, kind="ExternalInput").ap()
    x3_o = nc.dram_tensor("x3_o", [TOK, D], F32, kind="ExternalOutput").ap()
    amax_o = nc.dram_tensor("amax_o", [1, 1], F32, kind="ExternalOutput").ap()

    with ExitStack() as ctx:
        sb = lambda n_, sh, dt: ctx.enter_context(nc.sbuf_tensor(n_, sh, dt))
        pt = lambda n_, sh: ctx.enter_context(nc.psum_tensor(n_, sh, F32))
        dmall = sb("dmall", [128, 8 * 512], F32)
        pm8 = sb("pm8", [128, 8], F32)
        macc = sb("macc", [128, 512], F32)
        maccb = sb("maccb", [128, 512], BF16)
        cpw = sb("cpw", [128, 8 * D], BF16)
        cpb = sb("cpb", [1, D], BF16)
        ones1 = sb("ones1", [1, 128], BF16)
        ident = sb("ident_s", [128, 128], F32)
        q1l = sb("q1l", [1, D], BF16)
        identb = sb("identb", [128, 128], BF16)
        tql = sb("tql", [1, D], F32)
        q2l = sb("q2l", [1, D], BF16)
        q2lT = sb("q2lT", [128, 8], BF16)
        o2fr = sb("o2fr", [1, D], F32)
        o2fx = sb("o2fx", [64, 1], F32)
        flg = sb("flg", [1, 1], F32)
        x2f = sb("x2f", [128, 2 * D], F32)
        o1t = sb("o1t", [128, 2 * D], F32)
        o2t = sb("o2t", [128, 2 * D], F32)
        o2s = sb("o2s", [128, 2 * D], F32)
        mprev = sb("mprev", [128, 2 * 512], F32)
        mtot = sb("mtot", [128, 2 * 512], BF16)
        q1t = sb("q1t", [128, 2 * D], BF16)
        q2b = sb("q2b", [128, 2 * D], BF16)
        q2T = sb("q2T", [128, 2 * D], BF16)
        tq2 = sb("tq2", [128, D], F32)
        yb_ = sb("yb_", [128, 2 * D], BF16)
        yT_ = sb("yT_", [128, 2 * D], BF16)
        yl_ = sb("yl_", [128, 2 * D], F32)
        sqv = sb("sqv", [128, D], F32)
        mst = sb("mst", [128, 8], F32)
        amax = sb("amax", [128, 2], F32)
        psOa = pt("psOa", [128, 64])
        psOb = pt("psOb", [128, 64])
        psA = pt("psA", [128, 512])
        psB = pt("psB", [128, 512])
        psF = pt("psF", [64, 1])
        psG = pt("psG", [1, 64])
        psQ = ctx.enter_context(nc.psum_tensor("psQ", [128, 1], BF16))
        psH = pt("psH", [1, 128])
        block = ctx.enter_context(nc.Block())

        ld = Ctr(nc, "ld")
        ldg = Ctr(nc, "ldg")
        xin = [Ctr(nc, "xin0"), Ctr(nc, "xin1")]
        shf = [Ctr(nc, "shf0"), Ctr(nc, "shf1")]
        ttr = [Ctr(nc, "ttr0"), Ctr(nc, "ttr1")]
        tty = [Ctr(nc, "tty0"), Ctr(nc, "tty1")]
        ldv = Ctr(nc, "ldv")
        lds = Ctr(nc, "lds")
        ldt = Ctr(nc, "ldt")
        st = Ctr(nc, "st")

        NVA = 55
        NVB = 36
        NSB = 3
        NTB = 16 + 8 + 2
        NTA = 40
        xin_marks = {}
        vm = {}

        @block.sync
        def _(sync):
            ld.inc(sync.dma_start(out=dmall[:], in_=dmh[:, :]), 16)
            ld.inc(sync.dma_start(out=pm8[:], in_=pmh[:, :]), 16)
            ld.inc(sync.dma_start(out=q1l[:], in_=q1lh[:, :]), 16)
            ld.inc(sync.dma_start(out=flg[:], in_=flagh[:, :]), 16)
            ld.inc(sync.dma_start(out=ident[:], in_=identh[:, :]), 16)
            for i in range(NT):
                s_ = i % 2
                if i >= 2:
                    sync.wait_ge(st.sem, 16 * (i - 1))
                xin[s_].inc(sync.dma_start(out=x2f[:, s_ * D:(s_ + 1) * D],
                                           in_=x2h[i * 128:(i + 1) * 128, :]), 16)
                xin[s_].inc(sync.dma_start(out=o1t[:, s_ * D:(s_ + 1) * D],
                                           in_=o1h[i * 128:(i + 1) * 128, :]), 16)
                xin[s_].inc(sync.dma_start(out=o2t[:, s_ * D:(s_ + 1) * D],
                                           in_=o2ih[i * 128:(i + 1) * 128, :]), 16)
                xin[s_].inc(sync.dma_start(out=q1t[:, s_ * D:(s_ + 1) * D],
                                           in_=q1h[i * 128:(i + 1) * 128, :]), 16)
                if i >= 1:
                    xin[s_].inc(sync.dma_start(
                        out=mprev[:, s_ * 512:(s_ + 1) * 512],
                        in_=mah[i - 1, :, :]), 16)
                xin_marks[i] = xin[s_].n
                base_v = NVA + i * NVB
                if i >= 2:
                    sync.wait_ge(ldt.sem, NTA + (i - 1) * 18)
                sync.wait_ge(ldv.sem, base_v + 3)
                for kc in range(8):
                    ttr[s_].inc(sync.dma_start_transpose(
                        q2T[:, s_ * D + kc * 128: s_ * D + (kc + 1) * 128],
                        q2b[:, s_ * D + kc * 128: s_ * D + (kc + 1) * 128]), 16)
                sync.wait_ge(ldv.sem, base_v + 20)
                shf[s_].inc(sync.dma_start(
                    out=o2s[1:128, s_ * D:(s_ + 1) * D],
                    in_=o2t[0:127, s_ * D:(s_ + 1) * D]), 16)
                if i == 0:
                    sync.wait_ge(ldv.sem, NVA)
                    shf[s_].inc(sync.dma_start(
                        out=o2s[0:1, 0:D], in_=o2fr[0:1, :]), 16)
                else:
                    shf[s_].inc(sync.dma_start(
                        out=o2s[0:1, s_ * D:(s_ + 1) * D],
                        in_=o2t[127:128, (1 - s_) * D:(2 - s_) * D]), 16)
                sync.wait_ge(ldv.sem, base_v + 22)
                for kc in range(8):
                    tty[s_].inc(sync.dma_start_transpose(
                        yT_[:, s_ * D + kc * 128: s_ * D + (kc + 1) * 128],
                        yb_[:, s_ * D + kc * 128: s_ * D + (kc + 1) * 128]), 16)
                sync.wait_ge(lds.sem, i * NSB + 3)
                st.inc(sync.dma_start(out=x3_o[i * 128:(i + 1) * 128, :],
                                      in_=yl_[:, s_ * D:(s_ + 1) * D]), 16)
            sync.wait_ge(ldv.sem, NVA + NT * NVB + 1)
            st.inc(sync.dma_start(out=amax_o[:, :], in_=amax[0:1, 1:2]), 16)
            st.wait(sync)

        @block.gpsimd
        def _(gp):
            ldg.inc(gp.dma_start(out=cpw[:], in_=cpwh[:, :]), 16)
            ldg.inc(gp.dma_start(out=cpb[:], in_=cpbh[:, :]), 16)
            ldg.inc(gp.dma_start(out=ones1[:], in_=onesh[:, :]), 16)
            ldg.inc(gp.dma_start(out=identb[:], in_=identh[:, :]), 16)

        @block.tensor
        def _(t):
            ld.wait(t)
            ldg.wait(t)
            t.wait_ge(ldv.sem, 14)
            for kc in range(8):
                if kc >= 1:
                    t.wait_ge(ldv.sem, 14 + kc)
                ldt.inc(t.transpose(psQ[0:128, 0:1],
                                    q2l[0:1, kc * 128:(kc + 1) * 128],
                                    identb[0:1, 0:1]))
            for h in range(16):
                t.wait_ge(ldv.sem, 22)
                base = 22 + h * 2
                if h >= 1:
                    t.wait_ge(ldv.sem, base)
                ldt.inc(t.matmul(
                    psF[0:64, 0:1],
                    maccb[(h % 2) * 64:(h % 2) * 64 + 64,
                          (h // 2) * 64:(h // 2) * 64 + 64],
                    q2lT[(h % 2) * 64:(h % 2) * 64 + 64, (h // 2):(h // 2) + 1],
                    start=True, stop=True,
                    tile_position=((h % 2) * 64, 0)))
                t.wait_ge(ldv.sem, 22 + h * 2 + 1)
                ldt.inc(t.transpose(psG[0:1, 0:64], o2fx[0:64, 0:1],
                                    ident[0:64, 0:64]))
            for i in range(NT):
                s_ = i % 2
                base_v = NVA + i * NVB
                t.wait_ge(ttr[s_].sem, 128 * (i // 2 + 1))
                t.wait_ge(ldv.sem, base_v + 4)
                for hp in range(8):
                    for hh in range(2):
                        h = hp * 2 + hh
                        ps = psOa if hh == 0 else psOb
                        if h >= 2:
                            t.wait_ge(ldv.sem, base_v + 4 + h - 1)
                        elif i >= 1:
                            t.wait_ge(ldv.sem, NVA + (i - 1) * NVB + 4 + 15)
                        ldt.inc(t.matmul(
                            ps[:, 0:64],
                            q2T[(h % 2) * 64:(h % 2) * 64 + 64,
                                s_ * D + (h // 2) * 128: s_ * D + (h // 2) * 128 + 128],
                            mtot[(h % 2) * 64:(h % 2) * 64 + 64,
                                 s_ * 512 + (h // 2) * 64: s_ * 512 + (h // 2) * 64 + 64],
                            start=True, stop=True,
                            tile_position=((h % 2) * 64, 0)))
                t.wait_ge(tty[s_].sem, 128 * (i // 2 + 1))
                if i >= 1:
                    t.wait_ge(ldv.sem, NVA + (i - 1) * NVB + 24)
                for dc in range(2):
                    ps = psA if dc == 0 else psB
                    last = None
                    for kc in range(8):
                        last = t.matmul(
                            ps[:, 0:512],
                            yT_[:, s_ * D + kc * 128: s_ * D + (kc + 1) * 128],
                            cpw[:, kc * D + dc * 512: kc * D + dc * 512 + 512],
                            start=(kc == 0), stop=False)
                    last = t.matmul(ps[:, 0:512], ones1[:, :],
                                    cpb[:, dc * 512: dc * 512 + 512],
                                    start=False, stop=True)
                    ldt.inc(last)
            t.wait_ge(ldv.sem, NVA + NT * NVB)
            ldt.inc(t.transpose(psH[0:1, 0:128], amax[0:128, 0:1],
                                ident[:, :]))

        @block.vector
        def _(v):
            def ch(inst):
                ldv.inc(inst)
                v.wait_ge(ldv.sem, ldv.n)
                return inst
            ld.wait(v)
            ldg.wait(v)
            ch(v.memset(macc[:], 0.0))
            for c in range(8):
                ch(v.scalar_tensor_tensor(macc[:], dmall[:, c * 512:(c + 1) * 512],
                                          pm8[:, c:c + 1], macc[:],
                                          op0=ALU.mult, op1=ALU.add))
            ch(v.tensor_copy(maccb[:], macc[:]))
            ch(v.memset(amax[:], 0.0))
            ch(v.tensor_scalar(tql[:], q1l[:], 0.0, 0.125,
                               op0=ALU.min, op1=ALU.mult))
            ch(v.tensor_scalar(q2l[:], q1l[:], 0.0, 0.0025,
                               op0=ALU.max, op1=ALU.mult))
            ch(v.tensor_add(q2l[:], q2l[:], tql[:]))
            for kc in range(8):
                v.wait_ge(ldt.sem, kc + 1)
                ch(v.tensor_copy(q2lT[:, kc:kc + 1], psQ[0:128, 0:1]))
            for h in range(16):
                v.wait_ge(ldt.sem, 8 + h * 2 + 1)
                ch(v.tensor_copy(o2fx[:, :], psF[0:64, 0:1]))
                v.wait_ge(ldt.sem, 8 + h * 2 + 2)
                ch(v.tensor_copy(o2fr[0:1, h * 64:(h + 1) * 64],
                                 psG[0:1, 0:64]))
            ch(v.tensor_scalar_mul(o2fr[:], o2fr[:], flg[:]))
            for i in range(NT):
                s_ = i % 2
                base_t = NTA + i * 18
                v.wait_ge(xin[s_].sem, 16 * (xin_marks[i] // 16))
                ch(v.tensor_scalar(tq2[:], q1t[:, s_ * D:(s_ + 1) * D],
                                   0.0, 0.125, op0=ALU.min, op1=ALU.mult))
                ch(v.tensor_scalar(q2b[:, s_ * D:(s_ + 1) * D],
                                   q1t[:, s_ * D:(s_ + 1) * D],
                                   0.0, 0.0025, op0=ALU.max, op1=ALU.mult))
                ch(v.tensor_add(q2b[:, s_ * D:(s_ + 1) * D],
                                q2b[:, s_ * D:(s_ + 1) * D], tq2[:]))
                if i == 0:
                    ch(v.tensor_copy(mtot[:, 0:512], macc[:]))
                else:
                    ch(v.tensor_add(mtot[:, s_ * 512:(s_ + 1) * 512],
                                    macc[:], mprev[:, s_ * 512:(s_ + 1) * 512]))
                for hp in range(8):
                    for hh in range(2):
                        h = hp * 2 + hh
                        ps = psOa if hh == 0 else psOb
                        v.wait_ge(ldt.sem, base_t + h + 1)
                        ch(v.tensor_add(
                            o2t[:, s_ * D + h * 64: s_ * D + (h + 1) * 64],
                            o2t[:, s_ * D + h * 64: s_ * D + (h + 1) * 64],
                            ps[:, 0:64]))
                v.wait_ge(shf[s_].sem, 32 * (i // 2 + 1))
                ch(v.tensor_add(yl_[:, s_ * D:(s_ + 1) * D],
                                o1t[:, s_ * D:(s_ + 1) * D],
                                o2s[:, s_ * D:(s_ + 1) * D]))
                ch(v.tensor_copy(yb_[:, s_ * D:(s_ + 1) * D],
                                 yl_[:, s_ * D:(s_ + 1) * D]))
                v.wait_ge(ldt.sem, base_t + 18)
                ch(v.tensor_add(yl_[:, s_ * D: s_ * D + 512],
                                x2f[:, s_ * D: s_ * D + 512], psA[:, 0:512]))
                ch(v.tensor_add(yl_[:, s_ * D + 512: s_ * D + 1024],
                                x2f[:, s_ * D + 512: s_ * D + 1024],
                                psB[:, 0:512]))
                ch(v.tensor_reduce(mst[:, 0:1], yl_[:, s_ * D:(s_ + 1) * D],
                                   axis=AX.X, op=ALU.add))
                ch(v.memset(mst[:, 2:3], 0.0))
                ch(v.tensor_scalar_mul(mst[:, 1:2], mst[:, 0:1], 1.0 / D))
                v.wait_ge(lds.sem, i * NSB + 1)
                ch(v.tensor_scalar_mul(mst[:, 3:4], mst[:, 2:3], 1.0 / D))
                ch(v.tensor_tensor(mst[:, 4:5], mst[:, 1:2], mst[:, 1:2],
                                   ALU.mult))
                ch(v.tensor_sub(mst[:, 4:5], mst[:, 3:4], mst[:, 4:5]))
                ch(v.tensor_scalar_add(mst[:, 4:5], mst[:, 4:5], 1e-5))
                v.wait_ge(lds.sem, i * NSB + 2)
                ch(v.reciprocal(mst[:, 6:7], mst[:, 5:6]))
                ch(v.tensor_tensor(mst[:, 7:8], mst[:, 1:2], mst[:, 6:7],
                                   ALU.mult))
                ch(v.tensor_scalar_mul(mst[:, 7:8], mst[:, 7:8], -1.0))
                v.wait_ge(lds.sem, i * NSB + 3)
                ch(v.tensor_reduce(mst[:, 0:1], yl_[:, s_ * D:(s_ + 1) * D],
                                   axis=AX.X, op=ALU.max,
                                   apply_absolute_value=True))
                ch(v.tensor_max(amax[:, 0:1], amax[:, 0:1], mst[:, 0:1]))
            v.wait_ge(ldt.sem, NTA + NT * 18 + 1)
            ch(v.tensor_reduce(amax[0:1, 1:2], psH[0:1, 0:128],
                               axis=AX.X, op=ALU.max))

        @block.scalar
        def _(s):
            for i in range(NT):
                s_ = i % 2
                base_v = NVA + i * NVB
                s.wait_ge(ldv.sem, base_v + 27)
                lds.inc(s.activation(out=sqv[:], in_=yl_[:, s_ * D:(s_ + 1) * D],
                                     func=AF.Square, accum_out=mst[:, 2:3]))
                s.wait_ge(ldv.sem, base_v + 31)
                lds.inc(s.activation(out=mst[:, 5:6], in_=mst[:, 4:5],
                                     func=AF.Sqrt))
                s.wait_ge(ldv.sem, base_v + 34)
                lds.inc(s.activation(out=yl_[:, s_ * D:(s_ + 1) * D],
                                     in_=yl_[:, s_ * D:(s_ + 1) * D],
                                     func=AF.Identity,
                                     bias=mst[:, 7:8], scale=mst[:, 6:7]))
    return nc

def emit_qdq(v, ch, out_ap, in_ap, rsc, sc, ibuf, fbuf, lo, hi):
    ch(v.tensor_scalar_mul(fbuf, in_ap, rsc))
    ch(v.tensor_copy(ibuf, fbuf))
    ch(v.tensor_copy(fbuf, ibuf))
    ch(v.tensor_scalar(fbuf, fbuf, float(hi), float(lo), op0=ALU.min,
                       op1=ALU.max))
    ch(v.tensor_scalar_mul(out_ap, fbuf, sc))


def build_LG():
    nc = bass.Bass("TRN2", target_bir_lowering=False,
                   detect_race_conditions=False, dynamic_dma_scratch_size=4096)
    x3h = nc.dram_tensor("x3h", [TOK, D], F32, kind="ExternalInput").ap()
    am8h = nc.dram_tensor("am8h", [1, 8], F32, kind="ExternalInput").ap()
    wuh = nc.dram_tensor("wuh", [3072, D], F32, kind="ExternalInput").ap()
    wgh = nc.dram_tensor("wgh", [3072, D], F32, kind="ExternalInput").ap()
    identh = nc.dram_tensor("identh", [128, 128], F32, kind="ExternalInput").ap()
    up_o = nc.dram_tensor("up_o", [TOK, 3072], F32, kind="ExternalOutput").ap()
    gt_o = nc.dram_tensor("gt_o", [TOK, 3072], F32, kind="ExternalOutput").ap()
    mx_o = nc.dram_tensor("mx_o", [1, 2], F32, kind="ExternalOutput").ap()

    with ExitStack() as ctx:
        sb = lambda n_, sh, dt: ctx.enter_context(nc.sbuf_tensor(n_, sh, dt))
        pt = lambda n_, sh: ctx.enter_context(nc.psum_tensor(n_, sh, F32))
        wuT = sb("wuT", [128, 8 * 3072], BF16)
        wgT = sb("wgT", [128, 8 * 3072], BF16)
        wtl = sb("wtl", [128, 2 * D], F32)
        wsc = sb("wsc", [128, 2 * 32], F32)
        wrc = sb("wrc", [128, 32], F32)
        wqi = sb("wqi", [128, D], mybir.dt.int32)
        wqf = sb("wqf", [128, D], F32)
        wqb = sb("wqb", [128, 2 * D], BF16)
        am8 = sb("am8", [1, 8], F32)
        gmx = sb("gmx", [128, 4], F32)
        x3f = sb("x3f", [128, 2 * D], F32)
        xqi = sb("xqi", [128, D], mybir.dt.int32)
        xqf = sb("xqf", [128, D], F32)
        xqb = sb("xqb", [128, 2 * D], BF16)
        xqT = sb("xqT", [128, NT * D], BF16)
        urs = sb("urs", [128, 2 * 512], F32)
        mxc = sb("mxc", [128, 4], F32)
        mxr = sb("mxr", [1, 4], F32)
        mrow = sb("mrow", [1, 128], F32)
        ident = sb("ident_s", [128, 128], F32)
        psA = pt("psA", [128, 512])
        psB = pt("psB", [128, 512])
        psC = pt("psC", [128, 512])
        psD = pt("psD", [128, 512])
        psX = pt("psX", [128, 1])
        psT = pt("psT", [1, 128])
        block = ctx.enter_context(nc.Block())

        ld = Ctr(nc, "ld")
        win = [Ctr(nc, "win0"), Ctr(nc, "win1")]
        xin = [Ctr(nc, "xin0"), Ctr(nc, "xin1")]
        twq = [Ctr(nc, "twq0"), Ctr(nc, "twq1")]
        txq = [Ctr(nc, "txq0"), Ctr(nc, "txq1")]
        lgv = Ctr(nc, "lgv")
        lgt = Ctr(nc, "lgt")
        st = Ctr(nc, "st")

        NW = 48
        NVW = 8
        NVX = 6
        vm = {}

        @block.sync
        def _(sync):
            ld.inc(sync.dma_start(out=am8[:], in_=am8h[:, :]), 16)
            ld.inc(sync.dma_start(out=ident[:], in_=identh[:, :]), 16)
            for wi in range(NW):
                s_ = wi % 2
                src = wuh if wi < 24 else wgh
                t24 = wi % 24
                if wi >= 2:
                    sync.wait_ge(lgv.sem, (wi - 1) * NVW)
                win[s_].inc(sync.dma_start(
                    out=wtl[:, s_ * D:(s_ + 1) * D],
                    in_=src[t24 * 128:(t24 + 1) * 128, :]), 16)
                sync.wait_ge(lgv.sem, wi * NVW + 8)
                dst = wuT if wi < 24 else wgT
                for kc in range(8):
                    twq[s_].inc(sync.dma_start_transpose(
                        dst[:, kc * 3072 + t24 * 128: kc * 3072 + (t24 + 1) * 128],
                        wqb[:, s_ * D + kc * 128: s_ * D + (kc + 1) * 128]), 16)
            base_x = NW * NVW + 4
            for i in range(NT):
                s_ = i % 2
                if i >= 2:
                    sync.wait_ge(lgv.sem, base_x + (i - 1) * NVX)
                xin[s_].inc(sync.dma_start(
                    out=x3f[:, s_ * D:(s_ + 1) * D],
                    in_=x3h[i * 128:(i + 1) * 128, :]), 16)
                sync.wait_ge(lgv.sem, base_x + i * NVX + NVX)
                for kc in range(8):
                    txq[s_].inc(sync.dma_start_transpose(
                        xqT[:, i * D + kc * 128: i * D + (kc + 1) * 128],
                        xqb[:, s_ * D + kc * 128: s_ * D + (kc + 1) * 128]), 16)
            base_m = base_x + NT * NVX + 1
            g = 0
            for i in range(NT):
                for mi in range(2):
                    dsto = up_o if mi == 0 else gt_o
                    for oc in range(6):
                        sync.wait_ge(lgv.sem, base_m + g * 3 + 1)
                        st.inc(sync.dma_start(
                            out=dsto[i * 128:(i + 1) * 128,
                                     oc * 512:(oc + 1) * 512],
                            in_=urs[:, (g % 2) * 512:(g % 2 + 1) * 512]), 16)
                        g += 1
            sync.wait_ge(lgv.sem, 48 * 8 + 4 + 16 * 6 + 1 + 192 * 3 + 2)
            st.inc(sync.dma_start(out=mx_o[:, :], in_=mxr[0:1, 0:2]), 16)
            st.wait(sync)

        @block.tensor
        def _(t):
            ld.wait(t)
            t.wait_ge(lgv.sem, 48 * 8 + 2)
            lgt.inc(t.transpose(psX[0:128, 0:1], mrow[0:1, :],
                                ident[0:1, 0:1]))
            base_m = NW * NVW + 4 + NT * NVX + 1
            g = 0
            for i in range(NT):
                s_ = i % 2
                t.wait_ge(txq[s_].sem, 128 * (i // 2 + 1))
                t.wait_ge(twq[0].sem, 24 * 8 * 16)
                t.wait_ge(twq[1].sem, 24 * 8 * 16)
                for mi in range(2):
                    wT = wuT if mi == 0 else wgT
                    for oc in range(6):
                        ps = [psA, psB, psC, psD][g % 4]
                        if g >= 4:
                            t.wait_ge(lgv.sem, base_m + (g - 4) * 3 + 1)
                        last = None
                        for kc in range(8):
                            last = t.matmul(
                                ps[:, 0:512],
                                xqT[:, i * D + kc * 128: i * D + (kc + 1) * 128],
                                wT[:, kc * 3072 + oc * 512: kc * 3072 + oc * 512 + 512],
                                start=(kc == 0), stop=(kc == 7))
                        lgt.inc(last)
                        g += 1
            t.wait_ge(lgv.sem, 48 * 8 + 4 + 16 * 6 + 1 + 192 * 3)
            lgt.inc(t.transpose(psT[0:1, 0:128], mxc[0:128, 0:1],
                                ident[:, :]))
            t.wait_ge(lgv.sem, 48 * 8 + 4 + 16 * 6 + 1 + 192 * 3 + 1)
            lgt.inc(t.transpose(psT[0:1, 0:128], mxc[0:128, 1:2],
                                ident[:, :]))

        @block.vector
        def _(v):
            def ch(inst):
                lgv.inc(inst)
                v.wait_ge(lgv.sem, lgv.n)
                return inst
            ld.wait(v)
            for wi in range(NW):
                s_ = wi % 2
                v.wait_ge(win[s_].sem, 16 * (wi // 2 + 1))
                if wi >= 2:
                    v.wait_ge(twq[s_].sem, 128 * (wi // 2))
                w3 = wtl[:, s_ * D:(s_ + 1) * D].rearrange(
                    "p (a b) -> p a b", b=32)
                ch(v.tensor_reduce(wsc[:, s_ * 32:(s_ + 1) * 32], w3,
                                   axis=AX.X, op=ALU.max,
                                   apply_absolute_value=True))
                ch(v.tensor_scalar(wsc[:, s_ * 32:(s_ + 1) * 32],
                                   wsc[:, s_ * 32:(s_ + 1) * 32],
                                   1.0 / 7.0, 1e-12, op0=ALU.mult,
                                   op1=ALU.max))
                ch(v.reciprocal(wrc[:], wsc[:, s_ * 32:(s_ + 1) * 32]))
                rsb = wrc[:].unsqueeze(2).broadcast_to([128, 32, 32])
                ch(v.tensor_tensor(wqf[:].rearrange("p (a b) -> p a b", b=32),
                                   w3, rsb, ALU.mult))
                ch(v.tensor_copy(wqi[:], wqf[:]))
                ch(v.tensor_copy(wqf[:], wqi[:]))
                ch(v.tensor_scalar(wqf[:], wqf[:], 7.0, -8.0,
                                   op0=ALU.min, op1=ALU.max))
                ssb = wsc[:, s_ * 32:(s_ + 1) * 32].unsqueeze(2).broadcast_to(
                    [128, 32, 32])
                ch(v.tensor_tensor(wqb[:, s_ * D:(s_ + 1) * D].rearrange(
                    "p (a b) -> p a b", b=32),
                    wqf[:].rearrange("p (a b) -> p a b", b=32),
                    ssb, ALU.mult))
            ch(v.tensor_reduce(mxr[0:1, 2:3], am8[0:1, :], axis=AX.X,
                               op=ALU.max))
            ch(v.tensor_copy(mrow[0:1, :],
                             mxr[0:1, 2:3].broadcast_to([1, 128])))
            v.wait_ge(lgt.sem, 1)
            ch(v.tensor_scalar(gmx[:, 0:1], psX[0:128, 0:1], 1.0 / 32767.0,
                               1e-12, op0=ALU.mult, op1=ALU.max))
            ch(v.reciprocal(gmx[:, 1:2], gmx[:, 0:1]))
            for i in range(NT):
                s_ = i % 2
                v.wait_ge(xin[s_].sem, 16 * (i // 2 + 1))
                if i >= 2:
                    v.wait_ge(txq[s_].sem, 128 * (i // 2))
                ch(v.tensor_scalar_mul(xqf[:], x3f[:, s_ * D:(s_ + 1) * D],
                                       gmx[:, 1:2]))
                ch(v.tensor_copy(xqi[:], xqf[:]))
                ch(v.tensor_copy(xqf[:], xqi[:]))
                ch(v.tensor_scalar(xqf[:], xqf[:], 32767.0, -32768.0,
                                   op0=ALU.min, op1=ALU.max))
                ch(v.tensor_scalar_mul(xqf[:], xqf[:], gmx[:, 0:1]))
                ch(v.tensor_copy(xqb[:, s_ * D:(s_ + 1) * D], xqf[:]))
            ch(v.memset(mxc[:, 0:2], 0.0))
            g = 0
            base_m = NW * NVW + 4 + NT * NVX + 1
            for i in range(NT):
                for mi in range(2):
                    for oc in range(6):
                        ps = [psA, psB, psC, psD][g % 4]
                        v.wait_ge(lgt.sem, 1 + g + 1)
                        if g >= 2:
                            v.wait_ge(st.sem, 16 * (g - 1))
                        ch(v.tensor_copy(urs[:, (g % 2) * 512:(g % 2 + 1) * 512],
                                         ps[:, 0:512]))
                        ch(v.tensor_reduce(mxc[:, 2:3],
                                           urs[:, (g % 2) * 512:(g % 2 + 1) * 512],
                                           axis=AX.X, op=ALU.max,
                                           apply_absolute_value=True))
                        ch(v.tensor_max(mxc[:, mi:mi + 1], mxc[:, mi:mi + 1],
                                        mxc[:, 2:3]))
                        g += 1
            v.wait_ge(lgt.sem, 1 + 192 + 1)
            ch(v.tensor_reduce(mxr[0:1, 0:1], psT[0:1, 0:128], axis=AX.X,
                               op=ALU.max))
            v.wait_ge(lgt.sem, 1 + 192 + 2)
            ch(v.tensor_reduce(mxr[0:1, 1:2], psT[0:1, 0:128], axis=AX.X,
                               op=ALU.max))
    return nc

def build_LH():
    nc = bass.Bass("TRN2", target_bir_lowering=False,
                   detect_race_conditions=False, dynamic_dma_scratch_size=4096)
    uph = nc.dram_tensor("uph", [TOK, 3072], F32, kind="ExternalInput").ap()
    gth = nc.dram_tensor("gth", [TOK, 3072], F32, kind="ExternalInput").ap()
    um8h = nc.dram_tensor("um8h", [1, 8], F32, kind="ExternalInput").ap()
    gm8h = nc.dram_tensor("gm8h", [1, 8], F32, kind="ExternalInput").ap()
    identh = nc.dram_tensor("identh", [128, 128], F32, kind="ExternalInput").ap()
    uq_o = nc.dram_tensor("uq_o", [TOK, 3072], F32, kind="ExternalOutput").ap()
    g2_o = nc.dram_tensor("g2_o", [TOK, 3072], F32, kind="ExternalOutput").ap()
    mx_o = nc.dram_tensor("mx_o", [1, 1], F32, kind="ExternalOutput").ap()

    NB = NT * 6
    with ExitStack() as ctx:
        sb = lambda n_, sh, dt: ctx.enter_context(nc.sbuf_tensor(n_, sh, dt))
        pt = lambda n_, sh: ctx.enter_context(nc.psum_tensor(n_, sh, F32))
        um8 = sb("um8", [1, 8], F32)
        gm8 = sb("gm8", [1, 8], F32)
        mrow = sb("mrow", [1, 128], F32)
        ident = sb("ident_s", [128, 128], F32)
        sc4 = sb("sc4", [128, 4], F32)
        upt = sb("upt", [128, 2 * 512], F32)
        gtt = sb("gtt", [128, 2 * 512], F32)
        uqt = sb("uqt", [128, 2 * 512], F32)
        gqt = sb("gqt", [128, 2 * 512], F32)
        sgt = sb("sgt", [128, 512], F32)
        g2t = sb("g2t", [128, 2 * 512], F32)
        ibuf = sb("ibuf", [128, 512], mybir.dt.int32)
        fbuf = sb("fbuf", [128, 512], F32)
        mxc = sb("mxc", [128, 2], F32)
        mxr = sb("mxr", [1, 4], F32)
        psX = pt("psX", [128, 1])
        psT = pt("psT", [1, 128])
        block = ctx.enter_context(nc.Block())

        ld = Ctr(nc, "ld")
        bin_ = [Ctr(nc, "bin0"), Ctr(nc, "bin1")]
        lhv = Ctr(nc, "lhv")
        lhs_ = Ctr(nc, "lhs")
        lht = Ctr(nc, "lht")
        st = Ctr(nc, "st")

        NVA = 9
        NVB = 18
        blocks = [(i, oc) for i in range(NT) for oc in range(6)]

        @block.sync
        def _(sync):
            ld.inc(sync.dma_start(out=um8[:], in_=um8h[:, :]), 16)
            ld.inc(sync.dma_start(out=gm8[:], in_=gm8h[:, :]), 16)
            ld.inc(sync.dma_start(out=ident[:], in_=identh[:, :]), 16)
            for g, (i, oc) in enumerate(blocks):
                s_ = g % 2
                if g >= 2:
                    sync.wait_ge(lhv.sem, NVA + (g - 1) * NVB)
                bin_[s_].inc(sync.dma_start(
                    out=upt[:, s_ * 512:(s_ + 1) * 512],
                    in_=uph[i * 128:(i + 1) * 128, oc * 512:(oc + 1) * 512]), 16)
                bin_[s_].inc(sync.dma_start(
                    out=gtt[:, s_ * 512:(s_ + 1) * 512],
                    in_=gth[i * 128:(i + 1) * 128, oc * 512:(oc + 1) * 512]), 16)
                base = NVA + g * NVB
                sync.wait_ge(lhv.sem, base + 5)
                st.inc(sync.dma_start(
                    out=uq_o[i * 128:(i + 1) * 128, oc * 512:(oc + 1) * 512],
                    in_=uqt[:, s_ * 512:(s_ + 1) * 512]), 16)
                sync.wait_ge(lhv.sem, base + 16)
                st.inc(sync.dma_start(
                    out=g2_o[i * 128:(i + 1) * 128, oc * 512:(oc + 1) * 512],
                    in_=g2t[:, s_ * 512:(s_ + 1) * 512]), 16)
            sync.wait_ge(lhv.sem, NVA + NB * NVB + 1)
            st.inc(sync.dma_start(out=mx_o[:, :], in_=mxr[0:1, 0:1]), 16)
            st.wait(sync)

        @block.tensor
        def _(t):
            ld.wait(t)
            t.wait_ge(lhv.sem, 3)
            lht.inc(t.transpose(psX[0:128, 0:1], mrow[0:1, :], ident[0:1, 0:1]))
            t.wait_ge(lhv.sem, 6)
            lht.inc(t.transpose(psX[0:128, 0:1], mrow[0:1, :], ident[0:1, 0:1]))
            t.wait_ge(lhv.sem, NVA + NB * NVB)
            lht.inc(t.transpose(psT[0:1, 0:128], mxc[0:128, 0:1], ident[:, :]))

        @block.scalar
        def _(s):
            for g, (i, oc) in enumerate(blocks):
                s_ = g % 2
                base = NVA + g * NVB
                s.wait_ge(lhv.sem, base + 10)
                lhs_.inc(s.activation(out=sgt[:], in_=gqt[:, s_ * 512:(s_ + 1) * 512],
                                      func=AF.Sigmoid))

        @block.vector
        def _(v):
            def ch(inst):
                lhv.inc(inst)
                v.wait_ge(lhv.sem, lhv.n)
                return inst
            ld.wait(v)
            ch(v.memset(mxc[:], 0.0))
            ch(v.tensor_reduce(mxr[0:1, 2:3], um8[0:1, :], axis=AX.X,
                               op=ALU.max))
            ch(v.tensor_copy(mrow[0:1, :],
                             mxr[0:1, 2:3].broadcast_to([1, 128])))
            v.wait_ge(lht.sem, 1)
            ch(v.tensor_scalar(sc4[:, 0:1], psX[0:128, 0:1], 1.0 / 32767.0,
                               1e-12, op0=ALU.mult, op1=ALU.max))
            ch(v.tensor_reduce(mxr[0:1, 2:3], gm8[0:1, :], axis=AX.X,
                               op=ALU.max))
            ch(v.tensor_copy(mrow[0:1, :],
                             mxr[0:1, 2:3].broadcast_to([1, 128])))
            v.wait_ge(lht.sem, 2)
            ch(v.tensor_scalar(sc4[:, 2:3], psX[0:128, 0:1], 1.0 / 32767.0,
                               1e-12, op0=ALU.mult, op1=ALU.max))
            ch(v.reciprocal(sc4[:, 1:2], sc4[:, 0:1]))
            ch(v.reciprocal(sc4[:, 3:4], sc4[:, 2:3]))
            for g, (i, oc) in enumerate(blocks):
                s_ = g % 2
                v.wait_ge(bin_[s_].sem, 32 * (g // 2 + 1))
                if g >= 2:
                    v.wait_ge(st.sem, 32 * (g - 1))
                sl5 = slice(s_ * 512, (s_ + 1) * 512)
                emit_qdq(v, ch, uqt[:, sl5], upt[:, sl5], sc4[:, 1:2],
                         sc4[:, 0:1], ibuf[:], fbuf[:], -32768.0, 32767.0)
                emit_qdq(v, ch, gqt[:, sl5], gtt[:, sl5], sc4[:, 3:4],
                         sc4[:, 2:3], ibuf[:], fbuf[:], -32768.0, 32767.0)
                v.wait_ge(lhs_.sem, g + 1)
                ch(v.tensor_scalar_mul(fbuf[:], sgt[:], 65536.0))
                ch(v.tensor_copy(ibuf[:], fbuf[:]))
                ch(v.tensor_copy(fbuf[:], ibuf[:]))
                ch(v.tensor_scalar(fbuf[:], fbuf[:], 65535.0, 0.0,
                                   op0=ALU.min, op1=ALU.max))
                ch(v.tensor_mul(g2t[:, sl5], gqt[:, sl5], fbuf[:]))
                ch(v.tensor_scalar_mul(g2t[:, sl5], g2t[:, sl5],
                                       1.0 / 65536.0))
                ch(v.tensor_reduce(mxc[:, 1:2], g2t[:, sl5], axis=AX.X,
                                   op=ALU.max, apply_absolute_value=True))
                ch(v.tensor_max(mxc[:, 0:1], mxc[:, 0:1], mxc[:, 1:2]))
            v.wait_ge(lht.sem, 3)
            ch(v.tensor_reduce(mxr[0:1, 0:1], psT[0:1, 0:128], axis=AX.X,
                               op=ALU.max))
    return nc


def build_LI():
    nc = bass.Bass("TRN2", target_bir_lowering=False,
                   detect_race_conditions=False, dynamic_dma_scratch_size=4096)
    uqh = nc.dram_tensor("uqh", [TOK, 3072], F32, kind="ExternalInput").ap()
    g2h = nc.dram_tensor("g2h", [TOK, 3072], F32, kind="ExternalInput").ap()
    m8h = nc.dram_tensor("m8h", [1, 8], F32, kind="ExternalInput").ap()
    identh = nc.dram_tensor("identh", [128, 128], F32, kind="ExternalInput").ap()
    o_o = nc.dram_tensor("o_o", [TOK, 3072], F32, kind="ExternalOutput").ap()
    mx_o = nc.dram_tensor("mx_o", [1, 1], F32, kind="ExternalOutput").ap()

    NB = NT * 6
    with ExitStack() as ctx:
        sb = lambda n_, sh, dt: ctx.enter_context(nc.sbuf_tensor(n_, sh, dt))
        pt = lambda n_, sh: ctx.enter_context(nc.psum_tensor(n_, sh, F32))
        m8 = sb("m8", [1, 8], F32)
        mrow = sb("mrow", [1, 128], F32)
        ident = sb("ident_s", [128, 128], F32)
        sc2 = sb("sc2", [128, 2], F32)
        uqt = sb("uqt", [128, 2 * 512], F32)
        g2t = sb("g2t", [128, 2 * 512], F32)
        ot_ = sb("ot_", [128, 2 * 512], F32)
        ibuf = sb("ibuf", [128, 512], mybir.dt.int32)
        fbuf = sb("fbuf", [128, 512], F32)
        mxc = sb("mxc", [128, 2], F32)
        mxr = sb("mxr", [1, 4], F32)
        psX = pt("psX", [128, 1])
        psT = pt("psT", [1, 128])
        block = ctx.enter_context(nc.Block())

        ld = Ctr(nc, "ld")
        bin_ = [Ctr(nc, "bin0"), Ctr(nc, "bin1")]
        liv = Ctr(nc, "liv")
        lit = Ctr(nc, "lit")
        st = Ctr(nc, "st")

        NVA = 6
        NVB = 8
        blocks = [(i, oc) for i in range(NT) for oc in range(6)]

        @block.sync
        def _(sync):
            ld.inc(sync.dma_start(out=m8[:], in_=m8h[:, :]), 16)
            ld.inc(sync.dma_start(out=ident[:], in_=identh[:, :]), 16)
            for g, (i, oc) in enumerate(blocks):
                s_ = g % 2
                if g >= 2:
                    sync.wait_ge(liv.sem, NVA + (g - 1) * NVB)
                bin_[s_].inc(sync.dma_start(
                    out=uqt[:, s_ * 512:(s_ + 1) * 512],
                    in_=uqh[i * 128:(i + 1) * 128, oc * 512:(oc + 1) * 512]), 16)
                bin_[s_].inc(sync.dma_start(
                    out=g2t[:, s_ * 512:(s_ + 1) * 512],
                    in_=g2h[i * 128:(i + 1) * 128, oc * 512:(oc + 1) * 512]), 16)
                base = NVA + g * NVB
                sync.wait_ge(liv.sem, base + 6)
                st.inc(sync.dma_start(
                    out=o_o[i * 128:(i + 1) * 128, oc * 512:(oc + 1) * 512],
                    in_=ot_[:, s_ * 512:(s_ + 1) * 512]), 16)
            sync.wait_ge(liv.sem, NVA + NB * NVB + 1)
            st.inc(sync.dma_start(out=mx_o[:, :], in_=mxr[0:1, 0:1]), 16)
            st.wait(sync)

        @block.tensor
        def _(t):
            ld.wait(t)
            t.wait_ge(liv.sem, 3)
            lit.inc(t.transpose(psX[0:128, 0:1], mrow[0:1, :], ident[0:1, 0:1]))
            t.wait_ge(liv.sem, NVA + NB * NVB)
            lit.inc(t.transpose(psT[0:1, 0:128], mxc[0:128, 0:1], ident[:, :]))

        @block.vector
        def _(v):
            def ch(inst):
                liv.inc(inst)
                v.wait_ge(liv.sem, liv.n)
                return inst
            ld.wait(v)
            ch(v.memset(mxc[:], 0.0))
            ch(v.tensor_reduce(mxr[0:1, 2:3], m8[0:1, :], axis=AX.X,
                               op=ALU.max))
            ch(v.tensor_copy(mrow[0:1, :],
                             mxr[0:1, 2:3].broadcast_to([1, 128])))
            v.wait_ge(lit.sem, 1)
            ch(v.tensor_scalar(sc2[:, 0:1], psX[0:128, 0:1], 1.0 / 32767.0,
                               1e-12, op0=ALU.mult, op1=ALU.max))
            ch(v.reciprocal(sc2[:, 1:2], sc2[:, 0:1]))
            ch(v.memset(mxc[:, 1:2], 0.0))
            for g, (i, oc) in enumerate(blocks):
                s_ = g % 2
                v.wait_ge(bin_[s_].sem, 32 * (g // 2 + 1))
                if g >= 2:
                    v.wait_ge(st.sem, 16 * (g - 1))
                sl5 = slice(s_ * 512, (s_ + 1) * 512)
                emit_qdq(v, ch, fbuf[:], g2t[:, sl5], sc2[:, 1:2],
                         sc2[:, 0:1], ibuf[:], fbuf[:], -32768.0, 32767.0)
                ch(v.tensor_mul(ot_[:, sl5], fbuf[:], uqt[:, sl5]))
                ch(v.tensor_reduce(mxc[:, 1:2], ot_[:, sl5], axis=AX.X,
                                   op=ALU.max, apply_absolute_value=True))
                ch(v.tensor_max(mxc[:, 0:1], mxc[:, 0:1], mxc[:, 1:2]))
            v.wait_ge(lit.sem, 2)
            ch(v.tensor_reduce(mxr[0:1, 0:1], psT[0:1, 0:128], axis=AX.X,
                               op=ALU.max))
    return nc


def build_LJ():
    nc = bass.Bass("TRN2", target_bir_lowering=False,
                   detect_race_conditions=False, dynamic_dma_scratch_size=4096)
    oh = nc.dram_tensor("oh", [TOK, 3072], F32, kind="ExternalInput").ap()
    m8h = nc.dram_tensor("m8h", [1, 8], F32, kind="ExternalInput").ap()
    x3h = nc.dram_tensor("x3h", [TOK, D], F32, kind="ExternalInput").ap()
    wdh = nc.dram_tensor("wdh", [D, 3072], F32, kind="ExternalInput").ap()
    identh = nc.dram_tensor("identh", [128, 128], F32, kind="ExternalInput").ap()
    x4_o = nc.dram_tensor("x4_o", [TOK, D], F32, kind="ExternalOutput").ap()

    with ExitStack() as ctx:
        sb = lambda n_, sh, dt: ctx.enter_context(nc.sbuf_tensor(n_, sh, dt))
        pt = lambda n_, sh: ctx.enter_context(nc.psum_tensor(n_, sh, F32))
        m8 = sb("m8", [1, 8], F32)
        mrow = sb("mrow", [1, 128], F32)
        ident = sb("ident_s", [128, 128], F32)
        sc2 = sb("sc2", [128, 2], F32)
        wdT = sb("wdT", [128, 24 * D], BF16)
        wtl = sb("wtl", [128, 2 * 3072], F32)
        wsc = sb("wsc", [128, 96], F32)
        wrc = sb("wrc", [128, 96], F32)
        wqi = sb("wqi", [128, 3072], mybir.dt.int32)
        wqf = sb("wqf", [128, 3072], F32)
        wqb = sb("wqb", [128, 2 * 3072], BF16)
        otl = sb("otl", [128, 2 * 3072], F32)
        oqi = sb("oqi", [128, 3072], mybir.dt.int32)
        oqf = sb("oqf", [128, 3072], F32)
        oqb = sb("oqb", [128, 2 * 3072], BF16)
        oqT = sb("oqT", [128, 2 * 24 * 128], BF16)
        x3f = sb("x3f", [128, 2 * D], F32)
        yl_ = sb("yl_", [128, 2 * D], F32)
        sqv = sb("sqv", [128, D], F32)
        mst = sb("mst", [128, 8], F32)
        psA = pt("psA", [128, 512])
        psB = pt("psB", [128, 512])
        psX = pt("psX", [128, 1])
        block = ctx.enter_context(nc.Block())

        ld = Ctr(nc, "ld")
        win = [Ctr(nc, "win0"), Ctr(nc, "win1")]
        oin = [Ctr(nc, "oin0"), Ctr(nc, "oin1")]
        twq = [Ctr(nc, "twq0"), Ctr(nc, "twq1")]
        toq = [Ctr(nc, "toq0"), Ctr(nc, "toq1")]
        ljv = Ctr(nc, "ljv")
        ljs = Ctr(nc, "ljs")
        ljt = Ctr(nc, "ljt")
        st = Ctr(nc, "st")

        NW = 8
        NVW = 8
        NVA = 4
        NVB = 24
        NVB = 17
        NSB = 3
        blocksw = list(range(NW))

        @block.sync
        def _(sync):
            ld.inc(sync.dma_start(out=m8[:], in_=m8h[:, :]), 16)
            ld.inc(sync.dma_start(out=ident[:], in_=identh[:, :]), 16)
            for wi in range(NW):
                s_ = wi % 2
                if wi >= 2:
                    sync.wait_ge(ljv.sem, NVA + (wi - 1) * NVW)
                win[s_].inc(sync.dma_start(
                    out=wtl[:, s_ * 3072:(s_ + 1) * 3072],
                    in_=wdh[wi * 128:(wi + 1) * 128, :]), 16)
                sync.wait_ge(ljv.sem, NVA + wi * NVW + NVW)
                for kc in range(24):
                    twq[s_].inc(sync.dma_start_transpose(
                        wdT[:, kc * D + wi * 128: kc * D + (wi + 1) * 128],
                        wqb[:, s_ * 3072 + kc * 128: s_ * 3072 + (kc + 1) * 128]),
                        16)
            base_x = NVA + NW * NVW
            for i in range(NT):
                s_ = i % 2
                if i >= 2:
                    sync.wait_ge(ljv.sem, base_x + (i - 1) * NVB)
                    sync.wait_ge(st.sem, 16 * (i - 1))
                oin[s_].inc(sync.dma_start(
                    out=otl[:, s_ * 3072:(s_ + 1) * 3072],
                    in_=oh[i * 128:(i + 1) * 128, :]), 16)
                oin[s_].inc(sync.dma_start(
                    out=x3f[:, s_ * D:(s_ + 1) * D],
                    in_=x3h[i * 128:(i + 1) * 128, :]), 16)
                sync.wait_ge(ljv.sem, base_x + i * NVB + 5)
                for kc in range(24):
                    toq[s_].inc(sync.dma_start_transpose(
                        oqT[:, s_ * 3072 + kc * 128: s_ * 3072 + (kc + 1) * 128],
                        oqb[:, s_ * 3072 + kc * 128: s_ * 3072 + (kc + 1) * 128]),
                        16)
                sync.wait_ge(ljs.sem, i * NSB + 3)
                st.inc(sync.dma_start(out=x4_o[i * 128:(i + 1) * 128, :],
                                      in_=yl_[:, s_ * D:(s_ + 1) * D]), 16)
            st.wait(sync)

        @block.tensor
        def _(t):
            ld.wait(t)
            t.wait_ge(ljv.sem, 2)
            ljt.inc(t.transpose(psX[0:128, 0:1], mrow[0:1, :], ident[0:1, 0:1]))
            base_x = NVA + NW * NVW
            for i in range(NT):
                s_ = i % 2
                t.wait_ge(toq[s_].sem, 24 * 16 * (i // 2 + 1))
                t.wait_ge(twq[0].sem, 4 * 24 * 16)
                t.wait_ge(twq[1].sem, 4 * 24 * 16)
                if i >= 1:
                    t.wait_ge(ljv.sem, base_x + (i - 1) * NVB + 7)
                for dc in range(2):
                    ps = psA if dc == 0 else psB
                    last = None
                    for kc in range(24):
                        last = t.matmul(
                            ps[:, 0:512],
                            oqT[:, s_ * 3072 + kc * 128: s_ * 3072 + (kc + 1) * 128],
                            wdT[:, kc * D + dc * 512: kc * D + dc * 512 + 512],
                            start=(kc == 0), stop=(kc == 23))
                    ljt.inc(last)

        @block.scalar
        def _(s):
            base_x = NVA + NW * NVW
            for i in range(NT):
                s_ = i % 2
                base_v = base_x + i * NVB
                s.wait_ge(ljv.sem, base_v + 10)
                ljs.inc(s.activation(out=sqv[:], in_=yl_[:, s_ * D:(s_ + 1) * D],
                                     func=AF.Square, accum_out=mst[:, 2:3]))
                s.wait_ge(ljv.sem, base_v + 14)
                ljs.inc(s.activation(out=mst[:, 5:6], in_=mst[:, 4:5],
                                     func=AF.Sqrt))
                s.wait_ge(ljv.sem, base_v + 17)
                ljs.inc(s.activation(out=yl_[:, s_ * D:(s_ + 1) * D],
                                     in_=yl_[:, s_ * D:(s_ + 1) * D],
                                     func=AF.Identity,
                                     bias=mst[:, 7:8], scale=mst[:, 6:7]))

        @block.vector
        def _(v):
            def ch(inst):
                ljv.inc(inst)
                v.wait_ge(ljv.sem, ljv.n)
                return inst
            ld.wait(v)
            ch(v.tensor_reduce(wqf[0:1, 0:1], m8[0:1, :], axis=AX.X,
                               op=ALU.max))
            ch(v.tensor_copy(mrow[0:1, :],
                             wqf[0:1, 0:1].broadcast_to([1, 128])))
            v.wait_ge(ljt.sem, 1)
            ch(v.tensor_scalar(sc2[:, 0:1], psX[0:128, 0:1], 1.0 / 32767.0,
                               1e-12, op0=ALU.mult, op1=ALU.max))
            ch(v.reciprocal(sc2[:, 1:2], sc2[:, 0:1]))
            for wi in range(NW):
                s_ = wi % 2
                v.wait_ge(win[s_].sem, 16 * (wi // 2 + 1))
                if wi >= 2:
                    v.wait_ge(twq[s_].sem, 24 * 16 * (wi // 2))
                w3 = wtl[:, s_ * 3072:(s_ + 1) * 3072].rearrange(
                    "p (a b) -> p a b", b=32)
                ch(v.tensor_reduce(wsc[:], w3, axis=AX.X, op=ALU.max,
                                   apply_absolute_value=True))
                ch(v.tensor_scalar(wsc[:], wsc[:], 1.0 / 7.0, 1e-12,
                                   op0=ALU.mult, op1=ALU.max))
                ch(v.reciprocal(wrc[:], wsc[:]))
                rsb = wrc[:].unsqueeze(2).broadcast_to([128, 96, 32])
                ch(v.tensor_tensor(wqf[:].rearrange("p (a b) -> p a b", b=32),
                                   w3, rsb, ALU.mult))
                ch(v.tensor_copy(wqi[:], wqf[:]))
                ch(v.tensor_copy(wqf[:], wqi[:]))
                ch(v.tensor_scalar(wqf[:], wqf[:], 7.0, -8.0,
                                   op0=ALU.min, op1=ALU.max))
                ssb = wsc[:].unsqueeze(2).broadcast_to([128, 96, 32])
                ch(v.tensor_tensor(wqb[:, s_ * 3072:(s_ + 1) * 3072].rearrange(
                    "p (a b) -> p a b", b=32),
                    wqf[:].rearrange("p (a b) -> p a b", b=32),
                    ssb, ALU.mult))
            base_x = NVA + NW * NVW
            for i in range(NT):
                s_ = i % 2
                v.wait_ge(oin[s_].sem, 32 * (i // 2 + 1))
                if i >= 2:
                    v.wait_ge(toq[s_].sem, 24 * 16 * (i // 2))
                sl3 = slice(s_ * 3072, (s_ + 1) * 3072)
                ch(v.tensor_scalar_mul(oqf[:], otl[:, sl3], sc2[:, 1:2]))
                ch(v.tensor_copy(oqi[:], oqf[:]))
                ch(v.tensor_copy(oqf[:], oqi[:]))
                ch(v.tensor_scalar(oqf[:], oqf[:], 32767.0, -32768.0,
                                   op0=ALU.min, op1=ALU.max))
                ch(v.tensor_scalar_mul(oqb[:, sl3], oqf[:], sc2[:, 0:1]))
                if i >= 2:
                    v.wait_ge(st.sem, 16 * (i - 1))
                v.wait_ge(ljt.sem, 1 + i * 2 + 1)
                ch(v.tensor_add(yl_[:, s_ * D: s_ * D + 512],
                                x3f[:, s_ * D: s_ * D + 512], psA[:, 0:512]))
                v.wait_ge(ljt.sem, 1 + i * 2 + 2)
                ch(v.tensor_add(yl_[:, s_ * D + 512: s_ * D + 1024],
                                x3f[:, s_ * D + 512: s_ * D + 1024],
                                psB[:, 0:512]))
                ch(v.tensor_reduce(mst[:, 0:1], yl_[:, s_ * D:(s_ + 1) * D],
                                   axis=AX.X, op=ALU.add))
                ch(v.memset(mst[:, 2:3], 0.0))
                ch(v.tensor_scalar_mul(mst[:, 1:2], mst[:, 0:1], 1.0 / D))
                v.wait_ge(ljs.sem, i * NSB + 1)
                ch(v.tensor_scalar_mul(mst[:, 3:4], mst[:, 2:3], 1.0 / D))
                ch(v.tensor_tensor(mst[:, 4:5], mst[:, 1:2], mst[:, 1:2],
                                   ALU.mult))
                ch(v.tensor_sub(mst[:, 4:5], mst[:, 3:4], mst[:, 4:5]))
                ch(v.tensor_scalar_add(mst[:, 4:5], mst[:, 4:5], 1e-5))
                v.wait_ge(ljs.sem, i * NSB + 2)
                ch(v.reciprocal(mst[:, 6:7], mst[:, 5:6]))
                ch(v.tensor_tensor(mst[:, 7:8], mst[:, 1:2], mst[:, 6:7],
                                   ALU.mult))
                ch(v.tensor_scalar_mul(mst[:, 7:8], mst[:, 7:8], -1.0))
    return nc


def kernel(**inputs):
    x = np.asarray(inputs["L0_attn_x"], dtype=np.float32)
    xs = shard_x(x)
    ident = np.eye(128, dtype=np.float32)
    ones = np.ones((1, 128), np.float32)
    B = 2

    resA = host_LA(inputs, xs)
    ktb = {b: np.concatenate([resA[b * 4 + i]["kt_o"] for i in range(4)],
                             axis=1) for b in range(B)}
    vbt = {b: np.concatenate([resA[b * 4 + i]["v_o"] for i in range(4)],
                             axis=0) for b in range(B)}

    rel = np.asarray(inputs["L0_attn_rel_emb"])
    ridx = np.minimum(REL_CON - np.arange(REL_TT), 4094)
    relp = np.ascontiguousarray(rel[:, ridx]).astype(np.float32)
    x1 = {}
    for m in range(4):
        nc = build_LB(m)
        cs = [m, m + 4]
        ims = [{"x": xs[c], "qt": resA[c]["qt_o"], "ktb": ktb[c // 4],
                "vb": vbt[c // 4], "relp": relp, "ident": ident} for c in cs]
        res = run_bass_kernel_spmd(nc, ims, core_ids=cs)
        for i, c in enumerate(cs):
            x1[c] = res.results[i]["x1_o"]

    gwb = float(np.asarray(inputs["L1_attn_gw_b"]).reshape(-1)[0])
    swb = float(np.asarray(inputs["L1_attn_sw_b"]).reshape(-1)[0])
    gw_w = np.asarray(inputs["L1_attn_gw_w"])
    sw_w = np.asarray(inputs["L1_attn_sw_w"])
    wgbd = np.zeros((1024, 16), np.float32)
    wsbd = np.zeros((1024, 16), np.float32)
    for h in range(16):
        wgbd[h * 64:(h + 1) * 64, h] = gw_w[:, 0]
        wsbd[h * 64:(h + 1) * 64, h] = sw_w[:, 0]
    baseC = {
        "wrh": w_moving(np.ascontiguousarray(np.asarray(inputs["L0_ffn_W_r"]).T)),
        "wkh": w_moving(np.ascontiguousarray(np.asarray(inputs["L0_ffn_W_k"]).T)),
        "wvh": w_moving(np.ascontiguousarray(np.asarray(inputs["L0_ffn_W_v"]).T)),
        "murh": np.tile(np.asarray(inputs["L0_ffn_mu_r"]).reshape(1, -1),
                        (128, 1)).astype(np.float32),
        "mukh": np.tile(np.asarray(inputs["L0_ffn_mu_k"]).reshape(1, -1),
                        (128, 1)).astype(np.float32),
        "q1wh": w_moving(np.asarray(inputs["L1_attn_q1_w"])),
        "k1wh": w_moving(np.asarray(inputs["L1_attn_k1_w"])),
        "k2wh": w_moving(np.asarray(inputs["L1_attn_k2_w"])),
        "q1bh": np.asarray(inputs["L1_attn_q1_b"]).reshape(1, -1).astype(np.float32),
        "k1bh": np.asarray(inputs["L1_attn_k1_b"]).reshape(1, -1).astype(np.float32),
        "k2bh": np.asarray(inputs["L1_attn_k2_b"]).reshape(1, -1).astype(np.float32),
        "wgbdh": w_moving(wgbd),
        "wsbdh": w_moving(wsbd),
        "onesh": ones, "identh": ident,
        "gswbh": np.tile(np.array([[gwb, swb]], np.float32), (128, 1)),
    }
    ncC = build_LC(gwb, swb)
    imsC = []
    for c in range(8):
        pl = np.zeros((1, 1024), np.float32) if c % 4 == 0 else \
            np.ascontiguousarray(x1[c - 1][-1:]).astype(np.float32)
        imsC.append(dict(baseC, x1h=x1[c], plh=pl))
    resC = run_bass_kernel_spmd(ncC, imsC, core_ids=list(range(8))).results

    gs = np.zeros((16, 8), np.float32)
    for c in range(8):
        gs[:, c] = resC[c]["gsum_o"][:, 0]
    maskU = np.triu(np.ones((128, 128), np.float32))
    ncD = build_LD()
    imsD = []
    for c in range(8):
        pm = np.zeros((16, 8), np.float32)
        for c2 in range(8):
            if c2 // 4 == c // 4 and c2 < c:
                pm[:, c2] = 1.0
        imsD.append({"x2h": resC[c]["x2_o"], "q1h": resC[c]["q1_o"],
                     "k1h": resC[c]["k1_o"], "lgth": resC[c]["lgt_o"],
                     "rsh": resC[c]["rs_o"], "gsh": gs, "pmh": pm,
                     "maskh": maskU, "identh": ident})
    resD = run_bass_kernel_spmd(ncD, imsD, core_ids=list(range(8))).results

    dsall = np.zeros((128, 8 * 512), np.float32)
    for c in range(8):
        dsall[:, c * 512:(c + 1) * 512] = resD[c]["sa_o"][NT - 1]
    ncE = build_LE()
    imsE = []
    for c in range(8):
        pm = np.zeros((128, 8), np.float32)
        for c2 in range(8):
            if c2 // 4 == c // 4 and c2 < c:
                pm[:, c2] = 1.0
        if c % 4 == 3:
            x2n1 = np.zeros((1, 1024), np.float32)
        else:
            x2n1 = np.ascontiguousarray(resC[c + 1]["x2_o"][0:1])
        em = np.ones((128, NT), np.float32)
        if c % 4 == 3:
            em[127, NT - 1] = 0.0
        imsE.append({"x2h": resC[c]["x2_o"], "q1h": resC[c]["q1_o"],
                     "k2h": resC[c]["k2_o"], "gch": resD[c]["gc_o"],
                     "o1ih": resD[c]["o1_o"], "sah": resD[c]["sa_o"],
                     "dsh": dsall, "pmh": pm, "x2nh": x2n1, "emh": em,
                     "maskh": maskU})
    resE = run_bass_kernel_spmd(ncE, imsE, core_ids=list(range(8))).results

    dmall = np.zeros((128, 8 * 512), np.float32)
    for c in range(8):
        dmall[:, c * 512:(c + 1) * 512] = resE[c]["ma_o"][NT - 1]
    ncF = build_LF()
    imsF = []
    for c in range(8):
        pm = np.zeros((128, 8), np.float32)
        for c2 in range(8):
            if c2 // 4 == c // 4 and c2 < c:
                pm[:, c2] = 1.0
        if c % 4 == 0:
            q1l = np.zeros((1, 1024), BF16NP)
            flag = np.array([[0.0]], np.float32)
        else:
            q1l = np.ascontiguousarray(resC[c - 1]["q1_o"][-1:])
            flag = np.array([[1.0]], np.float32)
        imsF.append({"x2h": resC[c]["x2_o"], "o1h": resE[c]["o1_o"],
                     "o2ih": resE[c]["o2i_o"], "mah": resE[c]["ma_o"],
                     "dmh": dmall, "pmh": pm, "q1h": resC[c]["q1_o"],
                     "q1lh": q1l, "flagh": flag,
                     "cpwh": w_moving(np.asarray(inputs["L1_attn_cp_w"])),
                     "cpbh": np.asarray(inputs["L1_attn_cp_b"]).reshape(1, -1).astype(np.float32),
                     "onesh": ones, "identh": ident})
    resF = run_bass_kernel_spmd(ncF, imsF, core_ids=list(range(8))).results

    am8 = np.zeros((1, 8), np.float32)
    for c in range(8):
        am8[0, c] = resF[c]["amax_o"][0, 0]
    ncG = build_LG()
    wuf = np.asarray(inputs["L1_ffn_w_up"]).astype(np.float32)
    wgf = np.asarray(inputs["L1_ffn_w_gate"]).astype(np.float32)
    imsG = [{"x3h": resF[c]["x3_o"], "am8h": am8, "wuh": wuf, "wgh": wgf,
             "identh": ident} for c in range(8)]
    resG = run_bass_kernel_spmd(ncG, imsG, core_ids=list(range(8))).results

    um8 = np.zeros((1, 8), np.float32)
    gm8 = np.zeros((1, 8), np.float32)
    for c in range(8):
        um8[0, c] = resG[c]["mx_o"][0, 0]
        gm8[0, c] = resG[c]["mx_o"][0, 1]
    ncH = build_LH()
    imsH = [{"uph": resG[c]["up_o"], "gth": resG[c]["gt_o"], "um8h": um8,
             "gm8h": gm8, "identh": ident} for c in range(8)]
    resH = run_bass_kernel_spmd(ncH, imsH, core_ids=list(range(8))).results

    m8 = np.zeros((1, 8), np.float32)
    for c in range(8):
        m8[0, c] = resH[c]["mx_o"][0, 0]
    ncI = build_LI()
    imsI = [{"uqh": resH[c]["uq_o"], "g2h": resH[c]["g2_o"], "m8h": m8,
             "identh": ident} for c in range(8)]
    resI = run_bass_kernel_spmd(ncI, imsI, core_ids=list(range(8))).results

    m8o = np.zeros((1, 8), np.float32)
    for c in range(8):
        m8o[0, c] = resI[c]["mx_o"][0, 0]
    ncJ = build_LJ()
    wdf = np.asarray(inputs["L1_ffn_w_down"]).astype(np.float32)
    imsJ = [{"oh": resI[c]["o_o"], "m8h": m8o, "x3h": resF[c]["x3_o"],
             "wdh": wdf, "identh": ident} for c in range(8)]
    resJ = run_bass_kernel_spmd(ncJ, imsJ, core_ids=list(range(8))).results

    out = np.empty((2, 8192, 1024), np.float32)
    for c in range(8):
        b, t0 = divmod(c, 4)
        out[b, t0 * TOK:(t0 + 1) * TOK] = resJ[c]["x4_o"]
    return out


# version 02
# speedup vs baseline: 2.3548x; 2.3548x over previous
from contextlib import ExitStack
import numpy as np
import ml_dtypes
import concourse.bass as bass
import concourse.mybir as mybir
from concourse.bass_utils import run_bass_kernel_spmd

F32 = mybir.dt.float32
BF16 = mybir.dt.bfloat16
AF = mybir.ActivationFunctionType
ALU = mybir.AluOpType
AX = mybir.AxisListType
BF16NP = ml_dtypes.bfloat16

TOK = 2048
NT = TOK // 128
D = 1024
NH = 16
DH = 64
L = 8192
REL_CON = 4731
REL_TT = 4732


class Ctr:

    def __init__(self, nc, name):
        self.sem = nc.alloc_semaphore(name)
        self.n = 0

    def inc(self, inst, k=1):
        inst.then_inc(self.sem, k)
        self.n += k
        return inst

    def wait(self, eng, n=None):
        eng.wait_ge(self.sem, self.n if n is None else n)


def w_moving(W):
    dout = W.shape[1]
    return np.ascontiguousarray(
        W.reshape(8, 128, dout).transpose(1, 0, 2).reshape(128, 8 * dout)
    ).astype(np.float32)


def w_moving_g(W, kchunks):
    dout = W.shape[1]
    return np.ascontiguousarray(
        W.reshape(kchunks, 128, dout).transpose(1, 0, 2).reshape(128, kchunks * dout)
    ).astype(np.float32)


def b_fm(b):
    return np.ascontiguousarray(np.asarray(b).reshape(8, 128).T).astype(np.float32)


def shard_x(x):
    out = []
    for c in range(8):
        b, t0 = divmod(c, 4)
        out.append(np.ascontiguousarray(x[b, t0 * TOK:(t0 + 1) * TOK]))
    return out


def load_x_tokmaj(ctr, eng, dst_sb, src_hbm, ntiles=NT, d=D):
    for i in range(ntiles):
        ctr.inc(eng.dma_start(out=dst_sb[:, i * d:(i + 1) * d],
                              in_=src_hbm[i * 128:(i + 1) * 128, :]), 16)


def store_x_tokmaj(ctr, eng, dst_hbm, src_sb, ntiles=NT, d=D):
    for i in range(ntiles):
        ctr.inc(eng.dma_start(out=dst_hbm[i * 128:(i + 1) * 128, :],
                              in_=src_sb[:, i * d:(i + 1) * d]), 16)


def build_LA():
    nc = bass.Bass("TRN2", target_bir_lowering=False, detect_race_conditions=False)
    x = nc.dram_tensor("x", [TOK, D], F32, kind="ExternalInput").ap()
    wq = nc.dram_tensor("wq", [128, 8 * D], F32, kind="ExternalInput").ap()
    wk = nc.dram_tensor("wk", [128, 8 * D], F32, kind="ExternalInput").ap()
    wv = nc.dram_tensor("wv", [128, 8 * D], F32, kind="ExternalInput").ap()
    bq = nc.dram_tensor("bq", [128, 8], F32, kind="ExternalInput").ap()
    bk = nc.dram_tensor("bk", [128, 8], F32, kind="ExternalInput").ap()
    bvr = nc.dram_tensor("bvr", [1, D], F32, kind="ExternalInput").ap()
    ones_h = nc.dram_tensor("ones", [1, 128], F32, kind="ExternalInput").ap()
    qt_o = nc.dram_tensor("qt_o", [D, TOK], BF16, kind="ExternalOutput").ap()
    kt_o = nc.dram_tensor("kt_o", [D, TOK], BF16, kind="ExternalOutput").ap()
    v_o = nc.dram_tensor("v_o", [TOK, D], BF16, kind="ExternalOutput").ap()

    with (
        nc.sbuf_tensor("xb", [128, NT * D], BF16) as xb,
        nc.sbuf_tensor("xT", [128, 8 * TOK], BF16) as xT,
        nc.sbuf_tensor("wqs", [128, 8 * D], BF16) as wqs,
        nc.sbuf_tensor("wks", [128, 8 * D], BF16) as wks,
        nc.sbuf_tensor("wvs", [128, 8 * D], BF16) as wvs,
        nc.sbuf_tensor("bqs", [128, 8], F32) as bqs,
        nc.sbuf_tensor("bks", [128, 8], F32) as bks,
        nc.sbuf_tensor("bvrow", [1, D], BF16) as bvrow,
        nc.sbuf_tensor("ones1", [1, 128], BF16) as ones1,
        nc.sbuf_tensor("qt_s", [128, 8 * TOK], BF16) as qt_s,
        nc.sbuf_tensor("kt_s", [128, 8 * TOK], BF16) as kt_s,
        nc.psum_tensor("ps0", [128, 512], F32) as ps0,
        nc.psum_tensor("ps1", [128, 512], F32) as ps1,
        nc.psum_tensor("ps2", [128, 512], F32) as ps2,
        nc.psum_tensor("ps3", [128, 512], F32) as ps3,
        nc.Block() as block,
    ):
        psums = [ps0, ps1, ps2, ps3]
        ld = Ctr(nc, "ld")
        ldg = Ctr(nc, "ldg")
        xc = Ctr(nc, "xc")
        td = Ctr(nc, "td")
        mm = Ctr(nc, "mm")
        evq = Ctr(nc, "evq")
        evv = Ctr(nc, "evv")
        st = Ctr(nc, "st")

        @block.sync
        def _(sync):
            ld.inc(sync.dma_start(out=bqs[:], in_=bq[:, :]), 16)
            ld.inc(sync.dma_start(out=bks[:], in_=bk[:, :]), 16)

        @block.gpsimd
        def _(gp):
            ldg.inc(gp.dma_start(out=wqs[:], in_=wq[:, :]), 16)
            ldg.inc(gp.dma_start(out=wks[:], in_=wk[:, :]), 16)
            ldg.inc(gp.dma_start(out=wvs[:], in_=wv[:, :]), 16)
            ldg.inc(gp.dma_start(out=ones1[:], in_=ones_h[:, :]), 16)
            ldg.inc(gp.dma_start(out=bvrow[:], in_=bvr[:, :]), 16)
            for i in range(NT):
                xc.inc(gp.dma_start(out=xb[:, i * D:(i + 1) * D],
                                    in_=x[i * 128:(i + 1) * 128, :]), 16)

        @block.scalar
        def _(s):
            s.wait_ge(xc.sem, 16 * NT)
            for i in range(NT):
                for kc in range(8):
                    td.inc(s.dma_start_transpose(
                        xT[:, kc * TOK + i * 128: kc * TOK + (i + 1) * 128],
                        xb[:, i * D + kc * 128: i * D + (kc + 1) * 128]), 16)

        nqk = 64
        nv = 32
        @block.tensor
        def _(t):
            t.wait_ge(td.sem, td.n)
            ld.wait(t)
            ldg.wait(t)
            g = 0
            for name, wsb in (("q", wqs), ("k", wks)):
                for oc in range(8):
                    for tc in range(4):
                        if g >= 4:
                            t.wait_ge(evq.sem, g - 3)
                        ps = psums[g % 4]
                        for kc in range(8):
                            inst = t.matmul(ps[:, 0:512],
                                     wsb[:, kc * D + oc * 128: kc * D + (oc + 1) * 128],
                                     xT[:, kc * TOK + tc * 512: kc * TOK + (tc + 1) * 512],
                                     start=(kc == 0), stop=(kc == 7))
                        mm.inc(inst)
                        g += 1
            t.wait_ge(evq.sem, 64)
            for tt in range(NT):
                for dc in range(2):
                    gv = tt * 2 + dc
                    if gv >= 4:
                        t.wait_ge(evv.sem, gv - 3)
                    ps = psums[g % 4]
                    for kc in range(8):
                        t.matmul(ps[:, 0:512],
                                 xT[:, kc * TOK + tt * 128: kc * TOK + (tt + 1) * 128],
                                 wvs[:, kc * D + dc * 512: kc * D + (dc + 1) * 512],
                                 start=(kc == 0), stop=False)
                    inst = t.matmul(ps[:, 0:512], ones1[:, :],
                             bvrow[:, dc * 512:(dc + 1) * 512],
                             start=False, stop=True)
                    mm.inc(inst)
                    g += 1

        @block.scalar
        def _(s):
            g = 0
            for name, bsb in (("q", bqs), ("k", bks)):
                dst = qt_s if name == "q" else kt_s
                for oc in range(8):
                    for tc in range(4):
                        s.wait_ge(mm.sem, g + 1)
                        ps = psums[g % 4]
                        evq.inc(s.activation(
                            out=dst[:, oc * TOK + tc * 512: oc * TOK + (tc + 1) * 512],
                            in_=ps[:, 0:512], func=AF.Identity,
                            bias=bsb[:, oc:oc + 1]))
                        g += 1

        @block.vector
        def _(v):
            v.wait_ge(td.sem, 16 * 8 * NT)
            for tt in range(NT):
                for dc in range(2):
                    g = 64 + tt * 2 + dc
                    v.wait_ge(mm.sem, g + 1)
                    ps = psums[g % 4]
                    evv.inc(v.tensor_copy(
                        xb[:, tt * D + dc * 512: tt * D + (dc + 1) * 512],
                        ps[:, 0:512]))

        @block.sync
        def _(sync):
            sync.wait_ge(evq.sem, evq.n)
            sync.wait_ge(evv.sem, evv.n)
            for oc in range(8):
                st.inc(sync.dma_start(
                    out=qt_o[oc * 128:(oc + 1) * 128, :],
                    in_=qt_s[:, oc * TOK:(oc + 1) * TOK]), 16)
                st.inc(sync.dma_start(
                    out=kt_o[oc * 128:(oc + 1) * 128, :],
                    in_=kt_s[:, oc * TOK:(oc + 1) * TOK]), 16)
            store_x_tokmaj(st, sync, v_o, xb)
            st.wait(sync)
    return nc


def host_LA(inputs, xs):
    nc = build_LA()
    ones = np.ones((1, 128), dtype=np.float32)
    base = {
        "wq": w_moving(np.asarray(inputs["L0_attn_Wq"])),
        "wk": w_moving(np.asarray(inputs["L0_attn_Wk"])),
        "wv": w_moving(np.asarray(inputs["L0_attn_Wv"])),
        "bq": b_fm(inputs["L0_attn_bq"]),
        "bk": b_fm(inputs["L0_attn_bk"]),
        "bvr": np.asarray(inputs["L0_attn_bv"], np.float32).reshape(1, D),
        "ones": ones,
    }
    in_maps = [dict(base, x=xs[c]) for c in range(8)]
    res = run_bass_kernel_spmd(nc, in_maps, core_ids=list(range(8)))
    return res.results



def build_LB(core_id):
    qbase = (core_id % 4) * TOK
    nc = bass.Bass("TRN2", target_bir_lowering=False, detect_race_conditions=False)
    x = nc.dram_tensor("x", [TOK, D], F32, kind="ExternalInput").ap()
    qt = nc.dram_tensor("qt", [D, TOK], BF16, kind="ExternalInput").ap()
    ktb = nc.dram_tensor("ktb", [D, L], BF16, kind="ExternalInput").ap()
    vb = nc.dram_tensor("vb", [L, D], BF16, kind="ExternalInput").ap()
    relp = nc.dram_tensor("relp", [16, REL_TT], F32, kind="ExternalInput").ap()
    ident_h = nc.dram_tensor("ident", [128, 128], F32, kind="ExternalInput").ap()
    x1_o = nc.dram_tensor("x1_o", [TOK, D], F32, kind="ExternalOutput").ap()
    dbg_o = nc.dram_tensor("dbg_o", [TOK, D], F32, kind="ExternalOutput").ap()

    NKT = L // 128

    def tailp(qbp, qb, kt):
        q0 = qbase + qbp * 1024 + qb * 512
        return kt * 128 >= q0 + 3971

    steps = [(kt, qb) for kt in range(NKT) for qb in range(2)]
    NS = len(steps)

    pt_prereq = []
    exp_cnt = 0
    tad_cnt = 0
    tidx_total = 0
    for pair in range(8):
        for qbp in range(2):
            for (kt, qb) in steps:
                for hh in range(2):
                    exp_cnt += 1
                    if tailp(qbp, qb, kt):
                        tad_cnt += 1
                        pt_prereq.append(("tad", tad_cnt))
                        tidx_total += 1
                    else:
                        pt_prereq.append(("expd", exp_cnt))

    with ExitStack() as ctx:
        sb = lambda name, shape, dt: ctx.enter_context(nc.sbuf_tensor(name, shape, dt))
        pt = lambda name, shape: ctx.enter_context(nc.psum_tensor(name, shape, F32))
        qt_s = sb("qt_s", [128, 8 * TOK], BF16)
        osb = sb("osb", [128, NT * D], F32)
        rels = sb("rels", [16, REL_TT], F32)
        wt = sb("wt", [16, REL_TT], BF16)
        negbh = sb("negbh", [16, 1], F32)
        ident = sb("ident_s", [128, 128], F32)
        kfl = sb("kfl", [128, 4 * 128], BF16)
        vfl = sb("vfl", [128, 4 * 130], BF16)
        ptb = sb("ptb", [128, 8 * 512], BF16)
        ttl = sb("ttl", [128, 2 * 512], BF16)
        oTs = sb("oT", [65, 2 * 512], F32)
        rs = sb("rs", [128, 16], F32)
        xl = sb("xl", [128, 2 * D], F32)
        yl = sb("yl", [128, 2 * D], F32)
        sqs = sb("sqs", [128, D], F32)
        mst = sb("mst", [128, 8], F32)
        ps_st0 = pt("ps_st0", [128, 512])
        ps_st1 = pt("ps_st1", [128, 512])
        ps_av0 = pt("ps_av0", [65, 512])
        ps_av1 = pt("ps_av1", [65, 512])
        ps_av2 = pt("ps_av2", [65, 512])
        ps_av3 = pt("ps_av3", [65, 512])
        ps_tr0 = pt("ps_tr0", [128, 65])
        ps_tr1 = pt("ps_tr1", [128, 65])
        block = ctx.enter_context(nc.Block())
        ps_avs = [ps_av0, ps_av1, ps_av2, ps_av3]
        ps_trs = [ps_tr0, ps_tr1]
        ps_sts = [ps_st0, ps_st1]
        ld = Ctr(nc, "ld")
        vin = Ctr(nc, "vin")
        wtd = Ctr(nc, "wtd")
        kv = [Ctr(nc, f"kv{i}") for i in range(4)]
        std = Ctr(nc, "std")
        expd = Ctr(nc, "expd")
        tld = [Ctr(nc, f"tld{i}") for i in range(2)]
        tad = Ctr(nc, "tad")
        avd = Ctr(nc, "avd")
        oev = Ctr(nc, "oev")
        trd = Ctr(nc, "trd")
        osc = Ctr(nc, "osc")
        rec = Ctr(nc, "rec")
        xld = [Ctr(nc, f"xld{i}") for i in range(2)]
        lnv = Ctr(nc, "lnv")
        lns = Ctr(nc, "lns")
        st = Ctr(nc, "st")

        @block.sync
        def _(sync):
            for oc in range(8):
                ld.inc(sync.dma_start(
                    out=qt_s[:, oc * TOK:(oc + 1) * TOK],
                    in_=qt[oc * 128:(oc + 1) * 128, :]), 16)
            ld.inc(sync.dma_start(out=rels[:], in_=relp[:, :]), 16)
            ld.inc(sync.dma_start(out=ident[:], in_=ident_h[:, :]), 16)

        @block.vector
        def _(v):
            ld.wait(v)
            for s in range(4):
                v.memset(vfl[:, s * 130 + 64: s * 130 + 65], 1.0)
                v.memset(vfl[:, s * 130 + 129: s * 130 + 130], 1.0)
            vin.inc(v.tensor_scalar_mul(negbh[:], rels[:, 637:638], -1.0))

        @block.scalar
        def _(s):
            s.wait_ge(vin.sem, 1)
            wtd.inc(s.activation(out=wt[:], in_=rels[:], func=AF.Exp,
                                 bias=negbh[:]))

        @block.sync
        def _(sync):
            sync.wait_ge(wtd.sem, 1)
            sync.wait_ge(vin.sem, 1)
            g = 0
            tidx = 0
            for pair in range(8):
                for qbp in range(2):
                    for kt in range(NKT):
                        slot = g % 4
                        if g >= 4:
                            sync.wait_ge(std.sem, 4 * (g - 3))
                            sync.wait_ge(avd.sem, 4 * (g - 3))
                        kv[slot].inc(sync.dma_start(
                            out=kfl[:, slot * 128:(slot + 1) * 128],
                            in_=ktb[pair * 128:(pair + 1) * 128,
                                    kt * 128:(kt + 1) * 128]), 16)
                        dstv = bass.AP(tensor=vfl[:].tensor,
                                       offset=slot * 130,
                                       ap=[[4 * 130, 128], [65, 2], [1, 64]])
                        srcv = bass.AP(tensor=vb.tensor,
                                       offset=kt * 128 * D + pair * 128,
                                       ap=[[D, 128], [64, 2], [1, 64]])
                        kv[slot].inc(sync.dma_start(out=dstv, in_=srcv), 16)
                        for qb in range(2):
                            if tailp(qbp, qb, kt):
                                for hh in range(2):
                                    h = pair * 2 + hh
                                    ts = tidx % 2
                                    if tidx >= 2:
                                        sync.wait_ge(tad.sem, tidx - 1)
                                    q0 = qbase + qbp * 1024 + qb * 512
                                    base = kt * 128 - q0 - 3971
                                    src = bass.AP(
                                        tensor=wt[:].tensor,
                                        offset=h * REL_TT + base,
                                        ap=[[REL_TT, 1], [1, 128], [1, 512]])
                                    tld[ts].inc(sync.dma_start(
                                        out=ttl[:, ts * 512:(ts + 1) * 512],
                                        in_=src), 16)
                                    tidx += 1
                        g += 1

        @block.tensor
        def _(t):
            ld.wait(t)
            pt_idx = 0
            av_idx = 0
            oev_cnt = 0
            osc_cnt = 0
            g = 0

            def emit_st(pair, qbp, kt, qb, g):
                nonlocal pt_idx
                slot = g % 4
                t.wait_ge(kv[slot].sem, 32 * (g // 4 + 1))
                for hh in range(2):
                    prev = pt_idx + hh - 2
                    if prev >= 0:
                        t.wait_ge(expd.sem, prev + 1)
                    std.inc(t.matmul(
                        ps_sts[hh][:, 0:512],
                        kfl[hh * 64:(hh + 1) * 64, slot * 128:(slot + 1) * 128],
                        qt_s[hh * 64:(hh + 1) * 64,
                             pair * TOK + qbp * 1024 + qb * 512:
                             pair * TOK + qbp * 1024 + (qb + 1) * 512],
                        start=True, stop=True,
                        tile_position=(hh * 64, 0)))
                pt_idx += 2

            def emit_av(pair, qbp, kt, qb):
                nonlocal av_idx
                slot = (av_idx // 4) % 4
                for hh in range(2):
                    kind, cnt = pt_prereq[av_idx]
                    t.wait_ge((tad if kind == "tad" else expd).sem, cnt)
                    ptslot = av_idx % 8
                    avd.inc(t.matmul(
                        ps_avs[qb * 2 + hh][0:65, 0:512],
                        vfl[:, slot * 130 + hh * 65: slot * 130 + (hh + 1) * 65],
                        ptb[:, ptslot * 512:(ptslot + 1) * 512],
                        start=(kt == 0), stop=(kt == NKT - 1)))
                    av_idx += 1

            for pair in range(8):
                for qbp in range(2):
                    if oev_cnt > 0:
                        t.wait_ge(oev.sem, oev_cnt)
                    prev_step = None
                    for (kt, qb) in steps:
                        emit_st(pair, qbp, kt, qb, g + (0 if qb == 0 else 0))
                        if qb == 1:
                            g += 1
                        if prev_step is not None:
                            emit_av(pair, qbp, *prev_step)
                        prev_step = (kt, qb)
                    emit_av(pair, qbp, *prev_step)
                    for qb in range(2):
                        for hh in range(2):
                            idx = qb * 2 + hh
                            t.wait_ge(oev.sem, oev_cnt + idx + 1)
                            for j in range(4):
                                jj = (idx * 4 + j)
                                if jj >= 2:
                                    t.wait_ge(osc.sem, osc_cnt + jj - 1)
                                trd.inc(t.transpose(
                                    ps_trs[jj % 2][:, 0:65],
                                    oTs[0:65, (idx % 2) * 512 + j * 128:
                                        (idx % 2) * 512 + (j + 1) * 128],
                                    ident[0:65, 0:65]))
                    oev_cnt += 4
                    osc_cnt += 16

        @block.scalar
        def _(s):
            pt_idx = 0
            for pair in range(8):
                for qbp in range(2):
                    for (kt, qb) in steps:
                        for hh in range(2):
                            s.wait_ge(std.sem, pt_idx + 1)
                            if pt_idx >= 8:
                                s.wait_ge(avd.sem, pt_idx - 7)
                            slot = pt_idx % 8
                            expd.inc(s.activation(
                                out=ptb[:, slot * 512:(slot + 1) * 512],
                                in_=ps_sts[hh][:, 0:512],
                                func=AF.Exp, scale=0.125))
                            pt_idx += 1

        @block.vector
        def _(v):
            pt_idx = 0
            tidx = 0
            trd_cnt = 0
            for pi in range(16):
                pair, qbp = divmod(pi, 2)
                for (kt, qb) in steps:
                    for hh in range(2):
                        if tailp(qbp, qb, kt):
                            ts = tidx % 2
                            v.wait_ge(expd.sem, pt_idx + 1)
                            v.wait_ge(tld[ts].sem, 16 * (tidx // 2 + 1))
                            slot = pt_idx % 8
                            rev = bass.AP(tensor=ttl[:].tensor,
                                          offset=ts * 512 + 511,
                                          ap=[[2 * 512, 128], [-1, 512]])
                            tad.inc(v.tensor_tensor(
                                ptb[:, slot * 512:(slot + 1) * 512],
                                ptb[:, slot * 512:(slot + 1) * 512],
                                rev, ALU.mult))
                            tidx += 1
                        pt_idx += 1
                av_total = (pi + 1) * 2 * NS
                for qb in range(2):
                    for hh in range(2):
                        idx = qb * 2 + hh
                        v.wait_ge(avd.sem, av_total)
                        oev.inc(v.tensor_copy(
                            oTs[0:65, (idx % 2) * 512:(idx % 2 + 1) * 512],
                            ps_avs[idx][0:65, 0:512]))
                        for j in range(4):
                            trd_cnt += 1
                            v.wait_ge(trd.sem, trd_cnt)
                            ps = ps_trs[(idx * 4 + j) % 2]
                            rc_ = osc.n % 16
                            if osc.n >= 16:
                                v.wait_ge(osc.sem, osc.n - 15)
                            rec.inc(v.reciprocal(rs[:, rc_:rc_ + 1], ps[:, 64:65]))
                            v.wait_ge(rec.sem, rec.n)
                            ti = qbp * 8 + qb * 4 + j
                            hcol = (pair * 2 + hh) * 64
                            osc.inc(v.tensor_scalar_mul(
                                osb[:, ti * D + hcol: ti * D + hcol + 64],
                                ps[:, 0:64], rs[:, rc_:rc_ + 1]))

        @block.sync
        def _(sync):
            sync.wait_ge(osc.sem, osc.n)
            for i in range(NT):
                sl = i % 2
                if i >= 2:
                    sync.wait_ge(lns.sem, (i - 2) * 3 + 3)
                    st.inc(sync.dma_start(
                        out=x1_o[(i - 2) * 128:(i - 1) * 128, :],
                        in_=yl[:, sl * D:(sl + 1) * D]), 16)
                    sync.wait_ge(lnv.sem, (i - 2) * 11 + 11)
                xld[sl].inc(sync.dma_start(
                    out=xl[:, sl * D:(sl + 1) * D],
                    in_=x[i * 128:(i + 1) * 128, :]), 16)
            for i in (NT - 2, NT - 1):
                sync.wait_ge(lns.sem, i * 3 + 3)
                st.inc(sync.dma_start(
                    out=x1_o[i * 128:(i + 1) * 128, :],
                    in_=yl[:, (i % 2) * D:(i % 2 + 1) * D]), 16)
            store_x_tokmaj(st, sync, dbg_o, osb)
            st.wait(sync)

        @block.vector
        def _(v):
            def ch(inst):
                lnv.inc(inst)
                v.wait_ge(lnv.sem, lnv.n)
                return inst
            for i in range(NT):
                sl = i % 2
                v.wait_ge(xld[sl].sem, 16 * (i // 2 + 1))
                if i >= 2:
                    v.wait_ge(st.sem, 16 * (i - 1))
                ch(v.tensor_add(yl[:, sl * D:(sl + 1) * D],
                                xl[:, sl * D:(sl + 1) * D],
                                osb[:, i * D:(i + 1) * D]))
                ch(v.tensor_reduce(mst[:, 0:1], yl[:, sl * D:(sl + 1) * D],
                                   axis=AX.X, op=ALU.add))
                ch(v.memset(mst[:, 2:3], 0.0))
                ch(v.tensor_scalar_mul(mst[:, 1:2], mst[:, 0:1],
                                       1.0 / D))
                v.wait_ge(lns.sem, i * 3 + 1)
                ch(v.tensor_scalar_mul(mst[:, 3:4], mst[:, 2:3], 1.0 / D))
                ch(v.tensor_tensor(mst[:, 4:5], mst[:, 1:2], mst[:, 1:2],
                                   ALU.mult))
                ch(v.tensor_sub(mst[:, 4:5], mst[:, 3:4], mst[:, 4:5]))
                ch(v.tensor_scalar_add(mst[:, 4:5], mst[:, 4:5], 1e-5))
                v.wait_ge(lns.sem, i * 3 + 2)
                ch(v.reciprocal(mst[:, 6:7], mst[:, 5:6]))
                ch(v.tensor_tensor(mst[:, 7:8], mst[:, 1:2], mst[:, 6:7],
                                   ALU.mult))
                ch(v.tensor_scalar_mul(mst[:, 7:8], mst[:, 7:8], -1.0))

        @block.scalar
        def _(s):
            for i in range(NT):
                sl = i % 2
                s.wait_ge(lnv.sem, i * 11 + 4)
                lns.inc(s.activation(out=sqs[:], in_=yl[:, sl * D:(sl + 1) * D],
                                     func=AF.Square, accum_out=mst[:, 2:3]))
                s.wait_ge(lnv.sem, i * 11 + 8)
                lns.inc(s.activation(out=mst[:, 5:6], in_=mst[:, 4:5],
                                     func=AF.Sqrt))
                s.wait_ge(lnv.sem, i * 11 + 11)
                lns.inc(s.activation(out=yl[:, sl * D:(sl + 1) * D],
                                     in_=yl[:, sl * D:(sl + 1) * D],
                                     func=AF.Identity,
                                     bias=mst[:, 7:8], scale=mst[:, 6:7]))

    return nc



def build_LC(gwb, swb):
    nc = bass.Bass("TRN2", target_bir_lowering=False,
                   detect_race_conditions=False, dynamic_dma_scratch_size=4096)
    x1h = nc.dram_tensor("x1h", [TOK, D], F32, kind="ExternalInput").ap()
    plh = nc.dram_tensor("plh", [1, D], F32, kind="ExternalInput").ap()
    wrh = nc.dram_tensor("wrh", [128, 8 * D], F32, kind="ExternalInput").ap()
    wkh = nc.dram_tensor("wkh", [128, 8 * D], F32, kind="ExternalInput").ap()
    wvh = nc.dram_tensor("wvh", [128, 8 * D], F32, kind="ExternalInput").ap()
    murh = nc.dram_tensor("murh", [128, D], F32, kind="ExternalInput").ap()
    mukh = nc.dram_tensor("mukh", [128, D], F32, kind="ExternalInput").ap()
    q1wh = nc.dram_tensor("q1wh", [128, 8 * D], F32, kind="ExternalInput").ap()
    k1wh = nc.dram_tensor("k1wh", [128, 8 * D], F32, kind="ExternalInput").ap()
    k2wh = nc.dram_tensor("k2wh", [128, 8 * D], F32, kind="ExternalInput").ap()
    q1bh = nc.dram_tensor("q1bh", [1, D], F32, kind="ExternalInput").ap()
    k1bh = nc.dram_tensor("k1bh", [1, D], F32, kind="ExternalInput").ap()
    k2bh = nc.dram_tensor("k2bh", [1, D], F32, kind="ExternalInput").ap()
    wgbdh = nc.dram_tensor("wgbdh", [128, 8 * 16], F32, kind="ExternalInput").ap()
    wsbdh = nc.dram_tensor("wsbdh", [128, 8 * 16], F32, kind="ExternalInput").ap()
    onesh = nc.dram_tensor("onesh", [1, 128], F32, kind="ExternalInput").ap()
    identh = nc.dram_tensor("identh", [128, 128], F32, kind="ExternalInput").ap()
    gswbh = nc.dram_tensor("gswbh", [128, 2], F32, kind="ExternalInput").ap()
    x2_o = nc.dram_tensor("x2_o", [TOK, D], F32, kind="ExternalOutput").ap()
    q1_o = nc.dram_tensor("q1_o", [TOK, D], BF16, kind="ExternalOutput").ap()
    k1_o = nc.dram_tensor("k1_o", [TOK, D], BF16, kind="ExternalOutput").ap()
    k2_o = nc.dram_tensor("k2_o", [TOK, D], BF16, kind="ExternalOutput").ap()
    rs_o = nc.dram_tensor("rs_o", [TOK, 16], F32, kind="ExternalOutput").ap()
    lgt_o = nc.dram_tensor("lgt_o", [16, TOK], F32, kind="ExternalOutput").ap()
    gsum_o = nc.dram_tensor("gsum_o", [16, 1], F32, kind="ExternalOutput").ap()

    with ExitStack() as ctx:
        sb = lambda name, shape, dt: ctx.enter_context(nc.sbuf_tensor(name, shape, dt))
        pt = lambda name, shape: ctx.enter_context(nc.psum_tensor(name, shape, F32))
        wr = sb("wr", [128, 8 * D], BF16)
        wk = sb("wk", [128, 8 * D], BF16)
        wv = sb("wv", [128, 8 * D], BF16)
        q1w = sb("q1w", [128, 8 * D], BF16)
        k1w = sb("k1w", [128, 8 * D], BF16)
        k2w = sb("k2w", [128, 8 * D], BF16)
        wgbd = sb("wgbd", [128, 8 * 16], BF16)
        wsbd = sb("wsbd", [128, 8 * 16], BF16)
        mur = sb("mur", [128, D], F32)
        muk = sb("muk", [128, D], F32)
        q1b = sb("q1b", [1, D], BF16)
        k1br = sb("k1br", [1, D], BF16)
        k2br = sb("k2br", [1, D], BF16)
        ones1 = sb("ones1", [1, 128], BF16)
        ident = sb("ident_s", [128, 128], F32)
        gswb = sb("gswb", [128, 2], F32)
        xs_ = sb("xs_", [128, 2 * D], F32)
        xp_ = sb("xp_", [128, 2 * D], F32)
        db_ = sb("db_", [128, 2 * D], F32)
        tp_ = sb("tp_", [128, 2 * D], F32)
        xrb = sb("xrb", [128, 2 * D], BF16)
        xkb = sb("xkb", [128, 2 * D], BF16)
        kab = sb("kab", [128, 2 * D], BF16)
        rf_ = sb("rf_", [128, 2 * D], F32)
        vo_ = sb("vo_", [128, 2 * D], F32)
        yl_ = sb("yl_", [128, 2 * D], F32)
        x2b = sb("x2b", [128, 2 * D], BF16)
        q1s = sb("q1s", [128, 2 * D], BF16)
        k1s = sb("k1s", [128, 2 * D], BF16)
        k2s = sb("k2s", [128, 2 * D], BF16)
        gf_ = sb("gf_", [128, 2 * 16], F32)
        lgf = sb("lgf", [128, 2 * 16], F32)
        rsl = sb("rsl", [128, 2 * 16], F32)
        xrT = sb("xrT", [128, 2 * D], BF16)
        xkT = sb("xkT", [128, 2 * D], BF16)
        kaT = sb("kaT", [128, 2 * D], BF16)
        x2T = sb("x2T", [128, 2 * D], BF16)
        k1T = sb("k1T", [128, 2 * D], BF16)
        mst = sb("mst", [128, 8], F32)
        lgS = sb("lgS", [16, 2 * 128], F32)
        gacc = sb("gacc", [16, 2], F32)
        psA = pt("psA", [128, 512])
        psB = pt("psB", [128, 512])
        psC = pt("psC", [128, 512])
        psD = pt("psD", [128, 512])
        psG = pt("psG", [128, 16])
        psR = pt("psR", [128, 16])
        psT = pt("psT", [16, 2 * 128])
        block = ctx.enter_context(nc.Block())

        ld = Ctr(nc, "ld")
        ldg = Ctr(nc, "ldg")
        xin = [Ctr(nc, "xin0"), Ctr(nc, "xin1")]
        txr = [Ctr(nc, "txr0"), Ctr(nc, "txr1")]
        txk = [Ctr(nc, "txk0"), Ctr(nc, "txk1")]
        tka = [Ctr(nc, "tka0"), Ctr(nc, "tka1")]
        tx2 = [Ctr(nc, "tx20"), Ctr(nc, "tx21")]
        tk1 = [Ctr(nc, "tk10"), Ctr(nc, "tk11")]
        lct = Ctr(nc, "lct")
        lcp = Ctr(nc, "lcp")
        stl = Ctr(nc, "stl")
        lcs = Ctr(nc, "lcs")
        lcv = Ctr(nc, "lcv")
        st = Ctr(nc, "st")

        NVT = 28
        NSC = 11
        NTT = 15
        VD, VMR, VXR, VMK, VXK = 1, 2, 3, 4, 5
        VKR0, VKR1, VVO0, VVO1, VY = 6, 7, 8, 9, 10
        VX2B = 21
        VQ10, VQ11, VK10, VK11 = 22, 23, 24, 25
        VGCL = 26
        SR0, SR1, SKA = 1, 2, 3
        SSQ, SSQRT, SFIN = 4, 5, 6
        SK20, SK21, SG, SLN, SRS = 7, 8, 9, 10, 11
        TR0, TR1, TK0, TK1 = 1, 2, 3, 4
        TV0, TV1, TQ0, TQ1 = 5, 6, 7, 8
        TKA1, TKB1, TK2A, TK2B = 9, 10, 11, 12
        TG, TRS, TLG = 13, 14, 15

        XIN_PER = 3 * 16

        @block.sync
        def _(sync):
            ld.inc(sync.dma_start(out=mur[:], in_=murh[:, :]), 16)
            ld.inc(sync.dma_start(out=muk[:], in_=mukh[:, :]), 16)
            ld.inc(sync.dma_start(out=ident[:], in_=identh[:, :]), 16)
            ld.inc(sync.dma_start(out=gswb[:], in_=gswbh[:, :]), 16)

            def stores(i):
                s_ = i % 2
                sync.wait_ge(lcs.sem, i * NSC + SFIN)
                st.inc(sync.dma_start(out=x2_o[i * 128:(i + 1) * 128, :],
                                      in_=yl_[:, s_ * D:(s_ + 1) * D]), 16)
                sync.wait_ge(lcv.sem, 1 + i * NVT + 23)
                st.inc(sync.dma_start(out=q1_o[i * 128:(i + 1) * 128, :],
                                      in_=q1s[:, s_ * D:(s_ + 1) * D]), 16)
                sync.wait_ge(lcv.sem, 1 + i * NVT + 25)
                st.inc(sync.dma_start(out=k1_o[i * 128:(i + 1) * 128, :],
                                      in_=k1s[:, s_ * D:(s_ + 1) * D]), 16)
                sync.wait_ge(lcs.sem, i * NSC + SK21)
                st.inc(sync.dma_start(out=k2_o[i * 128:(i + 1) * 128, :],
                                      in_=k2s[:, s_ * D:(s_ + 1) * D]), 16)
                sync.wait_ge(lcs.sem, i * NSC + SRS)
                st.inc(sync.dma_start(out=rs_o[i * 128:(i + 1) * 128, :],
                                      in_=rsl[:, s_ * 16:(s_ + 1) * 16]), 16)
                sync.wait_ge(lcp.sem, i + 1)
                stl.inc(sync.dma_start(out=lgt_o[:, i * 128:(i + 1) * 128],
                                       in_=lgS[:, s_ * 128:(s_ + 1) * 128]), 16)

            for i in range(NT):
                s_ = i % 2
                off = lambda k: 1 + i * NVT + k
                offs = lambda k: i * NSC + k
                offt_prev = lambda k: (i - 2) * NTT + k
                if i >= 2:
                    stores(i - 2)
                    sync.wait_ge(lcv.sem, 1 + (i - 2) * NVT + VY)
                xin[s_].inc(sync.dma_start(
                    out=xs_[:, s_ * D:(s_ + 1) * D],
                    in_=x1h[i * 128:(i + 1) * 128, :]), 16)
                if i == 0:
                    xin[s_].inc(sync.dma_start(
                        out=xp_[0:1, s_ * D:(s_ + 1) * D], in_=plh[0:1, :]), 16)
                    xin[s_].inc(sync.dma_start(
                        out=xp_[1:128, s_ * D:(s_ + 1) * D],
                        in_=x1h[0:127, :]), 16)
                else:
                    xin[s_].inc(sync.dma_start(
                        out=xp_[1:128, s_ * D:(s_ + 1) * D],
                        in_=x1h[i * 128:(i + 1) * 128 - 1, :]), 16)
                    xin[s_].inc(sync.dma_start(
                        out=xp_[0:1, s_ * D:(s_ + 1) * D],
                        in_=x1h[i * 128 - 1:i * 128, :]), 16)
                if i >= 2:
                    sync.wait_ge(lct.sem, offt_prev(TR1))
                sync.wait_ge(lcv.sem, off(VXR))
                for kc in range(8):
                    txr[s_].inc(sync.dma_start_transpose(
                        xrT[:, s_ * D + kc * 128: s_ * D + (kc + 1) * 128],
                        xrb[:, s_ * D + kc * 128: s_ * D + (kc + 1) * 128]), 16)
                if i >= 2:
                    sync.wait_ge(lct.sem, offt_prev(TK1))
                sync.wait_ge(lcv.sem, off(VXK))
                for kc in range(8):
                    txk[s_].inc(sync.dma_start_transpose(
                        xkT[:, s_ * D + kc * 128: s_ * D + (kc + 1) * 128],
                        xkb[:, s_ * D + kc * 128: s_ * D + (kc + 1) * 128]), 16)
                if i >= 2:
                    sync.wait_ge(lct.sem, offt_prev(TV1))
                sync.wait_ge(lcs.sem, offs(SKA))
                for kc in range(8):
                    tka[s_].inc(sync.dma_start_transpose(
                        kaT[:, s_ * D + kc * 128: s_ * D + (kc + 1) * 128],
                        kab[:, s_ * D + kc * 128: s_ * D + (kc + 1) * 128]), 16)
                if i >= 2:
                    sync.wait_ge(lct.sem, offt_prev(TG))
                sync.wait_ge(lcv.sem, off(VX2B))
                for kc in range(8):
                    tx2[s_].inc(sync.dma_start_transpose(
                        x2T[:, s_ * D + kc * 128: s_ * D + (kc + 1) * 128],
                        x2b[:, s_ * D + kc * 128: s_ * D + (kc + 1) * 128]), 16)
                if i >= 2:
                    sync.wait_ge(lct.sem, offt_prev(TRS))
                sync.wait_ge(lcv.sem, off(VK11))
                for kc in range(8):
                    tk1[s_].inc(sync.dma_start_transpose(
                        k1T[:, s_ * D + kc * 128: s_ * D + (kc + 1) * 128],
                        k1s[:, s_ * D + kc * 128: s_ * D + (kc + 1) * 128]), 16)
            stores(NT - 2)
            stores(NT - 1)
            sync.wait_ge(lcv.sem, 1 + NT * NVT)
            st.inc(sync.dma_start(out=gsum_o[:, :], in_=gacc[:, 0:1]), 16)
            st.wait(sync)
            stl.wait(sync)

        @block.gpsimd
        def _(gp):
            for dst, src in ((wr, wrh), (wk, wkh), (wv, wvh), (q1w, q1wh),
                             (k1w, k1wh), (k2w, k2wh), (wgbd, wgbdh),
                             (wsbd, wsbdh), (q1b, q1bh), (k1br, k1bh),
                             (k2br, k2bh), (ones1, onesh)):
                ldg.inc(gp.dma_start(out=dst[:], in_=src[:, :]), 16)

        @block.tensor
        def _(t):
            ld.wait(t)
            ldg.wait(t)
            for i in range(NT):
                s_ = i % 2
                lap = i // 2 + 1

                def proj(ps, xTs, w_sb, dc, n=512, bias=None):
                    wd = w_sb.shape[1] // 8
                    last = None
                    for kc in range(8):
                        last = t.matmul(
                            ps[0:128, 0:n],
                            xTs[:, s_ * D + kc * 128: s_ * D + (kc + 1) * 128],
                            w_sb[:, kc * wd + dc * 512: kc * wd + dc * 512 + n],
                            start=(kc == 0),
                            stop=(kc == 7 and bias is None))
                    if bias is not None:
                        last = t.matmul(ps[0:128, 0:n], ones1[:, :],
                                        bias[:, dc * 512: dc * 512 + n],
                                        start=False, stop=True)
                    lct.inc(last)

                t.wait_ge(txr[s_].sem, 128 * lap)
                if i >= 1:
                    t.wait_ge(lcs.sem, (i - 1) * NSC + SR1)
                proj(psA, xrT, wr, 0)
                proj(psB, xrT, wr, 1)
                t.wait_ge(txk[s_].sem, 128 * lap)
                if i >= 1:
                    t.wait_ge(lcv.sem, 1 + (i - 1) * NVT + VKR1)
                proj(psC, xkT, wk, 0)
                proj(psD, xkT, wk, 1)
                t.wait_ge(tka[s_].sem, 128 * lap)
                t.wait_ge(lcs.sem, i * NSC + SR1)
                proj(psA, kaT, wv, 0)
                proj(psB, kaT, wv, 1)
                t.wait_ge(tx2[s_].sem, 128 * lap)
                t.wait_ge(lcv.sem, 1 + i * NVT + VKR1)
                proj(psC, x2T, q1w, 0, bias=q1b)
                proj(psD, x2T, q1w, 1, bias=q1b)
                t.wait_ge(lcv.sem, 1 + i * NVT + VVO1)
                proj(psA, x2T, k1w, 0, bias=k1br)
                proj(psB, x2T, k1w, 1, bias=k1br)
                t.wait_ge(lcv.sem, 1 + i * NVT + VQ11)
                proj(psC, x2T, k2w, 0, bias=k2br)
                proj(psD, x2T, k2w, 1, bias=k2br)
                if i >= 1:
                    t.wait_ge(lcs.sem, (i - 1) * NSC + SG)
                proj(psG, x2T, wgbd, 0, n=16)
                t.wait_ge(tk1[s_].sem, 128 * lap)
                if i >= 1:
                    t.wait_ge(lcs.sem, (i - 1) * NSC + SRS)
                proj(psR, k1T, wsbd, 0, n=16)
                t.wait_ge(lcs.sem, i * NSC + SLN)
                if i >= 2:
                    t.wait_ge(lcp.sem, i - 1)
                lct.inc(t.transpose(psT[0:16, s_ * 128:(s_ + 1) * 128],
                                    lgf[:, s_ * 16:(s_ + 1) * 16],
                                    ident[:, :]))

        @block.scalar
        def _(s):
            for i in range(NT):
                s_ = i % 2
                offv = lambda k: 1 + i * NVT + k
                offt = lambda k: i * NTT + k
                if i >= 2:
                    s.wait_ge(st.sem, 80 * (i - 1))
                s.wait_ge(lct.sem, offt(TR1))
                lcs.inc(s.activation(out=rf_[:, s_ * D: s_ * D + 512],
                                     in_=psA[:, 0:512], func=AF.Sigmoid))
                lcs.inc(s.activation(out=rf_[:, s_ * D + 512: s_ * D + 1024],
                                     in_=psB[:, 0:512], func=AF.Sigmoid))
                s.wait_ge(lcv.sem, offv(VKR1))
                lcs.inc(s.activation(out=kab[:, s_ * D:(s_ + 1) * D],
                                     in_=kab[:, s_ * D:(s_ + 1) * D],
                                     func=AF.Square))
                s.wait_ge(lcv.sem, offv(VY + 3))
                lcs.inc(s.activation(out=vo_[:, s_ * D:(s_ + 1) * D],
                                     in_=yl_[:, s_ * D:(s_ + 1) * D],
                                     func=AF.Square, accum_out=mst[:, 2:3]))
                s.wait_ge(lcv.sem, offv(VY + 7))
                lcs.inc(s.activation(out=mst[:, 5:6], in_=mst[:, 4:5],
                                     func=AF.Sqrt))
                s.wait_ge(lcv.sem, offv(VY + 10))
                lcs.inc(s.activation(out=yl_[:, s_ * D:(s_ + 1) * D],
                                     in_=yl_[:, s_ * D:(s_ + 1) * D],
                                     func=AF.Identity,
                                     bias=mst[:, 7:8], scale=mst[:, 6:7]))
                s.wait_ge(lct.sem, offt(TK2B))
                lcs.inc(s.activation(out=k2s[:, s_ * D: s_ * D + 512],
                                     in_=psC[:, 0:512], func=AF.Sigmoid,
                                     scale=0.000625))
                lcs.inc(s.activation(out=k2s[:, s_ * D + 512: s_ * D + 1024],
                                     in_=psD[:, 0:512], func=AF.Sigmoid,
                                     scale=0.000625))
                s.wait_ge(lct.sem, offt(TG))
                lcs.inc(s.activation(out=gf_[:, s_ * 16:(s_ + 1) * 16],
                                     in_=psG[:, 0:16], func=AF.Sigmoid,
                                     bias=gswb[:, 0:1]))
                s.wait_ge(lcv.sem, offv(VGCL))
                if i >= 2:
                    s.wait_ge(lct.sem, (i - 2) * NTT + TLG)
                lcs.inc(s.activation(out=lgf[:, s_ * 16:(s_ + 1) * 16],
                                     in_=gf_[:, s_ * 16:(s_ + 1) * 16],
                                     func=AF.Ln))
                s.wait_ge(lct.sem, offt(TRS))
                lcs.inc(s.activation(out=rsl[:, s_ * 16:(s_ + 1) * 16],
                                     in_=psR[:, 0:16], func=AF.Silu,
                                     bias=gswb[:, 1:2]))
                s.wait_ge(lct.sem, offt(TLG))
                if i >= 2:
                    s.wait_ge(stl.sem, 16 * (i - 1))
                lcp.inc(s.copy(lgS[:, s_ * 128:(s_ + 1) * 128],
                               psT[0:16, s_ * 128:(s_ + 1) * 128]))

        @block.vector
        def _(v):
            def ch(inst):
                lcv.inc(inst)
                v.wait_ge(lcv.sem, lcv.n)
                return inst
            ch(v.memset(gacc[:], 0.0))
            for i in range(NT):
                s_ = i % 2
                offs = lambda k: i * NSC + k
                offt = lambda k: i * NTT + k
                sl = slice(s_ * D, (s_ + 1) * D)
                sl16 = slice(s_ * 16, (s_ + 1) * 16)
                v.wait_ge(xin[s_].sem, XIN_PER // 16 * 16 * (i // 2 + 1))
                if i >= 2:
                    v.wait_ge(st.sem, 80 * (i - 1))
                ch(v.tensor_sub(db_[:, sl], xs_[:, sl], xp_[:, sl]))
                ch(v.tensor_mul(tp_[:, sl], mur[:, :], db_[:, sl]))
                ch(v.tensor_add(xrb[:, sl], xp_[:, sl], tp_[:, sl]))
                ch(v.tensor_mul(tp_[:, sl], muk[:, :], db_[:, sl]))
                ch(v.tensor_add(xkb[:, sl], xp_[:, sl], tp_[:, sl]))
                v.wait_ge(lct.sem, offt(TK0))
                ch(v.tensor_scalar_max(kab[:, s_ * D: s_ * D + 512],
                                       psC[:, 0:512], 0.0))
                v.wait_ge(lct.sem, offt(TK1))
                ch(v.tensor_scalar_max(kab[:, s_ * D + 512: s_ * D + 1024],
                                       psD[:, 0:512], 0.0))
                v.wait_ge(lct.sem, offt(TV0))
                v.wait_ge(lcs.sem, offs(SR0))
                ch(v.tensor_mul(vo_[:, s_ * D: s_ * D + 512],
                                rf_[:, s_ * D: s_ * D + 512], psA[:, 0:512]))
                v.wait_ge(lct.sem, offt(TV1))
                v.wait_ge(lcs.sem, offs(SR1))
                ch(v.tensor_mul(vo_[:, s_ * D + 512: s_ * D + 1024],
                                rf_[:, s_ * D + 512: s_ * D + 1024],
                                psB[:, 0:512]))
                ch(v.tensor_add(yl_[:, sl], xs_[:, sl], vo_[:, sl]))
                ch(v.tensor_reduce(mst[:, 0:1], yl_[:, sl], axis=AX.X,
                                   op=ALU.add))
                ch(v.memset(mst[:, 2:3], 0.0))
                ch(v.tensor_scalar_mul(mst[:, 1:2], mst[:, 0:1], 1.0 / D))
                v.wait_ge(lcs.sem, offs(SSQ))
                ch(v.tensor_scalar_mul(mst[:, 3:4], mst[:, 2:3], 1.0 / D))
                ch(v.tensor_tensor(mst[:, 4:5], mst[:, 1:2], mst[:, 1:2],
                                   ALU.mult))
                ch(v.tensor_sub(mst[:, 4:5], mst[:, 3:4], mst[:, 4:5]))
                ch(v.tensor_scalar_add(mst[:, 4:5], mst[:, 4:5], 1e-5))
                v.wait_ge(lcs.sem, offs(SSQRT))
                ch(v.reciprocal(mst[:, 6:7], mst[:, 5:6]))
                ch(v.tensor_tensor(mst[:, 7:8], mst[:, 1:2], mst[:, 6:7],
                                   ALU.mult))
                ch(v.tensor_scalar_mul(mst[:, 7:8], mst[:, 7:8], -1.0))
                v.wait_ge(lcs.sem, offs(SFIN))
                ch(v.tensor_copy(x2b[:, sl], yl_[:, sl]))
                v.wait_ge(lct.sem, offt(TQ0))
                ch(v.tensor_copy(q1s[:, s_ * D: s_ * D + 512], psC[:, 0:512]))
                v.wait_ge(lct.sem, offt(TQ1))
                ch(v.tensor_copy(q1s[:, s_ * D + 512: s_ * D + 1024],
                                 psD[:, 0:512]))
                v.wait_ge(lct.sem, offt(TKB1))
                ch(v.tensor_copy(k1s[:, s_ * D: s_ * D + 512], psA[:, 0:512]))
                ch(v.tensor_copy(k1s[:, s_ * D + 512: s_ * D + 1024],
                                 psB[:, 0:512]))
                v.wait_ge(lcs.sem, offs(SG))
                ch(v.tensor_scalar_max(gf_[:, sl16], gf_[:, sl16], 1e-6))
                v.wait_ge(lcp.sem, i + 1)
                ch(v.tensor_reduce(gacc[:, 1:2], lgS[:, s_ * 128:(s_ + 1) * 128],
                                   axis=AX.X, op=ALU.add))
                ch(v.tensor_add(gacc[:, 0:1], gacc[:, 0:1], gacc[:, 1:2]))

        @block.scalar
        def _(s):
            for i in range(NT):
                s_ = i % 2
                s.wait_ge(lct.sem, i * NTT + TLG)
                if i >= 2:
                    s.wait_ge(stl.sem, 16 * (i - 1))
                lcp.inc(s.copy(lgS[:, s_ * 128:(s_ + 1) * 128],
                               psT[0:16, s_ * 128:(s_ + 1) * 128]))

    return nc

def build_LD():
    nc = bass.Bass("TRN2", target_bir_lowering=False,
                   detect_race_conditions=False, dynamic_dma_scratch_size=4096)
    x2h = nc.dram_tensor("x2h", [TOK, D], F32, kind="ExternalInput").ap()
    q1h = nc.dram_tensor("q1h", [TOK, D], BF16, kind="ExternalInput").ap()
    k1h = nc.dram_tensor("k1h", [TOK, D], BF16, kind="ExternalInput").ap()
    lgth = nc.dram_tensor("lgth", [16, TOK], F32, kind="ExternalInput").ap()
    rsh = nc.dram_tensor("rsh", [TOK, 16], F32, kind="ExternalInput").ap()
    gsh = nc.dram_tensor("gsh", [16, 8], F32, kind="ExternalInput").ap()
    pmh = nc.dram_tensor("pmh", [16, 8], F32, kind="ExternalInput").ap()
    maskh = nc.dram_tensor("maskh", [128, 128], F32, kind="ExternalInput").ap()
    identh = nc.dram_tensor("identh", [128, 128], F32, kind="ExternalInput").ap()
    o1_o = nc.dram_tensor("o1_o", [TOK, D], F32, kind="ExternalOutput").ap()
    sa_o = nc.dram_tensor("sa_o", [NT, 128, 512], F32, kind="ExternalOutput").ap()
    gc_o = nc.dram_tensor("gc_o", [TOK, 16], F32, kind="ExternalOutput").ap()

    with ExitStack() as ctx:
        sb = lambda n_, sh, dt: ctx.enter_context(nc.sbuf_tensor(n_, sh, dt))
        pt = lambda n_, sh: ctx.enter_context(nc.psum_tensor(n_, sh, F32))
        lgt = sb("lgt", [16, TOK], F32)
        cums = sb("cums", [16, TOK], F32)
        gch = sb("gch", [16, TOK], F32)
        gs8 = sb("gs8", [16, 8], F32)
        pm8 = sb("pm8", [16, 8], F32)
        cumin = sb("cumin", [16, 2], F32)
        gct = sb("gct", [128, NT * 16], F32)
        wt_ = sb("wt_", [128, NT * 16], F32)
        rs_ = sb("rs_", [128, NT * 16], F32)
        mask = sb("mask", [128, 128], F32)
        ident = sb("ident_s", [128, 128], F32)
        x2f = sb("x2f", [128, 2 * D], F32)
        x2b = sb("x2b", [128, 2 * D], BF16)
        q1t = sb("q1t", [128, 2 * D], BF16)
        k1t = sb("k1t", [128, 2 * D], BF16)
        qpb = sb("qpb", [128, 2 * D], BF16)
        kpb = sb("kpb", [128, 2 * D], BF16)
        qpT = sb("qpT", [128, 2 * D], BF16)
        kpT = sb("kpT", [128, 2 * D], BF16)
        a1m = sb("a1m", [128, 2 * 256], BF16)
        o1t = sb("o1t", [128, 2 * D], F32)
        ssn = sb("ssn", [128, 2 * 512], F32)
        psS1a = pt("psS1a", [128, 128])
        psS1b = pt("psS1b", [128, 128])
        psS1c = pt("psS1c", [128, 128])
        psS1d = pt("psS1d", [128, 128])
        psOa = pt("psOa", [128, 64])
        psOb = pt("psOb", [128, 64])
        psS = pt("psS", [128, 512])
        psGc = pt("psGc", [128, 16])
        block = ctx.enter_context(nc.Block())

        ld = Ctr(nc, "ld")
        xin = [Ctr(nc, "xin0"), Ctr(nc, "xin1")]
        ttr = [Ctr(nc, "ttr0"), Ctr(nc, "ttr1")]
        ldv = Ctr(nc, "ldv")
        lds = Ctr(nc, "lds")
        ldt = Ctr(nc, "ldt")
        st = Ctr(nc, "st")

        NVA = 4 + 3 * NT
        NVB = 66
        NTA = NT
        NTB = 48

        @block.sync
        def _(sync):
            ld.inc(sync.dma_start(out=lgt[:], in_=lgth[:, :]), 16)
            ld.inc(sync.dma_start(out=gs8[:], in_=gsh[:, :]), 16)
            ld.inc(sync.dma_start(out=pm8[:], in_=pmh[:, :]), 16)
            ld.inc(sync.dma_start(out=mask[:], in_=maskh[:, :]), 16)
            ld.inc(sync.dma_start(out=ident[:], in_=identh[:, :]), 16)
            for i in range(NT):
                ld.inc(sync.dma_start(out=rs_[:, i * 16:(i + 1) * 16],
                                      in_=rsh[i * 128:(i + 1) * 128, :]), 16)

        @block.vector
        def _(v):
            def ch(inst):
                ldv.inc(inst)
                v.wait_ge(ldv.sem, ldv.n)
                return inst
            ld.wait(v)
            ch(v.tensor_mul(cumin[:, 0:1] if False else gs8[:], gs8[:], pm8[:]))
            ch(v.tensor_reduce(cumin[:, 0:1], gs8[:], axis=AX.X, op=ALU.add))
            ch(v.tensor_tensor_scan(cums[:], lgt[:], lgt[:], cumin[:, 0:1],
                                    op0=ALU.add, op1=ALU.bypass))
            ch(v.tensor_scalar_max(cums[:], cums[:], -30.0))
            v.wait_ge(lds.sem, 1)
            ch(v.tensor_scalar_add(gch[:], gch[:], 1e-6))
            for i in range(NT):
                v.wait_ge(ldt.sem, i + 1)
                ch(v.tensor_copy(gct[:, i * 16:(i + 1) * 16], psGc[:, 0:16]))
                ch(v.reciprocal(wt_[:, i * 16:(i + 1) * 16],
                                gct[:, i * 16:(i + 1) * 16]))
                ch(v.tensor_mul(wt_[:, i * 16:(i + 1) * 16],
                                wt_[:, i * 16:(i + 1) * 16],
                                rs_[:, i * 16:(i + 1) * 16]))
            ch(v.memset(psS[:, 0:512], 0.0))
            for i in range(NT):
                s_ = i % 2
                v.wait_ge(xin[s_].sem, 48 * (i // 2 + 1))
                if i >= 2:
                    v.wait_ge(st.sem, 32 * (i - 1))
                ch(v.tensor_copy(x2b[:, s_ * D:(s_ + 1) * D],
                                 x2f[:, s_ * D:(s_ + 1) * D]))
                for h in range(16):
                    ch(v.tensor_scalar_mul(
                        qpb[:, s_ * D + h * 64: s_ * D + (h + 1) * 64],
                        q1t[:, s_ * D + h * 64: s_ * D + (h + 1) * 64],
                        gct[:, i * 16 + h: i * 16 + h + 1]))
                    ch(v.tensor_scalar_mul(
                        kpb[:, s_ * D + h * 64: s_ * D + (h + 1) * 64],
                        k1t[:, s_ * D + h * 64: s_ * D + (h + 1) * 64],
                        wt_[:, i * 16 + h: i * 16 + h + 1]))
                base_t = NTA + i * NTB

                def evac_pair(hp2):
                    v.wait_ge(ldt.sem, base_t + hp2 * 6 + 4)
                    for hh in range(2):
                        h = hp2 * 2 + hh
                        ps = psOa if hh == 0 else psOb
                        ch(v.tensor_copy(
                            o1t[:, s_ * D + h * 64: s_ * D + (h + 1) * 64],
                            ps[:, 0:64]))
                for hp in range(8):
                    sl_ = hp % 2
                    banks = [psS1a, psS1b] if sl_ == 0 else [psS1c, psS1d]
                    v.wait_ge(ldt.sem, base_t + hp * 6 + 2)
                    for hh in range(2):
                        ch(v.tensor_mul(
                            a1m[:, sl_ * 256 + hh * 128: sl_ * 256 + (hh + 1) * 128],
                            banks[hh][:, 0:128], mask[:, :]))
                    if hp >= 1:
                        evac_pair(hp - 1)
                evac_pair(7)
                v.wait_ge(ldt.sem, base_t + NTB)
                ch(v.tensor_copy(ssn[:, s_ * 512:(s_ + 1) * 512],
                                 psS[:, 0:512]))

        @block.scalar
        def _(s):
            s.wait_ge(ldv.sem, 4)
            lds.inc(s.activation(out=gch[:], in_=cums[:], func=AF.Exp))

        @block.tensor
        def _(t):
            ld.wait(t)
            t.wait_ge(ldv.sem, 5)
            for i in range(NT):
                if i >= 1:
                    t.wait_ge(ldv.sem, 5 + 3 * i)
                ldt.inc(t.transpose(psGc[:, 0:16],
                                    gch[0:16, i * 128:(i + 1) * 128],
                                    ident[0:16, 0:16]))
            A_ = [35, 37, 41, 45, 49, 53, 57, 61]
            E_ = [39, 43, 47, 51, 55, 59, 63, 65]
            t.wait_ge(ldv.sem, 6 + 3 * NT)
            for i in range(NT):
                s_ = i % 2
                base_v = 6 + 3 * NT + i * NVB
                for hp in range(8):
                    sl_ = hp % 2
                    banks = [psS1a, psS1b] if sl_ == 0 else [psS1c, psS1d]
                    t.wait_ge(ttr[s_].sem, 256 * (i // 2 + 1))
                    if hp >= 2:
                        t.wait_ge(ldv.sem, base_v + A_[hp - 2])
                    elif i >= 1:
                        prev_v = 6 + 3 * NT + (i - 1) * NVB
                        t.wait_ge(ldv.sem, prev_v + A_[hp + 6])
                    for hh in range(2):
                        ldt.inc(t.matmul(
                            banks[hh][:, 0:128],
                            kpT[(hh) * 64:(hh + 1) * 64,
                                s_ * D + hp * 128: s_ * D + hp * 128 + 128],
                            qpT[(hh) * 64:(hh + 1) * 64,
                                s_ * D + hp * 128: s_ * D + hp * 128 + 128],
                            start=True, stop=True,
                            tile_position=(hh * 64, 0)))
                    for hh in range(2):
                        h = hp * 2 + hh
                        ps = psOa if hh == 0 else psOb
                        t.wait_ge(ldv.sem, base_v + A_[hp])
                        if hp >= 1:
                            t.wait_ge(ldv.sem, base_v + E_[hp - 1])
                        elif i >= 1:
                            prev_v = 6 + 3 * NT + (i - 1) * NVB
                            t.wait_ge(ldv.sem, prev_v + E_[7])
                        ldt.inc(t.matmul(
                            ps[:, 0:64],
                            a1m[:, sl_ * 256 + hh * 128: sl_ * 256 + (hh + 1) * 128],
                            x2b[:, s_ * D + h * 64: s_ * D + (h + 1) * 64],
                            start=True, stop=True))
                    for hh in range(2):
                        h = hp * 2 + hh
                        if i == 0 and hp == 0 and hh == 0:
                            pass
                        if i >= 2 and hp == 0 and hh == 0:
                            prev_v = 6 + 3 * NT + (i - 1) * NVB
                            t.wait_ge(ldv.sem, prev_v + NVB)
                        ldt.inc(t.matmul(
                            psS[(h % 2) * 64:(h % 2) * 64 + 64,
                                (h // 2) * 64:(h // 2) * 64 + 64],
                            kpb[:, s_ * D + h * 64: s_ * D + (h + 1) * 64],
                            x2b[:, s_ * D + h * 64: s_ * D + (h + 1) * 64],
                            start=False,
                            stop=(i == NT - 1 and h == 15),
                            skip_group_check=True,
                            tile_position=(0, (h % 2) * 64)))

        @block.sync
        def _(sync):
            for i in range(NT):
                s_ = i % 2
                base_v = 6 + 3 * NT + i * NVB
                if i >= 2:
                    prev_v = 6 + 3 * NT + (i - 2) * NVB
                    sync.wait_ge(ldv.sem, prev_v + 65)
                    st.inc(sync.dma_start(
                        out=o1_o[(i - 2) * 128:(i - 1) * 128, :],
                        in_=o1t[:, s_ * D:(s_ + 1) * D]), 16)
                    sync.wait_ge(ldv.sem, prev_v + NVB)
                    st.inc(sync.dma_start(
                        out=sa_o[i - 2, :, :],
                        in_=ssn[:, s_ * 512:(s_ + 1) * 512]), 16)
                    sync.wait_ge(ldv.sem, 5 + 3 * (i - 2) + 1)
                if i >= 2:
                    sync.wait_ge(ldv.sem, prev_v + 33)
                xin[s_].inc(sync.dma_start(out=x2f[:, s_ * D:(s_ + 1) * D],
                                           in_=x2h[i * 128:(i + 1) * 128, :]), 16)
                xin[s_].inc(sync.dma_start(out=q1t[:, s_ * D:(s_ + 1) * D],
                                           in_=q1h[i * 128:(i + 1) * 128, :]), 16)
                xin[s_].inc(sync.dma_start(out=k1t[:, s_ * D:(s_ + 1) * D],
                                           in_=k1h[i * 128:(i + 1) * 128, :]), 16)
                if i >= 2:
                    base_t_prev = NTA + (i - 2) * NTB
                    sync.wait_ge(ldt.sem, base_t_prev + NTB)
                sync.wait_ge(ldv.sem, base_v + 33)
                for kc in range(8):
                    ttr[s_].inc(sync.dma_start_transpose(
                        qpT[:, s_ * D + kc * 128: s_ * D + (kc + 1) * 128],
                        qpb[:, s_ * D + kc * 128: s_ * D + (kc + 1) * 128]), 16)
                for kc in range(8):
                    ttr[s_].inc(sync.dma_start_transpose(
                        kpT[:, s_ * D + kc * 128: s_ * D + (kc + 1) * 128],
                        kpb[:, s_ * D + kc * 128: s_ * D + (kc + 1) * 128]), 16)
            for i in (NT - 2, NT - 1):
                s_ = i % 2
                base_v = 6 + 3 * NT + i * NVB
                sync.wait_ge(ldv.sem, base_v + 65)
                st.inc(sync.dma_start(out=o1_o[i * 128:(i + 1) * 128, :],
                                      in_=o1t[:, s_ * D:(s_ + 1) * D]), 16)
                sync.wait_ge(ldv.sem, base_v + NVB)
                st.inc(sync.dma_start(out=sa_o[i, :, :],
                                      in_=ssn[:, s_ * 512:(s_ + 1) * 512]), 16)
            sync.wait_ge(ldv.sem, 6 + 3 * NT)
            for i in range(NT):
                st.inc(sync.dma_start(out=gc_o[i * 128:(i + 1) * 128, :],
                                      in_=gct[:, i * 16:(i + 1) * 16]), 16)
            st.wait(sync)
    return nc

def build_LE():
    nc = bass.Bass("TRN2", target_bir_lowering=False,
                   detect_race_conditions=False, dynamic_dma_scratch_size=4096)
    x2h = nc.dram_tensor("x2h", [TOK, D], F32, kind="ExternalInput").ap()
    q1h = nc.dram_tensor("q1h", [TOK, D], BF16, kind="ExternalInput").ap()
    k2h = nc.dram_tensor("k2h", [TOK, D], BF16, kind="ExternalInput").ap()
    gch = nc.dram_tensor("gch", [TOK, 16], F32, kind="ExternalInput").ap()
    o1ih = nc.dram_tensor("o1ih", [TOK, D], F32, kind="ExternalInput").ap()
    sah = nc.dram_tensor("sah", [NT, 128, 512], F32, kind="ExternalInput").ap()
    dsh = nc.dram_tensor("dsh", [128, 8 * 512], F32, kind="ExternalInput").ap()
    pmh = nc.dram_tensor("pmh", [128, 8], F32, kind="ExternalInput").ap()
    x2nh = nc.dram_tensor("x2nh", [1, D], F32, kind="ExternalInput").ap()
    emh = nc.dram_tensor("emh", [128, NT], F32, kind="ExternalInput").ap()
    maskh = nc.dram_tensor("maskh", [128, 128], F32, kind="ExternalInput").ap()
    o1_o = nc.dram_tensor("o1_o", [TOK, D], F32, kind="ExternalOutput").ap()
    o2i_o = nc.dram_tensor("o2i_o", [TOK, D], F32, kind="ExternalOutput").ap()
    ma_o = nc.dram_tensor("ma_o", [NT, 128, 512], F32, kind="ExternalOutput").ap()

    with ExitStack() as ctx:
        sb = lambda n_, sh, dt: ctx.enter_context(nc.sbuf_tensor(n_, sh, dt))
        pt = lambda n_, sh: ctx.enter_context(nc.psum_tensor(n_, sh, F32))
        dsall = sb("dsall", [128, 8 * 512], F32)
        pm8 = sb("pm8", [128, 8], F32)
        sacc = sb("sacc", [128, 512], F32)
        gct = sb("gct", [128, NT * 16], F32)
        mask = sb("mask", [128, 128], F32)
        em = sb("em", [128, NT], F32)
        x2f = sb("x2f", [128, 2 * D], F32)
        x2n = sb("x2n", [128, 2 * D], F32)
        x2b = sb("x2b", [128, 2 * D], BF16)
        q1t = sb("q1t", [128, 2 * D], BF16)
        k2t = sb("k2t", [128, 2 * D], BF16)
        o1i = sb("o1i", [128, 2 * D], F32)
        sprev = sb("sprev", [128, 2 * 512], F32)
        stot = sb("stot", [128, 2 * 512], BF16)
        qpb = sb("qpb", [128, 2 * D], BF16)
        qpT = sb("qpT", [128, 2 * D], BF16)
        q2b = sb("q2b", [128, 2 * D], BF16)
        q2T = sb("q2T", [128, 2 * D], BF16)
        k2T = sb("k2T", [128, 2 * D], BF16)
        tq2 = sb("tq2", [128, D], F32)
        eb_ = sb("eb_", [128, 2 * D], BF16)
        a2m = sb("a2m", [128, 2 * 256], BF16)
        o2i = sb("o2i", [128, 2 * D], F32)
        msn = sb("msn", [128, 2 * 512], F32)
        psS1a = pt("psS1a", [128, 128])
        psS1b = pt("psS1b", [128, 128])
        psS1c = pt("psS1c", [128, 128])
        psS1d = pt("psS1d", [128, 128])
        psOa = pt("psOa", [128, 64])
        psOb = pt("psOb", [128, 64])
        psM = pt("psM", [128, 512])
        block = ctx.enter_context(nc.Block())

        ld = Ctr(nc, "ld")
        xin = [Ctr(nc, "xin0"), Ctr(nc, "xin1")]
        ttr = [Ctr(nc, "ttr0"), Ctr(nc, "ttr1")]
        ldv = Ctr(nc, "ldv")
        ldt = Ctr(nc, "ldt")
        st = Ctr(nc, "st")

        NVA = 10
        NVB = 72
        NTB = 64
        A2 = [41, 43, 47, 51, 55, 59, 63, 67]
        E2 = [45, 49, 53, 57, 61, 65, 69, 71]
        xin_marks = {}
        ttr_marks = {}

        @block.sync
        def _(sync):
            ld.inc(sync.dma_start(out=dsall[:], in_=dsh[:, :]), 16)
            ld.inc(sync.dma_start(out=pm8[:], in_=pmh[:, :]), 16)
            ld.inc(sync.dma_start(out=mask[:], in_=maskh[:, :]), 16)
            ld.inc(sync.dma_start(out=em[:], in_=emh[:, :]), 16)
            for i in range(NT):
                ld.inc(sync.dma_start(out=gct[:, i * 16:(i + 1) * 16],
                                      in_=gch[i * 128:(i + 1) * 128, :]), 16)
            for i in range(NT):
                s_ = i % 2
                base_v = NVA + i * NVB
                if i >= 2:
                    prev_v = NVA + (i - 2) * NVB
                    sync.wait_ge(ldv.sem, prev_v + 34)
                    st.inc(sync.dma_start(
                        out=o1_o[(i - 2) * 128:(i - 1) * 128, :],
                        in_=o1i[:, s_ * D:(s_ + 1) * D]), 16)
                    sync.wait_ge(ldv.sem, prev_v + E2[7])
                    st.inc(sync.dma_start(
                        out=o2i_o[(i - 2) * 128:(i - 1) * 128, :],
                        in_=o2i[:, s_ * D:(s_ + 1) * D]), 16)
                    sync.wait_ge(ldv.sem, prev_v + NVB)
                    st.inc(sync.dma_start(
                        out=ma_o[i - 2, :, :],
                        in_=msn[:, s_ * 512:(s_ + 1) * 512]), 16)
                xin[s_].inc(sync.dma_start(out=x2f[:, s_ * D:(s_ + 1) * D],
                                           in_=x2h[i * 128:(i + 1) * 128, :]), 16)
                xin[s_].inc(sync.dma_start(out=q1t[:, s_ * D:(s_ + 1) * D],
                                           in_=q1h[i * 128:(i + 1) * 128, :]), 16)
                xin[s_].inc(sync.dma_start(out=k2t[:, s_ * D:(s_ + 1) * D],
                                           in_=k2h[i * 128:(i + 1) * 128, :]), 16)
                xin[s_].inc(sync.dma_start(out=o1i[:, s_ * D:(s_ + 1) * D],
                                           in_=o1ih[i * 128:(i + 1) * 128, :]), 16)
                if i < NT - 1:
                    xin[s_].inc(sync.dma_start(
                        out=x2n[:, s_ * D:(s_ + 1) * D],
                        in_=x2h[i * 128 + 1:(i + 1) * 128 + 1, :]), 16)
                else:
                    xin[s_].inc(sync.dma_start(
                        out=x2n[0:127, s_ * D:(s_ + 1) * D],
                        in_=x2h[i * 128 + 1:(i + 1) * 128, :]), 16)
                    xin[s_].inc(sync.dma_start(
                        out=x2n[127:128, s_ * D:(s_ + 1) * D],
                        in_=x2nh[0:1, :]), 16)
                if i >= 1:
                    xin[s_].inc(sync.dma_start(
                        out=sprev[:, s_ * 512:(s_ + 1) * 512],
                        in_=sah[i - 1, :, :]), 16)
                xin_marks[i] = xin[s_].n
                if i >= 2:
                    sync.wait_ge(ldt.sem, (i - 1) * NTB)
                sync.wait_ge(ldv.sem, base_v + 17)
                for kc in range(8):
                    ttr[s_].inc(sync.dma_start_transpose(
                        qpT[:, s_ * D + kc * 128: s_ * D + (kc + 1) * 128],
                        qpb[:, s_ * D + kc * 128: s_ * D + (kc + 1) * 128]), 16)
                ttr_marks[(i, 0)] = ttr[s_].n
                sync.wait_ge(ldv.sem, base_v + 39)
                for kc in range(8):
                    ttr[s_].inc(sync.dma_start_transpose(
                        q2T[:, s_ * D + kc * 128: s_ * D + (kc + 1) * 128],
                        q2b[:, s_ * D + kc * 128: s_ * D + (kc + 1) * 128]), 16)
                for kc in range(8):
                    ttr[s_].inc(sync.dma_start_transpose(
                        k2T[:, s_ * D + kc * 128: s_ * D + (kc + 1) * 128],
                        k2t[:, s_ * D + kc * 128: s_ * D + (kc + 1) * 128]), 16)
                ttr_marks[(i, 1)] = ttr[s_].n
            for i in (NT - 2, NT - 1):
                s_ = i % 2
                base_v = NVA + i * NVB
                sync.wait_ge(ldv.sem, base_v + 34)
                st.inc(sync.dma_start(out=o1_o[i * 128:(i + 1) * 128, :],
                                      in_=o1i[:, s_ * D:(s_ + 1) * D]), 16)
                sync.wait_ge(ldv.sem, base_v + E2[7])
                st.inc(sync.dma_start(out=o2i_o[i * 128:(i + 1) * 128, :],
                                      in_=o2i[:, s_ * D:(s_ + 1) * D]), 16)
                sync.wait_ge(ldv.sem, base_v + NVB)
                st.inc(sync.dma_start(out=ma_o[i, :, :],
                                      in_=msn[:, s_ * 512:(s_ + 1) * 512]), 16)
            st.wait(sync)

        @block.tensor
        def _(t):
            ld.wait(t)
            t.wait_ge(ldv.sem, NVA)
            for i in range(NT):
                s_ = i % 2
                base_v = NVA + i * NVB
                base_t = i * NTB
                t.wait_ge(ttr[s_].sem, 16 * ttr_marks[(i, 0)] // 16)
                t.wait_ge(ldv.sem, base_v + 18)
                for hp in range(8):
                    for hh in range(2):
                        h = hp * 2 + hh
                        ps = psOa if hh == 0 else psOb
                        if h >= 2:
                            t.wait_ge(ldv.sem, base_v + 18 + h - 1)
                        elif i >= 1:
                            t.wait_ge(ldv.sem, NVA + (i - 1) * NVB + 18 + 15)
                        ldt.inc(t.matmul(
                            ps[:, 0:64],
                            qpT[(h % 2) * 64:(h % 2) * 64 + 64,
                                s_ * D + (h // 2) * 128: s_ * D + (h // 2) * 128 + 128],
                            stot[(h % 2) * 64:(h % 2) * 64 + 64,
                                 s_ * 512 + (h // 2) * 64: s_ * 512 + (h // 2) * 64 + 64],
                            start=True, stop=True,
                            tile_position=((h % 2) * 64, 0)))
                t.wait_ge(ttr[s_].sem, 16 * ttr_marks[(i, 1)] // 16)
                for hp in range(8):
                    sl_ = hp % 2
                    banks = [psS1a, psS1b] if sl_ == 0 else [psS1c, psS1d]
                    if hp >= 2:
                        t.wait_ge(ldv.sem, base_v + A2[hp - 2])
                    elif i >= 1:
                        t.wait_ge(ldv.sem, NVA + (i - 1) * NVB + A2[hp + 6])
                    for hh in range(2):
                        ldt.inc(t.matmul(
                            banks[hh][:, 0:128],
                            k2T[hh * 64:(hh + 1) * 64,
                                s_ * D + hp * 128: s_ * D + hp * 128 + 128],
                            q2T[hh * 64:(hh + 1) * 64,
                                s_ * D + hp * 128: s_ * D + hp * 128 + 128],
                            start=True, stop=True,
                            tile_position=(hh * 64, 0)))
                    for hh in range(2):
                        h = hp * 2 + hh
                        ps = psOa if hh == 0 else psOb
                        t.wait_ge(ldv.sem, base_v + A2[hp])
                        if hp >= 1:
                            t.wait_ge(ldv.sem, base_v + E2[hp - 1])
                        elif i >= 1:
                            t.wait_ge(ldv.sem, NVA + (i - 1) * NVB + E2[7])
                        ldt.inc(t.matmul(
                            ps[:, 0:64],
                            a2m[:, sl_ * 256 + hh * 128: sl_ * 256 + (hh + 1) * 128],
                            eb_[:, s_ * D + h * 64: s_ * D + (h + 1) * 64],
                            start=True, stop=True))
                    for hh in range(2):
                        h = hp * 2 + hh
                        if i >= 2 and hp == 0 and hh == 0:
                            t.wait_ge(ldv.sem, NVA + (i - 1) * NVB + NVB)
                        ldt.inc(t.matmul(
                            psM[(h % 2) * 64:(h % 2) * 64 + 64,
                                (h // 2) * 64:(h // 2) * 64 + 64],
                            k2t[:, s_ * D + h * 64: s_ * D + (h + 1) * 64],
                            eb_[:, s_ * D + h * 64: s_ * D + (h + 1) * 64],
                            start=False, stop=(i == NT - 1 and h == 15),
                            skip_group_check=True,
                            tile_position=(0, (h % 2) * 64)))

        @block.vector
        def _(v):
            def ch(inst):
                ldv.inc(inst)
                v.wait_ge(ldv.sem, ldv.n)
                return inst
            ld.wait(v)
            ch(v.memset(sacc[:], 0.0))
            for c in range(8):
                ch(v.scalar_tensor_tensor(sacc[:], dsall[:, c * 512:(c + 1) * 512],
                                          pm8[:, c:c + 1], sacc[:],
                                          op0=ALU.mult, op1=ALU.add))
            ch(v.memset(psM[:, 0:512], 0.0))
            for i in range(NT):
                s_ = i % 2
                base_t = i * NTB
                v.wait_ge(xin[s_].sem, 16 * (xin_marks[i] // 16))
                if i >= 2:
                    v.wait_ge(st.sem, 48 * (i - 1))
                ch(v.tensor_copy(x2b[:, s_ * D:(s_ + 1) * D],
                                 x2f[:, s_ * D:(s_ + 1) * D]))
                for h in range(16):
                    ch(v.tensor_scalar_mul(
                        qpb[:, s_ * D + h * 64: s_ * D + (h + 1) * 64],
                        q1t[:, s_ * D + h * 64: s_ * D + (h + 1) * 64],
                        gct[:, i * 16 + h: i * 16 + h + 1]))
                if i == 0:
                    ch(v.tensor_copy(stot[:, 0:512], sacc[:]))
                else:
                    ch(v.tensor_add(stot[:, s_ * 512:(s_ + 1) * 512],
                                    sacc[:], sprev[:, s_ * 512:(s_ + 1) * 512]))
                for hp in range(8):
                    for hh in range(2):
                        h = hp * 2 + hh
                        ps = psOa if hh == 0 else psOb
                        v.wait_ge(ldt.sem, base_t + h + 1)
                        ch(v.tensor_add(
                            o1i[:, s_ * D + h * 64: s_ * D + (h + 1) * 64],
                            o1i[:, s_ * D + h * 64: s_ * D + (h + 1) * 64],
                            ps[:, 0:64]))
                ch(v.tensor_scalar(tq2[:], q1t[:, s_ * D:(s_ + 1) * D],
                                   0.0, 0.125, op0=ALU.min, op1=ALU.mult))
                ch(v.tensor_scalar(q2b[:, s_ * D:(s_ + 1) * D],
                                   q1t[:, s_ * D:(s_ + 1) * D],
                                   0.0, 0.0025, op0=ALU.max, op1=ALU.mult))
                ch(v.tensor_add(q2b[:, s_ * D:(s_ + 1) * D],
                                q2b[:, s_ * D:(s_ + 1) * D], tq2[:]))
                ch(v.tensor_sub(eb_[:, s_ * D:(s_ + 1) * D],
                                x2n[:, s_ * D:(s_ + 1) * D],
                                o1i[:, s_ * D:(s_ + 1) * D]))
                ch(v.tensor_scalar_mul(eb_[:, s_ * D:(s_ + 1) * D],
                                       eb_[:, s_ * D:(s_ + 1) * D],
                                       em[:, i:i + 1]))
                def evac2(hp2):
                    v.wait_ge(ldt.sem, base_t + 16 + hp2 * 6 + 4)
                    for hh in range(2):
                        h = hp2 * 2 + hh
                        ps = psOa if hh == 0 else psOb
                        ch(v.tensor_copy(
                            o2i[:, s_ * D + h * 64: s_ * D + (h + 1) * 64],
                            ps[:, 0:64]))
                for hp in range(8):
                    sl_ = hp % 2
                    banks = [psS1a, psS1b] if sl_ == 0 else [psS1c, psS1d]
                    v.wait_ge(ldt.sem, base_t + 16 + hp * 6 + 2)
                    for hh in range(2):
                        ch(v.tensor_mul(
                            a2m[:, sl_ * 256 + hh * 128: sl_ * 256 + (hh + 1) * 128],
                            banks[hh][:, 0:128], mask[:, :]))
                    if hp >= 1:
                        evac2(hp - 1)
                evac2(7)
                v.wait_ge(ldt.sem, base_t + NTB)
                ch(v.tensor_copy(msn[:, s_ * 512:(s_ + 1) * 512],
                                 psM[:, 0:512]))
    return nc

def build_LF():
    nc = bass.Bass("TRN2", target_bir_lowering=False,
                   detect_race_conditions=False, dynamic_dma_scratch_size=4096)
    x2h = nc.dram_tensor("x2h", [TOK, D], F32, kind="ExternalInput").ap()
    o1h = nc.dram_tensor("o1h", [TOK, D], F32, kind="ExternalInput").ap()
    o2ih = nc.dram_tensor("o2ih", [TOK, D], F32, kind="ExternalInput").ap()
    mah = nc.dram_tensor("mah", [NT, 128, 512], F32, kind="ExternalInput").ap()
    dmh = nc.dram_tensor("dmh", [128, 8 * 512], F32, kind="ExternalInput").ap()
    pmh = nc.dram_tensor("pmh", [128, 8], F32, kind="ExternalInput").ap()
    q1h = nc.dram_tensor("q1h", [TOK, D], BF16, kind="ExternalInput").ap()
    q1lh = nc.dram_tensor("q1lh", [1, D], BF16, kind="ExternalInput").ap()
    flagh = nc.dram_tensor("flagh", [1, 1], F32, kind="ExternalInput").ap()
    cpwh = nc.dram_tensor("cpwh", [128, 8 * D], F32, kind="ExternalInput").ap()
    cpbh = nc.dram_tensor("cpbh", [1, D], F32, kind="ExternalInput").ap()
    onesh = nc.dram_tensor("onesh", [1, 128], F32, kind="ExternalInput").ap()
    identh = nc.dram_tensor("identh", [128, 128], F32, kind="ExternalInput").ap()
    x3_o = nc.dram_tensor("x3_o", [TOK, D], F32, kind="ExternalOutput").ap()
    amax_o = nc.dram_tensor("amax_o", [1, 1], F32, kind="ExternalOutput").ap()

    with ExitStack() as ctx:
        sb = lambda n_, sh, dt: ctx.enter_context(nc.sbuf_tensor(n_, sh, dt))
        pt = lambda n_, sh: ctx.enter_context(nc.psum_tensor(n_, sh, F32))
        dmall = sb("dmall", [128, 8 * 512], F32)
        pm8 = sb("pm8", [128, 8], F32)
        macc = sb("macc", [128, 512], F32)
        maccb = sb("maccb", [128, 512], BF16)
        cpw = sb("cpw", [128, 8 * D], BF16)
        cpb = sb("cpb", [1, D], BF16)
        ones1 = sb("ones1", [1, 128], BF16)
        ident = sb("ident_s", [128, 128], F32)
        q1l = sb("q1l", [1, D], BF16)
        identb = sb("identb", [128, 128], BF16)
        tql = sb("tql", [1, D], F32)
        q2l = sb("q2l", [1, D], BF16)
        q2lT = sb("q2lT", [128, 8], BF16)
        o2fr = sb("o2fr", [1, D], F32)
        o2fx = sb("o2fx", [64, 1], F32)
        flg = sb("flg", [1, 1], F32)
        x2f = sb("x2f", [128, 2 * D], F32)
        o1t = sb("o1t", [128, 2 * D], F32)
        o2t = sb("o2t", [128, 2 * D], F32)
        o2s = sb("o2s", [128, 2 * D], F32)
        mprev = sb("mprev", [128, 2 * 512], F32)
        mtot = sb("mtot", [128, 2 * 512], BF16)
        q1t = sb("q1t", [128, 2 * D], BF16)
        q2b = sb("q2b", [128, 2 * D], BF16)
        q2T = sb("q2T", [128, 2 * D], BF16)
        tq2 = sb("tq2", [128, D], F32)
        yb_ = sb("yb_", [128, 2 * D], BF16)
        yT_ = sb("yT_", [128, 2 * D], BF16)
        yl_ = sb("yl_", [128, 2 * D], F32)
        sqv = sb("sqv", [128, D], F32)
        mst = sb("mst", [128, 8], F32)
        amax = sb("amax", [128, 2], F32)
        psOa = pt("psOa", [128, 64])
        psOb = pt("psOb", [128, 64])
        psA = pt("psA", [128, 512])
        psB = pt("psB", [128, 512])
        psF = pt("psF", [64, 1])
        psG = pt("psG", [1, 64])
        psQ = ctx.enter_context(nc.psum_tensor("psQ", [128, 1], BF16))
        psH = pt("psH", [1, 128])
        block = ctx.enter_context(nc.Block())

        ld = Ctr(nc, "ld")
        ldg = Ctr(nc, "ldg")
        xin = [Ctr(nc, "xin0"), Ctr(nc, "xin1")]
        shf = [Ctr(nc, "shf0"), Ctr(nc, "shf1")]
        ttr = [Ctr(nc, "ttr0"), Ctr(nc, "ttr1")]
        tty = [Ctr(nc, "tty0"), Ctr(nc, "tty1")]
        ldv = Ctr(nc, "ldv")
        lds = Ctr(nc, "lds")
        ldt = Ctr(nc, "ldt")
        st = Ctr(nc, "st")

        NVA = 55
        NVB = 36
        NSB = 3
        NTB = 16 + 8 + 2
        NTA = 40
        xin_marks = {}
        vm = {}

        @block.sync
        def _(sync):
            ld.inc(sync.dma_start(out=dmall[:], in_=dmh[:, :]), 16)
            ld.inc(sync.dma_start(out=pm8[:], in_=pmh[:, :]), 16)
            ld.inc(sync.dma_start(out=q1l[:], in_=q1lh[:, :]), 16)
            ld.inc(sync.dma_start(out=flg[:], in_=flagh[:, :]), 16)
            ld.inc(sync.dma_start(out=ident[:], in_=identh[:, :]), 16)
            for i in range(NT):
                s_ = i % 2
                if i >= 2:
                    sync.wait_ge(st.sem, 16 * (i - 1))
                xin[s_].inc(sync.dma_start(out=x2f[:, s_ * D:(s_ + 1) * D],
                                           in_=x2h[i * 128:(i + 1) * 128, :]), 16)
                xin[s_].inc(sync.dma_start(out=o1t[:, s_ * D:(s_ + 1) * D],
                                           in_=o1h[i * 128:(i + 1) * 128, :]), 16)
                xin[s_].inc(sync.dma_start(out=o2t[:, s_ * D:(s_ + 1) * D],
                                           in_=o2ih[i * 128:(i + 1) * 128, :]), 16)
                xin[s_].inc(sync.dma_start(out=q1t[:, s_ * D:(s_ + 1) * D],
                                           in_=q1h[i * 128:(i + 1) * 128, :]), 16)
                if i >= 1:
                    xin[s_].inc(sync.dma_start(
                        out=mprev[:, s_ * 512:(s_ + 1) * 512],
                        in_=mah[i - 1, :, :]), 16)
                xin_marks[i] = xin[s_].n
                base_v = NVA + i * NVB
                if i >= 2:
                    sync.wait_ge(ldt.sem, NTA + (i - 1) * 18)
                sync.wait_ge(ldv.sem, base_v + 3)
                for kc in range(8):
                    ttr[s_].inc(sync.dma_start_transpose(
                        q2T[:, s_ * D + kc * 128: s_ * D + (kc + 1) * 128],
                        q2b[:, s_ * D + kc * 128: s_ * D + (kc + 1) * 128]), 16)
                sync.wait_ge(ldv.sem, base_v + 20)
                shf[s_].inc(sync.dma_start(
                    out=o2s[1:128, s_ * D:(s_ + 1) * D],
                    in_=o2t[0:127, s_ * D:(s_ + 1) * D]), 16)
                if i == 0:
                    sync.wait_ge(ldv.sem, NVA)
                    shf[s_].inc(sync.dma_start(
                        out=o2s[0:1, 0:D], in_=o2fr[0:1, :]), 16)
                else:
                    shf[s_].inc(sync.dma_start(
                        out=o2s[0:1, s_ * D:(s_ + 1) * D],
                        in_=o2t[127:128, (1 - s_) * D:(2 - s_) * D]), 16)
                sync.wait_ge(ldv.sem, base_v + 22)
                for kc in range(8):
                    tty[s_].inc(sync.dma_start_transpose(
                        yT_[:, s_ * D + kc * 128: s_ * D + (kc + 1) * 128],
                        yb_[:, s_ * D + kc * 128: s_ * D + (kc + 1) * 128]), 16)
                sync.wait_ge(lds.sem, i * NSB + 3)
                st.inc(sync.dma_start(out=x3_o[i * 128:(i + 1) * 128, :],
                                      in_=yl_[:, s_ * D:(s_ + 1) * D]), 16)
            sync.wait_ge(ldv.sem, NVA + NT * NVB + 1)
            st.inc(sync.dma_start(out=amax_o[:, :], in_=amax[0:1, 1:2]), 16)
            st.wait(sync)

        @block.gpsimd
        def _(gp):
            ldg.inc(gp.dma_start(out=cpw[:], in_=cpwh[:, :]), 16)
            ldg.inc(gp.dma_start(out=cpb[:], in_=cpbh[:, :]), 16)
            ldg.inc(gp.dma_start(out=ones1[:], in_=onesh[:, :]), 16)
            ldg.inc(gp.dma_start(out=identb[:], in_=identh[:, :]), 16)

        @block.tensor
        def _(t):
            ld.wait(t)
            ldg.wait(t)
            t.wait_ge(ldv.sem, 14)
            for kc in range(8):
                if kc >= 1:
                    t.wait_ge(ldv.sem, 14 + kc)
                ldt.inc(t.transpose(psQ[0:128, 0:1],
                                    q2l[0:1, kc * 128:(kc + 1) * 128],
                                    identb[0:1, 0:1]))
            for h in range(16):
                t.wait_ge(ldv.sem, 22)
                base = 22 + h * 2
                if h >= 1:
                    t.wait_ge(ldv.sem, base)
                ldt.inc(t.matmul(
                    psF[0:64, 0:1],
                    maccb[(h % 2) * 64:(h % 2) * 64 + 64,
                          (h // 2) * 64:(h // 2) * 64 + 64],
                    q2lT[(h % 2) * 64:(h % 2) * 64 + 64, (h // 2):(h // 2) + 1],
                    start=True, stop=True,
                    tile_position=((h % 2) * 64, 0)))
                t.wait_ge(ldv.sem, 22 + h * 2 + 1)
                ldt.inc(t.transpose(psG[0:1, 0:64], o2fx[0:64, 0:1],
                                    ident[0:64, 0:64]))
            for i in range(NT):
                s_ = i % 2
                base_v = NVA + i * NVB
                t.wait_ge(ttr[s_].sem, 128 * (i // 2 + 1))
                t.wait_ge(ldv.sem, base_v + 4)
                for hp in range(8):
                    for hh in range(2):
                        h = hp * 2 + hh
                        ps = psOa if hh == 0 else psOb
                        if h >= 2:
                            t.wait_ge(ldv.sem, base_v + 4 + h - 1)
                        elif i >= 1:
                            t.wait_ge(ldv.sem, NVA + (i - 1) * NVB + 4 + 15)
                        ldt.inc(t.matmul(
                            ps[:, 0:64],
                            q2T[(h % 2) * 64:(h % 2) * 64 + 64,
                                s_ * D + (h // 2) * 128: s_ * D + (h // 2) * 128 + 128],
                            mtot[(h % 2) * 64:(h % 2) * 64 + 64,
                                 s_ * 512 + (h // 2) * 64: s_ * 512 + (h // 2) * 64 + 64],
                            start=True, stop=True,
                            tile_position=((h % 2) * 64, 0)))
                t.wait_ge(tty[s_].sem, 128 * (i // 2 + 1))
                if i >= 1:
                    t.wait_ge(ldv.sem, NVA + (i - 1) * NVB + 24)
                for dc in range(2):
                    ps = psA if dc == 0 else psB
                    last = None
                    for kc in range(8):
                        last = t.matmul(
                            ps[:, 0:512],
                            yT_[:, s_ * D + kc * 128: s_ * D + (kc + 1) * 128],
                            cpw[:, kc * D + dc * 512: kc * D + dc * 512 + 512],
                            start=(kc == 0), stop=False)
                    last = t.matmul(ps[:, 0:512], ones1[:, :],
                                    cpb[:, dc * 512: dc * 512 + 512],
                                    start=False, stop=True)
                    ldt.inc(last)
            t.wait_ge(ldv.sem, NVA + NT * NVB)
            ldt.inc(t.transpose(psH[0:1, 0:128], amax[0:128, 0:1],
                                ident[:, :]))

        @block.vector
        def _(v):
            def ch(inst):
                ldv.inc(inst)
                v.wait_ge(ldv.sem, ldv.n)
                return inst
            ld.wait(v)
            ldg.wait(v)
            ch(v.memset(macc[:], 0.0))
            for c in range(8):
                ch(v.scalar_tensor_tensor(macc[:], dmall[:, c * 512:(c + 1) * 512],
                                          pm8[:, c:c + 1], macc[:],
                                          op0=ALU.mult, op1=ALU.add))
            ch(v.tensor_copy(maccb[:], macc[:]))
            ch(v.memset(amax[:], 0.0))
            ch(v.tensor_scalar(tql[:], q1l[:], 0.0, 0.125,
                               op0=ALU.min, op1=ALU.mult))
            ch(v.tensor_scalar(q2l[:], q1l[:], 0.0, 0.0025,
                               op0=ALU.max, op1=ALU.mult))
            ch(v.tensor_add(q2l[:], q2l[:], tql[:]))
            for kc in range(8):
                v.wait_ge(ldt.sem, kc + 1)
                ch(v.tensor_copy(q2lT[:, kc:kc + 1], psQ[0:128, 0:1]))
            for h in range(16):
                v.wait_ge(ldt.sem, 8 + h * 2 + 1)
                ch(v.tensor_copy(o2fx[:, :], psF[0:64, 0:1]))
                v.wait_ge(ldt.sem, 8 + h * 2 + 2)
                ch(v.tensor_copy(o2fr[0:1, h * 64:(h + 1) * 64],
                                 psG[0:1, 0:64]))
            ch(v.tensor_scalar_mul(o2fr[:], o2fr[:], flg[:]))
            for i in range(NT):
                s_ = i % 2
                base_t = NTA + i * 18
                v.wait_ge(xin[s_].sem, 16 * (xin_marks[i] // 16))
                ch(v.tensor_scalar(tq2[:], q1t[:, s_ * D:(s_ + 1) * D],
                                   0.0, 0.125, op0=ALU.min, op1=ALU.mult))
                ch(v.tensor_scalar(q2b[:, s_ * D:(s_ + 1) * D],
                                   q1t[:, s_ * D:(s_ + 1) * D],
                                   0.0, 0.0025, op0=ALU.max, op1=ALU.mult))
                ch(v.tensor_add(q2b[:, s_ * D:(s_ + 1) * D],
                                q2b[:, s_ * D:(s_ + 1) * D], tq2[:]))
                if i == 0:
                    ch(v.tensor_copy(mtot[:, 0:512], macc[:]))
                else:
                    ch(v.tensor_add(mtot[:, s_ * 512:(s_ + 1) * 512],
                                    macc[:], mprev[:, s_ * 512:(s_ + 1) * 512]))
                for hp in range(8):
                    for hh in range(2):
                        h = hp * 2 + hh
                        ps = psOa if hh == 0 else psOb
                        v.wait_ge(ldt.sem, base_t + h + 1)
                        ch(v.tensor_add(
                            o2t[:, s_ * D + h * 64: s_ * D + (h + 1) * 64],
                            o2t[:, s_ * D + h * 64: s_ * D + (h + 1) * 64],
                            ps[:, 0:64]))
                v.wait_ge(shf[s_].sem, 32 * (i // 2 + 1))
                ch(v.tensor_add(yl_[:, s_ * D:(s_ + 1) * D],
                                o1t[:, s_ * D:(s_ + 1) * D],
                                o2s[:, s_ * D:(s_ + 1) * D]))
                ch(v.tensor_copy(yb_[:, s_ * D:(s_ + 1) * D],
                                 yl_[:, s_ * D:(s_ + 1) * D]))
                v.wait_ge(ldt.sem, base_t + 18)
                ch(v.tensor_add(yl_[:, s_ * D: s_ * D + 512],
                                x2f[:, s_ * D: s_ * D + 512], psA[:, 0:512]))
                ch(v.tensor_add(yl_[:, s_ * D + 512: s_ * D + 1024],
                                x2f[:, s_ * D + 512: s_ * D + 1024],
                                psB[:, 0:512]))
                ch(v.tensor_reduce(mst[:, 0:1], yl_[:, s_ * D:(s_ + 1) * D],
                                   axis=AX.X, op=ALU.add))
                ch(v.memset(mst[:, 2:3], 0.0))
                ch(v.tensor_scalar_mul(mst[:, 1:2], mst[:, 0:1], 1.0 / D))
                v.wait_ge(lds.sem, i * NSB + 1)
                ch(v.tensor_scalar_mul(mst[:, 3:4], mst[:, 2:3], 1.0 / D))
                ch(v.tensor_tensor(mst[:, 4:5], mst[:, 1:2], mst[:, 1:2],
                                   ALU.mult))
                ch(v.tensor_sub(mst[:, 4:5], mst[:, 3:4], mst[:, 4:5]))
                ch(v.tensor_scalar_add(mst[:, 4:5], mst[:, 4:5], 1e-5))
                v.wait_ge(lds.sem, i * NSB + 2)
                ch(v.reciprocal(mst[:, 6:7], mst[:, 5:6]))
                ch(v.tensor_tensor(mst[:, 7:8], mst[:, 1:2], mst[:, 6:7],
                                   ALU.mult))
                ch(v.tensor_scalar_mul(mst[:, 7:8], mst[:, 7:8], -1.0))
                v.wait_ge(lds.sem, i * NSB + 3)
                ch(v.tensor_reduce(mst[:, 0:1], yl_[:, s_ * D:(s_ + 1) * D],
                                   axis=AX.X, op=ALU.max,
                                   apply_absolute_value=True))
                ch(v.tensor_max(amax[:, 0:1], amax[:, 0:1], mst[:, 0:1]))
            v.wait_ge(ldt.sem, NTA + NT * 18 + 1)
            ch(v.tensor_reduce(amax[0:1, 1:2], psH[0:1, 0:128],
                               axis=AX.X, op=ALU.max))

        @block.scalar
        def _(s):
            for i in range(NT):
                s_ = i % 2
                base_v = NVA + i * NVB
                s.wait_ge(ldv.sem, base_v + 27)
                lds.inc(s.activation(out=sqv[:], in_=yl_[:, s_ * D:(s_ + 1) * D],
                                     func=AF.Square, accum_out=mst[:, 2:3]))
                s.wait_ge(ldv.sem, base_v + 31)
                lds.inc(s.activation(out=mst[:, 5:6], in_=mst[:, 4:5],
                                     func=AF.Sqrt))
                s.wait_ge(ldv.sem, base_v + 34)
                lds.inc(s.activation(out=yl_[:, s_ * D:(s_ + 1) * D],
                                     in_=yl_[:, s_ * D:(s_ + 1) * D],
                                     func=AF.Identity,
                                     bias=mst[:, 7:8], scale=mst[:, 6:7]))
    return nc

def emit_qdq(v, ch, out_ap, in_ap, rsc, sc, ibuf, fbuf, lo, hi):
    ch(v.tensor_scalar_mul(fbuf, in_ap, rsc))
    ch(v.tensor_copy(ibuf, fbuf))
    ch(v.tensor_copy(fbuf, ibuf))
    ch(v.tensor_scalar(fbuf, fbuf, float(hi), float(lo), op0=ALU.min,
                       op1=ALU.max))
    ch(v.tensor_scalar_mul(out_ap, fbuf, sc))


def build_LG():
    nc = bass.Bass("TRN2", target_bir_lowering=False,
                   detect_race_conditions=False, dynamic_dma_scratch_size=4096)
    x3h = nc.dram_tensor("x3h", [TOK, D], F32, kind="ExternalInput").ap()
    am8h = nc.dram_tensor("am8h", [1, 8], F32, kind="ExternalInput").ap()
    wuh = nc.dram_tensor("wuh", [3072, D], F32, kind="ExternalInput").ap()
    wgh = nc.dram_tensor("wgh", [3072, D], F32, kind="ExternalInput").ap()
    identh = nc.dram_tensor("identh", [128, 128], F32, kind="ExternalInput").ap()
    up_o = nc.dram_tensor("up_o", [TOK, 3072], F32, kind="ExternalOutput").ap()
    gt_o = nc.dram_tensor("gt_o", [TOK, 3072], F32, kind="ExternalOutput").ap()
    mx_o = nc.dram_tensor("mx_o", [1, 2], F32, kind="ExternalOutput").ap()

    with ExitStack() as ctx:
        sb = lambda n_, sh, dt: ctx.enter_context(nc.sbuf_tensor(n_, sh, dt))
        pt = lambda n_, sh: ctx.enter_context(nc.psum_tensor(n_, sh, F32))
        wuT = sb("wuT", [128, 8 * 3072], BF16)
        wgT = sb("wgT", [128, 8 * 3072], BF16)
        wtl = sb("wtl", [128, 2 * D], F32)
        wsc = sb("wsc", [128, 2 * 32], F32)
        wrc = sb("wrc", [128, 32], F32)
        wqi = sb("wqi", [128, D], mybir.dt.int32)
        wqf = sb("wqf", [128, D], F32)
        wqb = sb("wqb", [128, 2 * D], BF16)
        am8 = sb("am8", [1, 8], F32)
        gmx = sb("gmx", [128, 4], F32)
        x3f = sb("x3f", [128, 2 * D], F32)
        xqi = sb("xqi", [128, D], mybir.dt.int32)
        xqf = sb("xqf", [128, D], F32)
        xqb = sb("xqb", [128, 2 * D], BF16)
        xqT = sb("xqT", [128, NT * D], BF16)
        urs = sb("urs", [128, 2 * 512], F32)
        mxc = sb("mxc", [128, 4], F32)
        mxr = sb("mxr", [1, 4], F32)
        mrow = sb("mrow", [1, 128], F32)
        ident = sb("ident_s", [128, 128], F32)
        psA = pt("psA", [128, 512])
        psB = pt("psB", [128, 512])
        psC = pt("psC", [128, 512])
        psD = pt("psD", [128, 512])
        psX = pt("psX", [128, 1])
        psT = pt("psT", [1, 128])
        block = ctx.enter_context(nc.Block())

        ld = Ctr(nc, "ld")
        win = [Ctr(nc, "win0"), Ctr(nc, "win1")]
        xin = [Ctr(nc, "xin0"), Ctr(nc, "xin1")]
        twq = [Ctr(nc, "twq0"), Ctr(nc, "twq1")]
        txq = [Ctr(nc, "txq0"), Ctr(nc, "txq1")]
        lgv = Ctr(nc, "lgv")
        lgt = Ctr(nc, "lgt")
        st = Ctr(nc, "st")

        NW = 48
        NVW = 8
        NVX = 6
        vm = {}

        @block.sync
        def _(sync):
            ld.inc(sync.dma_start(out=am8[:], in_=am8h[:, :]), 16)
            ld.inc(sync.dma_start(out=ident[:], in_=identh[:, :]), 16)
            for wi in range(NW):
                s_ = wi % 2
                src = wuh if wi < 24 else wgh
                t24 = wi % 24
                if wi >= 2:
                    sync.wait_ge(lgv.sem, (wi - 1) * NVW)
                win[s_].inc(sync.dma_start(
                    out=wtl[:, s_ * D:(s_ + 1) * D],
                    in_=src[t24 * 128:(t24 + 1) * 128, :]), 16)
                sync.wait_ge(lgv.sem, wi * NVW + 8)
                dst = wuT if wi < 24 else wgT
                for kc in range(8):
                    twq[s_].inc(sync.dma_start_transpose(
                        dst[:, kc * 3072 + t24 * 128: kc * 3072 + (t24 + 1) * 128],
                        wqb[:, s_ * D + kc * 128: s_ * D + (kc + 1) * 128]), 16)
            base_x = NW * NVW + 4
            for i in range(NT):
                s_ = i % 2
                if i >= 2:
                    sync.wait_ge(lgv.sem, base_x + (i - 1) * NVX)
                xin[s_].inc(sync.dma_start(
                    out=x3f[:, s_ * D:(s_ + 1) * D],
                    in_=x3h[i * 128:(i + 1) * 128, :]), 16)
                sync.wait_ge(lgv.sem, base_x + i * NVX + NVX)
                for kc in range(8):
                    txq[s_].inc(sync.dma_start_transpose(
                        xqT[:, i * D + kc * 128: i * D + (kc + 1) * 128],
                        xqb[:, s_ * D + kc * 128: s_ * D + (kc + 1) * 128]), 16)
            base_m = base_x + NT * NVX + 1
            g = 0
            for i in range(NT):
                for mi in range(2):
                    dsto = up_o if mi == 0 else gt_o
                    for oc in range(6):
                        sync.wait_ge(lgv.sem, base_m + g * 3 + 1)
                        st.inc(sync.dma_start(
                            out=dsto[i * 128:(i + 1) * 128,
                                     oc * 512:(oc + 1) * 512],
                            in_=urs[:, (g % 2) * 512:(g % 2 + 1) * 512]), 16)
                        g += 1
            sync.wait_ge(lgv.sem, 48 * 8 + 4 + 16 * 6 + 1 + 192 * 3 + 2)
            st.inc(sync.dma_start(out=mx_o[:, :], in_=mxr[0:1, 0:2]), 16)
            st.wait(sync)

        @block.tensor
        def _(t):
            ld.wait(t)
            t.wait_ge(lgv.sem, 48 * 8 + 2)
            lgt.inc(t.transpose(psX[0:128, 0:1], mrow[0:1, :],
                                ident[0:1, 0:1]))
            base_m = NW * NVW + 4 + NT * NVX + 1
            g = 0
            for i in range(NT):
                s_ = i % 2
                t.wait_ge(txq[s_].sem, 128 * (i // 2 + 1))
                t.wait_ge(twq[0].sem, 24 * 8 * 16)
                t.wait_ge(twq[1].sem, 24 * 8 * 16)
                for mi in range(2):
                    wT = wuT if mi == 0 else wgT
                    for oc in range(6):
                        ps = [psA, psB, psC, psD][g % 4]
                        if g >= 4:
                            t.wait_ge(lgv.sem, base_m + (g - 4) * 3 + 1)
                        last = None
                        for kc in range(8):
                            last = t.matmul(
                                ps[:, 0:512],
                                xqT[:, i * D + kc * 128: i * D + (kc + 1) * 128],
                                wT[:, kc * 3072 + oc * 512: kc * 3072 + oc * 512 + 512],
                                start=(kc == 0), stop=(kc == 7))
                        lgt.inc(last)
                        g += 1
            t.wait_ge(lgv.sem, 48 * 8 + 4 + 16 * 6 + 1 + 192 * 3)
            lgt.inc(t.transpose(psT[0:1, 0:128], mxc[0:128, 0:1],
                                ident[:, :]))
            t.wait_ge(lgv.sem, 48 * 8 + 4 + 16 * 6 + 1 + 192 * 3 + 1)
            lgt.inc(t.transpose(psT[0:1, 0:128], mxc[0:128, 1:2],
                                ident[:, :]))

        @block.vector
        def _(v):
            def ch(inst):
                lgv.inc(inst)
                v.wait_ge(lgv.sem, lgv.n)
                return inst
            ld.wait(v)
            for wi in range(NW):
                s_ = wi % 2
                v.wait_ge(win[s_].sem, 16 * (wi // 2 + 1))
                if wi >= 2:
                    v.wait_ge(twq[s_].sem, 128 * (wi // 2))
                w3 = wtl[:, s_ * D:(s_ + 1) * D].rearrange(
                    "p (a b) -> p a b", b=32)
                ch(v.tensor_reduce(wsc[:, s_ * 32:(s_ + 1) * 32], w3,
                                   axis=AX.X, op=ALU.max,
                                   apply_absolute_value=True))
                ch(v.tensor_scalar(wsc[:, s_ * 32:(s_ + 1) * 32],
                                   wsc[:, s_ * 32:(s_ + 1) * 32],
                                   1.0 / 7.0, 1e-12, op0=ALU.mult,
                                   op1=ALU.max))
                ch(v.reciprocal(wrc[:], wsc[:, s_ * 32:(s_ + 1) * 32]))
                rsb = wrc[:].unsqueeze(2).broadcast_to([128, 32, 32])
                ch(v.tensor_tensor(wqf[:].rearrange("p (a b) -> p a b", b=32),
                                   w3, rsb, ALU.mult))
                ch(v.tensor_copy(wqi[:], wqf[:]))
                ch(v.tensor_copy(wqf[:], wqi[:]))
                ch(v.tensor_scalar(wqf[:], wqf[:], 7.0, -8.0,
                                   op0=ALU.min, op1=ALU.max))
                ssb = wsc[:, s_ * 32:(s_ + 1) * 32].unsqueeze(2).broadcast_to(
                    [128, 32, 32])
                ch(v.tensor_tensor(wqb[:, s_ * D:(s_ + 1) * D].rearrange(
                    "p (a b) -> p a b", b=32),
                    wqf[:].rearrange("p (a b) -> p a b", b=32),
                    ssb, ALU.mult))
            ch(v.tensor_reduce(mxr[0:1, 2:3], am8[0:1, :], axis=AX.X,
                               op=ALU.max))
            ch(v.tensor_copy(mrow[0:1, :],
                             mxr[0:1, 2:3].broadcast_to([1, 128])))
            v.wait_ge(lgt.sem, 1)
            ch(v.tensor_scalar(gmx[:, 0:1], psX[0:128, 0:1], 1.0 / 32767.0,
                               1e-12, op0=ALU.mult, op1=ALU.max))
            ch(v.reciprocal(gmx[:, 1:2], gmx[:, 0:1]))
            for i in range(NT):
                s_ = i % 2
                v.wait_ge(xin[s_].sem, 16 * (i // 2 + 1))
                if i >= 2:
                    v.wait_ge(txq[s_].sem, 128 * (i // 2))
                ch(v.tensor_scalar_mul(xqf[:], x3f[:, s_ * D:(s_ + 1) * D],
                                       gmx[:, 1:2]))
                ch(v.tensor_copy(xqi[:], xqf[:]))
                ch(v.tensor_copy(xqf[:], xqi[:]))
                ch(v.tensor_scalar(xqf[:], xqf[:], 32767.0, -32768.0,
                                   op0=ALU.min, op1=ALU.max))
                ch(v.tensor_scalar_mul(xqf[:], xqf[:], gmx[:, 0:1]))
                ch(v.tensor_copy(xqb[:, s_ * D:(s_ + 1) * D], xqf[:]))
            ch(v.memset(mxc[:, 0:2], 0.0))
            g = 0
            base_m = NW * NVW + 4 + NT * NVX + 1
            for i in range(NT):
                for mi in range(2):
                    for oc in range(6):
                        ps = [psA, psB, psC, psD][g % 4]
                        v.wait_ge(lgt.sem, 1 + g + 1)
                        if g >= 2:
                            v.wait_ge(st.sem, 16 * (g - 1))
                        ch(v.tensor_copy(urs[:, (g % 2) * 512:(g % 2 + 1) * 512],
                                         ps[:, 0:512]))
                        ch(v.tensor_reduce(mxc[:, 2:3],
                                           urs[:, (g % 2) * 512:(g % 2 + 1) * 512],
                                           axis=AX.X, op=ALU.max,
                                           apply_absolute_value=True))
                        ch(v.tensor_max(mxc[:, mi:mi + 1], mxc[:, mi:mi + 1],
                                        mxc[:, 2:3]))
                        g += 1
            v.wait_ge(lgt.sem, 1 + 192 + 1)
            ch(v.tensor_reduce(mxr[0:1, 0:1], psT[0:1, 0:128], axis=AX.X,
                               op=ALU.max))
            v.wait_ge(lgt.sem, 1 + 192 + 2)
            ch(v.tensor_reduce(mxr[0:1, 1:2], psT[0:1, 0:128], axis=AX.X,
                               op=ALU.max))
    return nc

def build_LH():
    nc = bass.Bass("TRN2", target_bir_lowering=False,
                   detect_race_conditions=False, dynamic_dma_scratch_size=4096)
    uph = nc.dram_tensor("uph", [TOK, 3072], F32, kind="ExternalInput").ap()
    gth = nc.dram_tensor("gth", [TOK, 3072], F32, kind="ExternalInput").ap()
    um8h = nc.dram_tensor("um8h", [1, 8], F32, kind="ExternalInput").ap()
    gm8h = nc.dram_tensor("gm8h", [1, 8], F32, kind="ExternalInput").ap()
    identh = nc.dram_tensor("identh", [128, 128], F32, kind="ExternalInput").ap()
    uq_o = nc.dram_tensor("uq_o", [TOK, 3072], F32, kind="ExternalOutput").ap()
    g2_o = nc.dram_tensor("g2_o", [TOK, 3072], F32, kind="ExternalOutput").ap()
    mx_o = nc.dram_tensor("mx_o", [1, 1], F32, kind="ExternalOutput").ap()

    NB = NT * 6
    with ExitStack() as ctx:
        sb = lambda n_, sh, dt: ctx.enter_context(nc.sbuf_tensor(n_, sh, dt))
        pt = lambda n_, sh: ctx.enter_context(nc.psum_tensor(n_, sh, F32))
        um8 = sb("um8", [1, 8], F32)
        gm8 = sb("gm8", [1, 8], F32)
        mrow = sb("mrow", [1, 128], F32)
        ident = sb("ident_s", [128, 128], F32)
        sc4 = sb("sc4", [128, 4], F32)
        upt = sb("upt", [128, 2 * 512], F32)
        gtt = sb("gtt", [128, 2 * 512], F32)
        uqt = sb("uqt", [128, 2 * 512], F32)
        gqt = sb("gqt", [128, 2 * 512], F32)
        sgt = sb("sgt", [128, 512], F32)
        g2t = sb("g2t", [128, 2 * 512], F32)
        ibuf = sb("ibuf", [128, 512], mybir.dt.int32)
        fbuf = sb("fbuf", [128, 512], F32)
        mxc = sb("mxc", [128, 2], F32)
        mxr = sb("mxr", [1, 4], F32)
        psX = pt("psX", [128, 1])
        psT = pt("psT", [1, 128])
        block = ctx.enter_context(nc.Block())

        ld = Ctr(nc, "ld")
        bin_ = [Ctr(nc, "bin0"), Ctr(nc, "bin1")]
        lhv = Ctr(nc, "lhv")
        lhs_ = Ctr(nc, "lhs")
        lht = Ctr(nc, "lht")
        st = Ctr(nc, "st")

        NVA = 9
        NVB = 18
        blocks = [(i, oc) for i in range(NT) for oc in range(6)]

        @block.sync
        def _(sync):
            ld.inc(sync.dma_start(out=um8[:], in_=um8h[:, :]), 16)
            ld.inc(sync.dma_start(out=gm8[:], in_=gm8h[:, :]), 16)
            ld.inc(sync.dma_start(out=ident[:], in_=identh[:, :]), 16)
            for g, (i, oc) in enumerate(blocks):
                s_ = g % 2
                if g >= 2:
                    sync.wait_ge(lhv.sem, NVA + (g - 1) * NVB)
                bin_[s_].inc(sync.dma_start(
                    out=upt[:, s_ * 512:(s_ + 1) * 512],
                    in_=uph[i * 128:(i + 1) * 128, oc * 512:(oc + 1) * 512]), 16)
                bin_[s_].inc(sync.dma_start(
                    out=gtt[:, s_ * 512:(s_ + 1) * 512],
                    in_=gth[i * 128:(i + 1) * 128, oc * 512:(oc + 1) * 512]), 16)
                base = NVA + g * NVB
                sync.wait_ge(lhv.sem, base + 5)
                st.inc(sync.dma_start(
                    out=uq_o[i * 128:(i + 1) * 128, oc * 512:(oc + 1) * 512],
                    in_=uqt[:, s_ * 512:(s_ + 1) * 512]), 16)
                sync.wait_ge(lhv.sem, base + 16)
                st.inc(sync.dma_start(
                    out=g2_o[i * 128:(i + 1) * 128, oc * 512:(oc + 1) * 512],
                    in_=g2t[:, s_ * 512:(s_ + 1) * 512]), 16)
            sync.wait_ge(lhv.sem, NVA + NB * NVB + 1)
            st.inc(sync.dma_start(out=mx_o[:, :], in_=mxr[0:1, 0:1]), 16)
            st.wait(sync)

        @block.tensor
        def _(t):
            ld.wait(t)
            t.wait_ge(lhv.sem, 3)
            lht.inc(t.transpose(psX[0:128, 0:1], mrow[0:1, :], ident[0:1, 0:1]))
            t.wait_ge(lhv.sem, 6)
            lht.inc(t.transpose(psX[0:128, 0:1], mrow[0:1, :], ident[0:1, 0:1]))
            t.wait_ge(lhv.sem, NVA + NB * NVB)
            lht.inc(t.transpose(psT[0:1, 0:128], mxc[0:128, 0:1], ident[:, :]))

        @block.scalar
        def _(s):
            for g, (i, oc) in enumerate(blocks):
                s_ = g % 2
                base = NVA + g * NVB
                s.wait_ge(lhv.sem, base + 10)
                lhs_.inc(s.activation(out=sgt[:], in_=gqt[:, s_ * 512:(s_ + 1) * 512],
                                      func=AF.Sigmoid))

        @block.vector
        def _(v):
            def ch(inst):
                lhv.inc(inst)
                v.wait_ge(lhv.sem, lhv.n)
                return inst
            ld.wait(v)
            ch(v.memset(mxc[:], 0.0))
            ch(v.tensor_reduce(mxr[0:1, 2:3], um8[0:1, :], axis=AX.X,
                               op=ALU.max))
            ch(v.tensor_copy(mrow[0:1, :],
                             mxr[0:1, 2:3].broadcast_to([1, 128])))
            v.wait_ge(lht.sem, 1)
            ch(v.tensor_scalar(sc4[:, 0:1], psX[0:128, 0:1], 1.0 / 32767.0,
                               1e-12, op0=ALU.mult, op1=ALU.max))
            ch(v.tensor_reduce(mxr[0:1, 2:3], gm8[0:1, :], axis=AX.X,
                               op=ALU.max))
            ch(v.tensor_copy(mrow[0:1, :],
                             mxr[0:1, 2:3].broadcast_to([1, 128])))
            v.wait_ge(lht.sem, 2)
            ch(v.tensor_scalar(sc4[:, 2:3], psX[0:128, 0:1], 1.0 / 32767.0,
                               1e-12, op0=ALU.mult, op1=ALU.max))
            ch(v.reciprocal(sc4[:, 1:2], sc4[:, 0:1]))
            ch(v.reciprocal(sc4[:, 3:4], sc4[:, 2:3]))
            for g, (i, oc) in enumerate(blocks):
                s_ = g % 2
                v.wait_ge(bin_[s_].sem, 32 * (g // 2 + 1))
                if g >= 2:
                    v.wait_ge(st.sem, 32 * (g - 1))
                sl5 = slice(s_ * 512, (s_ + 1) * 512)
                emit_qdq(v, ch, uqt[:, sl5], upt[:, sl5], sc4[:, 1:2],
                         sc4[:, 0:1], ibuf[:], fbuf[:], -32768.0, 32767.0)
                emit_qdq(v, ch, gqt[:, sl5], gtt[:, sl5], sc4[:, 3:4],
                         sc4[:, 2:3], ibuf[:], fbuf[:], -32768.0, 32767.0)
                v.wait_ge(lhs_.sem, g + 1)
                ch(v.tensor_scalar_mul(fbuf[:], sgt[:], 65536.0))
                ch(v.tensor_copy(ibuf[:], fbuf[:]))
                ch(v.tensor_copy(fbuf[:], ibuf[:]))
                ch(v.tensor_scalar(fbuf[:], fbuf[:], 65535.0, 0.0,
                                   op0=ALU.min, op1=ALU.max))
                ch(v.tensor_mul(g2t[:, sl5], gqt[:, sl5], fbuf[:]))
                ch(v.tensor_scalar_mul(g2t[:, sl5], g2t[:, sl5],
                                       1.0 / 65536.0))
                ch(v.tensor_reduce(mxc[:, 1:2], g2t[:, sl5], axis=AX.X,
                                   op=ALU.max, apply_absolute_value=True))
                ch(v.tensor_max(mxc[:, 0:1], mxc[:, 0:1], mxc[:, 1:2]))
            v.wait_ge(lht.sem, 3)
            ch(v.tensor_reduce(mxr[0:1, 0:1], psT[0:1, 0:128], axis=AX.X,
                               op=ALU.max))
    return nc


def build_LI():
    nc = bass.Bass("TRN2", target_bir_lowering=False,
                   detect_race_conditions=False, dynamic_dma_scratch_size=4096)
    uqh = nc.dram_tensor("uqh", [TOK, 3072], F32, kind="ExternalInput").ap()
    g2h = nc.dram_tensor("g2h", [TOK, 3072], F32, kind="ExternalInput").ap()
    m8h = nc.dram_tensor("m8h", [1, 8], F32, kind="ExternalInput").ap()
    identh = nc.dram_tensor("identh", [128, 128], F32, kind="ExternalInput").ap()
    o_o = nc.dram_tensor("o_o", [TOK, 3072], F32, kind="ExternalOutput").ap()
    mx_o = nc.dram_tensor("mx_o", [1, 1], F32, kind="ExternalOutput").ap()

    NB = NT * 6
    with ExitStack() as ctx:
        sb = lambda n_, sh, dt: ctx.enter_context(nc.sbuf_tensor(n_, sh, dt))
        pt = lambda n_, sh: ctx.enter_context(nc.psum_tensor(n_, sh, F32))
        m8 = sb("m8", [1, 8], F32)
        mrow = sb("mrow", [1, 128], F32)
        ident = sb("ident_s", [128, 128], F32)
        sc2 = sb("sc2", [128, 2], F32)
        uqt = sb("uqt", [128, 2 * 512], F32)
        g2t = sb("g2t", [128, 2 * 512], F32)
        ot_ = sb("ot_", [128, 2 * 512], F32)
        ibuf = sb("ibuf", [128, 512], mybir.dt.int32)
        fbuf = sb("fbuf", [128, 512], F32)
        mxc = sb("mxc", [128, 2], F32)
        mxr = sb("mxr", [1, 4], F32)
        psX = pt("psX", [128, 1])
        psT = pt("psT", [1, 128])
        block = ctx.enter_context(nc.Block())

        ld = Ctr(nc, "ld")
        bin_ = [Ctr(nc, "bin0"), Ctr(nc, "bin1")]
        liv = Ctr(nc, "liv")
        lit = Ctr(nc, "lit")
        st = Ctr(nc, "st")

        NVA = 6
        NVB = 8
        blocks = [(i, oc) for i in range(NT) for oc in range(6)]

        @block.sync
        def _(sync):
            ld.inc(sync.dma_start(out=m8[:], in_=m8h[:, :]), 16)
            ld.inc(sync.dma_start(out=ident[:], in_=identh[:, :]), 16)
            for g, (i, oc) in enumerate(blocks):
                s_ = g % 2
                if g >= 2:
                    sync.wait_ge(liv.sem, NVA + (g - 1) * NVB)
                bin_[s_].inc(sync.dma_start(
                    out=uqt[:, s_ * 512:(s_ + 1) * 512],
                    in_=uqh[i * 128:(i + 1) * 128, oc * 512:(oc + 1) * 512]), 16)
                bin_[s_].inc(sync.dma_start(
                    out=g2t[:, s_ * 512:(s_ + 1) * 512],
                    in_=g2h[i * 128:(i + 1) * 128, oc * 512:(oc + 1) * 512]), 16)
                base = NVA + g * NVB
                sync.wait_ge(liv.sem, base + 6)
                st.inc(sync.dma_start(
                    out=o_o[i * 128:(i + 1) * 128, oc * 512:(oc + 1) * 512],
                    in_=ot_[:, s_ * 512:(s_ + 1) * 512]), 16)
            sync.wait_ge(liv.sem, NVA + NB * NVB + 1)
            st.inc(sync.dma_start(out=mx_o[:, :], in_=mxr[0:1, 0:1]), 16)
            st.wait(sync)

        @block.tensor
        def _(t):
            ld.wait(t)
            t.wait_ge(liv.sem, 3)
            lit.inc(t.transpose(psX[0:128, 0:1], mrow[0:1, :], ident[0:1, 0:1]))
            t.wait_ge(liv.sem, NVA + NB * NVB)
            lit.inc(t.transpose(psT[0:1, 0:128], mxc[0:128, 0:1], ident[:, :]))

        @block.vector
        def _(v):
            def ch(inst):
                liv.inc(inst)
                v.wait_ge(liv.sem, liv.n)
                return inst
            ld.wait(v)
            ch(v.memset(mxc[:], 0.0))
            ch(v.tensor_reduce(mxr[0:1, 2:3], m8[0:1, :], axis=AX.X,
                               op=ALU.max))
            ch(v.tensor_copy(mrow[0:1, :],
                             mxr[0:1, 2:3].broadcast_to([1, 128])))
            v.wait_ge(lit.sem, 1)
            ch(v.tensor_scalar(sc2[:, 0:1], psX[0:128, 0:1], 1.0 / 32767.0,
                               1e-12, op0=ALU.mult, op1=ALU.max))
            ch(v.reciprocal(sc2[:, 1:2], sc2[:, 0:1]))
            ch(v.memset(mxc[:, 1:2], 0.0))
            for g, (i, oc) in enumerate(blocks):
                s_ = g % 2
                v.wait_ge(bin_[s_].sem, 32 * (g // 2 + 1))
                if g >= 2:
                    v.wait_ge(st.sem, 16 * (g - 1))
                sl5 = slice(s_ * 512, (s_ + 1) * 512)
                emit_qdq(v, ch, fbuf[:], g2t[:, sl5], sc2[:, 1:2],
                         sc2[:, 0:1], ibuf[:], fbuf[:], -32768.0, 32767.0)
                ch(v.tensor_mul(ot_[:, sl5], fbuf[:], uqt[:, sl5]))
                ch(v.tensor_reduce(mxc[:, 1:2], ot_[:, sl5], axis=AX.X,
                                   op=ALU.max, apply_absolute_value=True))
                ch(v.tensor_max(mxc[:, 0:1], mxc[:, 0:1], mxc[:, 1:2]))
            v.wait_ge(lit.sem, 2)
            ch(v.tensor_reduce(mxr[0:1, 0:1], psT[0:1, 0:128], axis=AX.X,
                               op=ALU.max))
    return nc


def build_LJ():
    nc = bass.Bass("TRN2", target_bir_lowering=False,
                   detect_race_conditions=False, dynamic_dma_scratch_size=4096)
    oh = nc.dram_tensor("oh", [TOK, 3072], F32, kind="ExternalInput").ap()
    m8h = nc.dram_tensor("m8h", [1, 8], F32, kind="ExternalInput").ap()
    x3h = nc.dram_tensor("x3h", [TOK, D], F32, kind="ExternalInput").ap()
    wdh = nc.dram_tensor("wdh", [D, 3072], F32, kind="ExternalInput").ap()
    identh = nc.dram_tensor("identh", [128, 128], F32, kind="ExternalInput").ap()
    x4_o = nc.dram_tensor("x4_o", [TOK, D], F32, kind="ExternalOutput").ap()

    with ExitStack() as ctx:
        sb = lambda n_, sh, dt: ctx.enter_context(nc.sbuf_tensor(n_, sh, dt))
        pt = lambda n_, sh: ctx.enter_context(nc.psum_tensor(n_, sh, F32))
        m8 = sb("m8", [1, 8], F32)
        mrow = sb("mrow", [1, 128], F32)
        ident = sb("ident_s", [128, 128], F32)
        sc2 = sb("sc2", [128, 2], F32)
        wdT = sb("wdT", [128, 24 * D], BF16)
        wtl = sb("wtl", [128, 2 * 3072], F32)
        wsc = sb("wsc", [128, 96], F32)
        wrc = sb("wrc", [128, 96], F32)
        wqi = sb("wqi", [128, 3072], mybir.dt.int32)
        wqf = sb("wqf", [128, 3072], F32)
        wqb = sb("wqb", [128, 2 * 3072], BF16)
        otl = sb("otl", [128, 2 * 3072], F32)
        oqi = sb("oqi", [128, 3072], mybir.dt.int32)
        oqf = sb("oqf", [128, 3072], F32)
        oqb = sb("oqb", [128, 2 * 3072], BF16)
        oqT = sb("oqT", [128, 2 * 24 * 128], BF16)
        x3f = sb("x3f", [128, 2 * D], F32)
        yl_ = sb("yl_", [128, 2 * D], F32)
        sqv = sb("sqv", [128, D], F32)
        mst = sb("mst", [128, 8], F32)
        psA = pt("psA", [128, 512])
        psB = pt("psB", [128, 512])
        psX = pt("psX", [128, 1])
        block = ctx.enter_context(nc.Block())

        ld = Ctr(nc, "ld")
        win = [Ctr(nc, "win0"), Ctr(nc, "win1")]
        oin = [Ctr(nc, "oin0"), Ctr(nc, "oin1")]
        twq = [Ctr(nc, "twq0"), Ctr(nc, "twq1")]
        toq = [Ctr(nc, "toq0"), Ctr(nc, "toq1")]
        ljv = Ctr(nc, "ljv")
        ljs = Ctr(nc, "ljs")
        ljt = Ctr(nc, "ljt")
        st = Ctr(nc, "st")

        NW = 8
        NVW = 8
        NVA = 4
        NVB = 24
        NVB = 17
        NSB = 3
        blocksw = list(range(NW))

        @block.sync
        def _(sync):
            ld.inc(sync.dma_start(out=m8[:], in_=m8h[:, :]), 16)
            ld.inc(sync.dma_start(out=ident[:], in_=identh[:, :]), 16)
            for wi in range(NW):
                s_ = wi % 2
                if wi >= 2:
                    sync.wait_ge(ljv.sem, NVA + (wi - 1) * NVW)
                win[s_].inc(sync.dma_start(
                    out=wtl[:, s_ * 3072:(s_ + 1) * 3072],
                    in_=wdh[wi * 128:(wi + 1) * 128, :]), 16)
                sync.wait_ge(ljv.sem, NVA + wi * NVW + NVW)
                for kc in range(24):
                    twq[s_].inc(sync.dma_start_transpose(
                        wdT[:, kc * D + wi * 128: kc * D + (wi + 1) * 128],
                        wqb[:, s_ * 3072 + kc * 128: s_ * 3072 + (kc + 1) * 128]),
                        16)
            base_x = NVA + NW * NVW
            for i in range(NT):
                s_ = i % 2
                if i >= 2:
                    sync.wait_ge(ljv.sem, base_x + (i - 1) * NVB)
                    sync.wait_ge(st.sem, 16 * (i - 1))
                oin[s_].inc(sync.dma_start(
                    out=otl[:, s_ * 3072:(s_ + 1) * 3072],
                    in_=oh[i * 128:(i + 1) * 128, :]), 16)
                oin[s_].inc(sync.dma_start(
                    out=x3f[:, s_ * D:(s_ + 1) * D],
                    in_=x3h[i * 128:(i + 1) * 128, :]), 16)
                sync.wait_ge(ljv.sem, base_x + i * NVB + 5)
                for kc in range(24):
                    toq[s_].inc(sync.dma_start_transpose(
                        oqT[:, s_ * 3072 + kc * 128: s_ * 3072 + (kc + 1) * 128],
                        oqb[:, s_ * 3072 + kc * 128: s_ * 3072 + (kc + 1) * 128]),
                        16)
                sync.wait_ge(ljs.sem, i * NSB + 3)
                st.inc(sync.dma_start(out=x4_o[i * 128:(i + 1) * 128, :],
                                      in_=yl_[:, s_ * D:(s_ + 1) * D]), 16)
            st.wait(sync)

        @block.tensor
        def _(t):
            ld.wait(t)
            t.wait_ge(ljv.sem, 2)
            ljt.inc(t.transpose(psX[0:128, 0:1], mrow[0:1, :], ident[0:1, 0:1]))
            base_x = NVA + NW * NVW
            for i in range(NT):
                s_ = i % 2
                t.wait_ge(toq[s_].sem, 24 * 16 * (i // 2 + 1))
                t.wait_ge(twq[0].sem, 4 * 24 * 16)
                t.wait_ge(twq[1].sem, 4 * 24 * 16)
                if i >= 1:
                    t.wait_ge(ljv.sem, base_x + (i - 1) * NVB + 7)
                for dc in range(2):
                    ps = psA if dc == 0 else psB
                    last = None
                    for kc in range(24):
                        last = t.matmul(
                            ps[:, 0:512],
                            oqT[:, s_ * 3072 + kc * 128: s_ * 3072 + (kc + 1) * 128],
                            wdT[:, kc * D + dc * 512: kc * D + dc * 512 + 512],
                            start=(kc == 0), stop=(kc == 23))
                    ljt.inc(last)

        @block.scalar
        def _(s):
            base_x = NVA + NW * NVW
            for i in range(NT):
                s_ = i % 2
                base_v = base_x + i * NVB
                s.wait_ge(ljv.sem, base_v + 10)
                ljs.inc(s.activation(out=sqv[:], in_=yl_[:, s_ * D:(s_ + 1) * D],
                                     func=AF.Square, accum_out=mst[:, 2:3]))
                s.wait_ge(ljv.sem, base_v + 14)
                ljs.inc(s.activation(out=mst[:, 5:6], in_=mst[:, 4:5],
                                     func=AF.Sqrt))
                s.wait_ge(ljv.sem, base_v + 17)
                ljs.inc(s.activation(out=yl_[:, s_ * D:(s_ + 1) * D],
                                     in_=yl_[:, s_ * D:(s_ + 1) * D],
                                     func=AF.Identity,
                                     bias=mst[:, 7:8], scale=mst[:, 6:7]))

        @block.vector
        def _(v):
            def ch(inst):
                ljv.inc(inst)
                v.wait_ge(ljv.sem, ljv.n)
                return inst
            ld.wait(v)
            ch(v.tensor_reduce(wqf[0:1, 0:1], m8[0:1, :], axis=AX.X,
                               op=ALU.max))
            ch(v.tensor_copy(mrow[0:1, :],
                             wqf[0:1, 0:1].broadcast_to([1, 128])))
            v.wait_ge(ljt.sem, 1)
            ch(v.tensor_scalar(sc2[:, 0:1], psX[0:128, 0:1], 1.0 / 32767.0,
                               1e-12, op0=ALU.mult, op1=ALU.max))
            ch(v.reciprocal(sc2[:, 1:2], sc2[:, 0:1]))
            for wi in range(NW):
                s_ = wi % 2
                v.wait_ge(win[s_].sem, 16 * (wi // 2 + 1))
                if wi >= 2:
                    v.wait_ge(twq[s_].sem, 24 * 16 * (wi // 2))
                w3 = wtl[:, s_ * 3072:(s_ + 1) * 3072].rearrange(
                    "p (a b) -> p a b", b=32)
                ch(v.tensor_reduce(wsc[:], w3, axis=AX.X, op=ALU.max,
                                   apply_absolute_value=True))
                ch(v.tensor_scalar(wsc[:], wsc[:], 1.0 / 7.0, 1e-12,
                                   op0=ALU.mult, op1=ALU.max))
                ch(v.reciprocal(wrc[:], wsc[:]))
                rsb = wrc[:].unsqueeze(2).broadcast_to([128, 96, 32])
                ch(v.tensor_tensor(wqf[:].rearrange("p (a b) -> p a b", b=32),
                                   w3, rsb, ALU.mult))
                ch(v.tensor_copy(wqi[:], wqf[:]))
                ch(v.tensor_copy(wqf[:], wqi[:]))
                ch(v.tensor_scalar(wqf[:], wqf[:], 7.0, -8.0,
                                   op0=ALU.min, op1=ALU.max))
                ssb = wsc[:].unsqueeze(2).broadcast_to([128, 96, 32])
                ch(v.tensor_tensor(wqb[:, s_ * 3072:(s_ + 1) * 3072].rearrange(
                    "p (a b) -> p a b", b=32),
                    wqf[:].rearrange("p (a b) -> p a b", b=32),
                    ssb, ALU.mult))
            base_x = NVA + NW * NVW
            for i in range(NT):
                s_ = i % 2
                v.wait_ge(oin[s_].sem, 32 * (i // 2 + 1))
                if i >= 2:
                    v.wait_ge(toq[s_].sem, 24 * 16 * (i // 2))
                sl3 = slice(s_ * 3072, (s_ + 1) * 3072)
                ch(v.tensor_scalar_mul(oqf[:], otl[:, sl3], sc2[:, 1:2]))
                ch(v.tensor_copy(oqi[:], oqf[:]))
                ch(v.tensor_copy(oqf[:], oqi[:]))
                ch(v.tensor_scalar(oqf[:], oqf[:], 32767.0, -32768.0,
                                   op0=ALU.min, op1=ALU.max))
                ch(v.tensor_scalar_mul(oqb[:, sl3], oqf[:], sc2[:, 0:1]))
                if i >= 2:
                    v.wait_ge(st.sem, 16 * (i - 1))
                v.wait_ge(ljt.sem, 1 + i * 2 + 1)
                ch(v.tensor_add(yl_[:, s_ * D: s_ * D + 512],
                                x3f[:, s_ * D: s_ * D + 512], psA[:, 0:512]))
                v.wait_ge(ljt.sem, 1 + i * 2 + 2)
                ch(v.tensor_add(yl_[:, s_ * D + 512: s_ * D + 1024],
                                x3f[:, s_ * D + 512: s_ * D + 1024],
                                psB[:, 0:512]))
                ch(v.tensor_reduce(mst[:, 0:1], yl_[:, s_ * D:(s_ + 1) * D],
                                   axis=AX.X, op=ALU.add))
                ch(v.memset(mst[:, 2:3], 0.0))
                ch(v.tensor_scalar_mul(mst[:, 1:2], mst[:, 0:1], 1.0 / D))
                v.wait_ge(ljs.sem, i * NSB + 1)
                ch(v.tensor_scalar_mul(mst[:, 3:4], mst[:, 2:3], 1.0 / D))
                ch(v.tensor_tensor(mst[:, 4:5], mst[:, 1:2], mst[:, 1:2],
                                   ALU.mult))
                ch(v.tensor_sub(mst[:, 4:5], mst[:, 3:4], mst[:, 4:5]))
                ch(v.tensor_scalar_add(mst[:, 4:5], mst[:, 4:5], 1e-5))
                v.wait_ge(ljs.sem, i * NSB + 2)
                ch(v.reciprocal(mst[:, 6:7], mst[:, 5:6]))
                ch(v.tensor_tensor(mst[:, 7:8], mst[:, 1:2], mst[:, 6:7],
                                   ALU.mult))
                ch(v.tensor_scalar_mul(mst[:, 7:8], mst[:, 7:8], -1.0))
    return nc


BW = 4605


def band_host(rel, qbase):
    I = np.minimum(4731 - (np.arange(BW)[None, :] + np.arange(128)[:, None]
                           - qbase), 4094)
    return np.ascontiguousarray(
        np.asarray(rel)[:, I].transpose(1, 0, 2).reshape(128, 16 * BW)
    ).astype(np.float32)


def build_LB2():
    nc = bass.Bass("TRN2", target_bir_lowering=False, detect_race_conditions=False)
    x = nc.dram_tensor("x", [TOK, D], F32, kind="ExternalInput").ap()
    qt = nc.dram_tensor("qt", [D, TOK], BF16, kind="ExternalInput").ap()
    ktb = nc.dram_tensor("ktb", [D, L], BF16, kind="ExternalInput").ap()
    vb = nc.dram_tensor("vb", [L, D], BF16, kind="ExternalInput").ap()
    bandh = nc.dram_tensor("bandh", [128, 16 * BW], F32, kind="ExternalInput").ap()
    relch = nc.dram_tensor("relch", [128, 16], F32, kind="ExternalInput").ap()
    ident_h = nc.dram_tensor("ident", [128, 128], F32, kind="ExternalInput").ap()
    x1_o = nc.dram_tensor("x1_o", [TOK, D], F32, kind="ExternalOutput").ap()

    NKT = L // 128

    def tailp(qbp, qb, kt):
        q0 = qbp * 1024 + qb * 512
        return kt * 128 >= q0 + 3971

    steps = [(kt, qb) for kt in range(NKT) for qb in range(2)]
    NS = len(steps)

    pt_prereq = []
    exp_cnt = 0
    tad_cnt = 0
    tail_done_at_pair = []
    for pair in range(8):
        for qbp in range(2):
            for (kt, qb) in steps:
                for hh in range(2):
                    exp_cnt += 1
                    if tailp(qbp, qb, kt):
                        tad_cnt += 1
                        pt_prereq.append(("tad", tad_cnt))
                    else:
                        pt_prereq.append(("expd", exp_cnt))
        tail_done_at_pair.append(tad_cnt)

    with ExitStack() as ctx:
        sb = lambda name, shape, dt: ctx.enter_context(nc.sbuf_tensor(name, shape, dt))
        pt = lambda name, shape: ctx.enter_context(nc.psum_tensor(name, shape, F32))
        qt_s = sb("qt_s", [128, 8 * TOK], BF16)
        osb = sb("osb", [128, NT * D], F32)
        nb128 = sb("nb128", [128, 16], F32)
        relc = sb("relc", [128, 16], F32)
        ident = sb("ident_s", [128, 128], F32)
        kfl = sb("kfl", [128, 4 * 128], BF16)
        vfl = sb("vfl", [128, 4 * 130], BF16)
        ptb = sb("ptb", [128, 8 * 512], BF16)
        bandf = sb("bandf", [128, BW], F32)
        w2 = sb("w2", [128, 4 * BW], BF16)
        oTs = sb("oT", [65, 2 * 512], F32)
        rs = sb("rs", [128, 16], F32)
        xl = sb("xl", [128, 2 * D], F32)
        yl = sb("yl", [128, 2 * D], F32)
        sqs = sb("sqs", [128, D], F32)
        mst = sb("mst", [128, 8], F32)
        ps_st0 = pt("ps_st0", [128, 512])
        ps_st1 = pt("ps_st1", [128, 512])
        ps_av0 = pt("ps_av0", [65, 512])
        ps_av1 = pt("ps_av1", [65, 512])
        ps_av2 = pt("ps_av2", [65, 512])
        ps_av3 = pt("ps_av3", [65, 512])
        ps_tr0 = pt("ps_tr0", [128, 65])
        ps_tr1 = pt("ps_tr1", [128, 65])
        block = ctx.enter_context(nc.Block())
        ps_avs = [ps_av0, ps_av1, ps_av2, ps_av3]
        ps_trs = [ps_tr0, ps_tr1]
        ps_sts = [ps_st0, ps_st1]
        ld = Ctr(nc, "ld")
        vin = Ctr(nc, "vin")
        bld = Ctr(nc, "bld")
        wexp = Ctr(nc, "wexp")
        kv = [Ctr(nc, f"kv{i}") for i in range(4)]
        std = Ctr(nc, "std")
        expd = Ctr(nc, "expd")
        tad = Ctr(nc, "tad")
        avd = Ctr(nc, "avd")
        oev = Ctr(nc, "oev")
        trd = Ctr(nc, "trd")
        osc = Ctr(nc, "osc")
        rec = Ctr(nc, "rec")
        xld = [Ctr(nc, f"xld{i}") for i in range(2)]
        lnv = Ctr(nc, "lnv")
        lns = Ctr(nc, "lns")
        st = Ctr(nc, "st")

        @block.sync
        def _(sync):
            for oc in range(8):
                ld.inc(sync.dma_start(
                    out=qt_s[:, oc * TOK:(oc + 1) * TOK],
                    in_=qt[oc * 128:(oc + 1) * 128, :]), 16)
            ld.inc(sync.dma_start(out=relc[:], in_=relch[:, :]), 16)
            ld.inc(sync.dma_start(out=ident[:], in_=ident_h[:, :]), 16)

        @block.vector
        def _(v):
            ld.wait(v)
            for s in range(4):
                v.memset(vfl[:, s * 130 + 64: s * 130 + 65], 1.0)
                v.memset(vfl[:, s * 130 + 129: s * 130 + 130], 1.0)
            vin.inc(v.tensor_scalar_mul(nb128[:], relc[:], -1.0))

        @block.sync
        def _(sync):
            sync.wait_ge(vin.sem, 1)
            g = 0
            for pair in range(8):
                for hh in range(2):
                    bi = pair * 2 + hh
                    if bi >= 1:
                        sync.wait_ge(wexp.sem, bi)
                    if pair >= 2:
                        sync.wait_ge(tad.sem, tail_done_at_pair[pair - 2])
                    bld.inc(sync.dma_start(
                        out=bandf[:],
                        in_=bandh[:, bi * BW:(bi + 1) * BW]), 16)
                for qbp in range(2):
                    for kt in range(NKT):
                        slot = g % 4
                        if g >= 4:
                            sync.wait_ge(std.sem, 4 * (g - 3))
                            sync.wait_ge(avd.sem, 4 * (g - 3))
                        kv[slot].inc(sync.dma_start(
                            out=kfl[:, slot * 128:(slot + 1) * 128],
                            in_=ktb[pair * 128:(pair + 1) * 128,
                                    kt * 128:(kt + 1) * 128]), 16)
                        dstv = bass.AP(tensor=vfl[:].tensor,
                                       offset=slot * 130,
                                       ap=[[4 * 130, 128], [65, 2], [1, 64]])
                        srcv = bass.AP(tensor=vb.tensor,
                                       offset=kt * 128 * D + pair * 128,
                                       ap=[[D, 128], [64, 2], [1, 64]])
                        kv[slot].inc(sync.dma_start(out=dstv, in_=srcv), 16)
                        g += 1

        @block.tensor
        def _(t):
            ld.wait(t)
            pt_idx = 0
            av_idx = 0
            oev_cnt = 0
            osc_cnt = 0
            g = 0

            def emit_st(pair, qbp, kt, qb, g):
                nonlocal pt_idx
                slot = g % 4
                t.wait_ge(kv[slot].sem, 32 * (g // 4 + 1))
                for hh in range(2):
                    prev = pt_idx + hh - 2
                    if prev >= 0:
                        t.wait_ge(expd.sem, prev + 1)
                    std.inc(t.matmul(
                        ps_sts[hh][:, 0:512],
                        kfl[hh * 64:(hh + 1) * 64, slot * 128:(slot + 1) * 128],
                        qt_s[hh * 64:(hh + 1) * 64,
                             pair * TOK + qbp * 1024 + qb * 512:
                             pair * TOK + qbp * 1024 + (qb + 1) * 512],
                        start=True, stop=True,
                        tile_position=(hh * 64, 0)))
                pt_idx += 2

            def emit_av(pair, qbp, kt, qb):
                nonlocal av_idx
                slot = (av_idx // 4) % 4
                for hh in range(2):
                    kind, cnt = pt_prereq[av_idx]
                    t.wait_ge((tad if kind == "tad" else expd).sem, cnt)
                    ptslot = av_idx % 8
                    avd.inc(t.matmul(
                        ps_avs[qb * 2 + hh][0:65, 0:512],
                        vfl[:, slot * 130 + hh * 65: slot * 130 + (hh + 1) * 65],
                        ptb[:, ptslot * 512:(ptslot + 1) * 512],
                        start=(kt == 0), stop=(kt == NKT - 1)))
                    av_idx += 1

            for pair in range(8):
                for qbp in range(2):
                    if oev_cnt > 0:
                        t.wait_ge(oev.sem, oev_cnt)
                    prev_step = None
                    for (kt, qb) in steps:
                        emit_st(pair, qbp, kt, qb, g)
                        if qb == 1:
                            g += 1
                        if prev_step is not None:
                            emit_av(pair, qbp, *prev_step)
                        prev_step = (kt, qb)
                    emit_av(pair, qbp, *prev_step)
                    for qb in range(2):
                        for hh in range(2):
                            idx = qb * 2 + hh
                            t.wait_ge(oev.sem, oev_cnt + idx + 1)
                            for j in range(4):
                                jj = (idx * 4 + j)
                                if jj >= 2:
                                    t.wait_ge(osc.sem, osc_cnt + jj - 1)
                                trd.inc(t.transpose(
                                    ps_trs[jj % 2][:, 0:65],
                                    oTs[0:65, (idx % 2) * 512 + j * 128:
                                        (idx % 2) * 512 + (j + 1) * 128],
                                    ident[0:65, 0:65]))
                    oev_cnt += 4
                    osc_cnt += 16

        @block.scalar
        def _(s):
            pt_idx = 0
            for pair in range(8):
                for hh in range(2):
                    bi = pair * 2 + hh
                    s.wait_ge(bld.sem, 16 * (bi + 1))
                    wexp.inc(s.activation(
                        out=w2[:, (pair % 2) * 2 * BW + hh * BW:
                               (pair % 2) * 2 * BW + (hh + 1) * BW],
                        in_=bandf[:], func=AF.Exp,
                        bias=nb128[:, bi:bi + 1]))
                for qbp in range(2):
                    for (kt, qb) in steps:
                        for hh in range(2):
                            s.wait_ge(std.sem, pt_idx + 1)
                            if pt_idx >= 8:
                                s.wait_ge(avd.sem, pt_idx - 7)
                            slot = pt_idx % 8
                            expd.inc(s.activation(
                                out=ptb[:, slot * 512:(slot + 1) * 512],
                                in_=ps_sts[hh][:, 0:512],
                                func=AF.Exp, scale=0.125))
                            pt_idx += 1

        @block.vector
        def _(v):
            pt_idx = 0
            tidx = 0
            trd_cnt = 0
            for pi in range(16):
                pair, qbp = divmod(pi, 2)
                for (kt, qb) in steps:
                    q0 = qbp * 1024 + qb * 512
                    for hh in range(2):
                        if tailp(qbp, qb, kt):
                            v.wait_ge(expd.sem, pt_idx + 1)
                            v.wait_ge(wexp.sem, pair * 2 + hh + 1)
                            slot = pt_idx % 8
                            base = kt * 128 - q0 - 3971
                            rev = bass.AP(
                                tensor=w2[:].tensor,
                                offset=(pair % 2) * 2 * BW + hh * BW
                                + base + 511,
                                ap=[[4 * BW, 128], [-1, 512]])
                            tad.inc(v.tensor_tensor(
                                ptb[:, slot * 512:(slot + 1) * 512],
                                ptb[:, slot * 512:(slot + 1) * 512],
                                rev, ALU.mult))
                            tidx += 1
                        pt_idx += 1
                av_total = (pi + 1) * 2 * NS
                for qb in range(2):
                    for hh in range(2):
                        idx = qb * 2 + hh
                        v.wait_ge(avd.sem, av_total)
                        oev.inc(v.tensor_copy(
                            oTs[0:65, (idx % 2) * 512:(idx % 2 + 1) * 512],
                            ps_avs[idx][0:65, 0:512]))
                        for j in range(4):
                            trd_cnt += 1
                            v.wait_ge(trd.sem, trd_cnt)
                            ps = ps_trs[(idx * 4 + j) % 2]
                            rc_ = osc.n % 16
                            if osc.n >= 16:
                                v.wait_ge(osc.sem, osc.n - 15)
                            rec.inc(v.reciprocal(rs[:, rc_:rc_ + 1], ps[:, 64:65]))
                            v.wait_ge(rec.sem, rec.n)
                            ti = qbp * 8 + qb * 4 + j
                            hcol = (pair * 2 + hh) * 64
                            osc.inc(v.tensor_scalar_mul(
                                osb[:, ti * D + hcol: ti * D + hcol + 64],
                                ps[:, 0:64], rs[:, rc_:rc_ + 1]))

        @block.sync
        def _(sync):
            sync.wait_ge(osc.sem, osc.n)
            for i in range(NT):
                sl = i % 2
                if i >= 2:
                    sync.wait_ge(lns.sem, (i - 2) * 3 + 3)
                    st.inc(sync.dma_start(
                        out=x1_o[(i - 2) * 128:(i - 1) * 128, :],
                        in_=yl[:, sl * D:(sl + 1) * D]), 16)
                    sync.wait_ge(lnv.sem, (i - 2) * 11 + 11)
                xld[sl].inc(sync.dma_start(
                    out=xl[:, sl * D:(sl + 1) * D],
                    in_=x[i * 128:(i + 1) * 128, :]), 16)
            for i in (NT - 2, NT - 1):
                sync.wait_ge(lns.sem, i * 3 + 3)
                st.inc(sync.dma_start(
                    out=x1_o[i * 128:(i + 1) * 128, :],
                    in_=yl[:, (i % 2) * D:(i % 2 + 1) * D]), 16)
            st.wait(sync)

        @block.vector
        def _(v):
            def ch(inst):
                lnv.inc(inst)
                v.wait_ge(lnv.sem, lnv.n)
                return inst
            for i in range(NT):
                sl = i % 2
                v.wait_ge(xld[sl].sem, 16 * (i // 2 + 1))
                if i >= 2:
                    v.wait_ge(st.sem, 16 * (i - 1))
                ch(v.tensor_add(yl[:, sl * D:(sl + 1) * D],
                                xl[:, sl * D:(sl + 1) * D],
                                osb[:, i * D:(i + 1) * D]))
                ch(v.tensor_reduce(mst[:, 0:1], yl[:, sl * D:(sl + 1) * D],
                                   axis=AX.X, op=ALU.add))
                ch(v.memset(mst[:, 2:3], 0.0))
                ch(v.tensor_scalar_mul(mst[:, 1:2], mst[:, 0:1], 1.0 / D))
                v.wait_ge(lns.sem, i * 3 + 1)
                ch(v.tensor_scalar_mul(mst[:, 3:4], mst[:, 2:3], 1.0 / D))
                ch(v.tensor_tensor(mst[:, 4:5], mst[:, 1:2], mst[:, 1:2],
                                   ALU.mult))
                ch(v.tensor_sub(mst[:, 4:5], mst[:, 3:4], mst[:, 4:5]))
                ch(v.tensor_scalar_add(mst[:, 4:5], mst[:, 4:5], 1e-5))
                v.wait_ge(lns.sem, i * 3 + 2)
                ch(v.reciprocal(mst[:, 6:7], mst[:, 5:6]))
                ch(v.tensor_tensor(mst[:, 7:8], mst[:, 1:2], mst[:, 6:7],
                                   ALU.mult))
                ch(v.tensor_scalar_mul(mst[:, 7:8], mst[:, 7:8], -1.0))

        @block.scalar
        def _(s):
            for i in range(NT):
                sl = i % 2
                s.wait_ge(lnv.sem, i * 11 + 4)
                lns.inc(s.activation(out=sqs[:], in_=yl[:, sl * D:(sl + 1) * D],
                                     func=AF.Square, accum_out=mst[:, 2:3]))
                s.wait_ge(lnv.sem, i * 11 + 8)
                lns.inc(s.activation(out=mst[:, 5:6], in_=mst[:, 4:5],
                                     func=AF.Sqrt))
                s.wait_ge(lnv.sem, i * 11 + 11)
                lns.inc(s.activation(out=yl[:, sl * D:(sl + 1) * D],
                                     in_=yl[:, sl * D:(sl + 1) * D],
                                     func=AF.Identity,
                                     bias=mst[:, 7:8], scale=mst[:, 6:7]))

    return nc


def kernel(**inputs):
    x = np.asarray(inputs["L0_attn_x"], dtype=np.float32)
    xs = shard_x(x)
    ident = np.eye(128, dtype=np.float32)
    ones = np.ones((1, 128), np.float32)
    B = 2

    resA = host_LA(inputs, xs)
    ktb = {b: np.concatenate([resA[b * 4 + i]["kt_o"] for i in range(4)],
                             axis=1) for b in range(B)}
    vbt = {b: np.concatenate([resA[b * 4 + i]["v_o"] for i in range(4)],
                             axis=0) for b in range(B)}

    rel = np.asarray(inputs["L0_attn_rel_emb"])
    relc = np.tile(np.ascontiguousarray(rel[:, 4094:4095]).T,
                   (128, 1)).astype(np.float32)
    ncB = build_LB2()
    imsB = [{"x": xs[c], "qt": resA[c]["qt_o"], "ktb": ktb[c // 4],
             "vb": vbt[c // 4], "bandh": band_host(rel, (c % 4) * 2048),
             "relch": relc, "ident": ident} for c in range(8)]
    resB = run_bass_kernel_spmd(ncB, imsB, core_ids=list(range(8))).results
    x1 = {c: resB[c]["x1_o"] for c in range(8)}

    gwb = float(np.asarray(inputs["L1_attn_gw_b"]).reshape(-1)[0])
    swb = float(np.asarray(inputs["L1_attn_sw_b"]).reshape(-1)[0])
    gw_w = np.asarray(inputs["L1_attn_gw_w"])
    sw_w = np.asarray(inputs["L1_attn_sw_w"])
    wgbd = np.zeros((1024, 16), np.float32)
    wsbd = np.zeros((1024, 16), np.float32)
    for h in range(16):
        wgbd[h * 64:(h + 1) * 64, h] = gw_w[:, 0]
        wsbd[h * 64:(h + 1) * 64, h] = sw_w[:, 0]
    baseC = {
        "wrh": w_moving(np.ascontiguousarray(np.asarray(inputs["L0_ffn_W_r"]).T)),
        "wkh": w_moving(np.ascontiguousarray(np.asarray(inputs["L0_ffn_W_k"]).T)),
        "wvh": w_moving(np.ascontiguousarray(np.asarray(inputs["L0_ffn_W_v"]).T)),
        "murh": np.tile(np.asarray(inputs["L0_ffn_mu_r"]).reshape(1, -1),
                        (128, 1)).astype(np.float32),
        "mukh": np.tile(np.asarray(inputs["L0_ffn_mu_k"]).reshape(1, -1),
                        (128, 1)).astype(np.float32),
        "q1wh": w_moving(np.asarray(inputs["L1_attn_q1_w"])),
        "k1wh": w_moving(np.asarray(inputs["L1_attn_k1_w"])),
        "k2wh": w_moving(np.asarray(inputs["L1_attn_k2_w"])),
        "q1bh": np.asarray(inputs["L1_attn_q1_b"]).reshape(1, -1).astype(np.float32),
        "k1bh": np.asarray(inputs["L1_attn_k1_b"]).reshape(1, -1).astype(np.float32),
        "k2bh": np.asarray(inputs["L1_attn_k2_b"]).reshape(1, -1).astype(np.float32),
        "wgbdh": w_moving(wgbd),
        "wsbdh": w_moving(wsbd),
        "onesh": ones, "identh": ident,
        "gswbh": np.tile(np.array([[gwb, swb]], np.float32), (128, 1)),
    }
    ncC = build_LC(gwb, swb)
    imsC = []
    for c in range(8):
        pl = np.zeros((1, 1024), np.float32) if c % 4 == 0 else \
            np.ascontiguousarray(x1[c - 1][-1:]).astype(np.float32)
        imsC.append(dict(baseC, x1h=x1[c], plh=pl))
    resC = run_bass_kernel_spmd(ncC, imsC, core_ids=list(range(8))).results

    gs = np.zeros((16, 8), np.float32)
    for c in range(8):
        gs[:, c] = resC[c]["gsum_o"][:, 0]
    maskU = np.triu(np.ones((128, 128), np.float32))
    ncD = build_LD()
    imsD = []
    for c in range(8):
        pm = np.zeros((16, 8), np.float32)
        for c2 in range(8):
            if c2 // 4 == c // 4 and c2 < c:
                pm[:, c2] = 1.0
        imsD.append({"x2h": resC[c]["x2_o"], "q1h": resC[c]["q1_o"],
                     "k1h": resC[c]["k1_o"], "lgth": resC[c]["lgt_o"],
                     "rsh": resC[c]["rs_o"], "gsh": gs, "pmh": pm,
                     "maskh": maskU, "identh": ident})
    resD = run_bass_kernel_spmd(ncD, imsD, core_ids=list(range(8))).results

    dsall = np.zeros((128, 8 * 512), np.float32)
    for c in range(8):
        dsall[:, c * 512:(c + 1) * 512] = resD[c]["sa_o"][NT - 1]
    ncE = build_LE()
    imsE = []
    for c in range(8):
        pm = np.zeros((128, 8), np.float32)
        for c2 in range(8):
            if c2 // 4 == c // 4 and c2 < c:
                pm[:, c2] = 1.0
        if c % 4 == 3:
            x2n1 = np.zeros((1, 1024), np.float32)
        else:
            x2n1 = np.ascontiguousarray(resC[c + 1]["x2_o"][0:1])
        em = np.ones((128, NT), np.float32)
        if c % 4 == 3:
            em[127, NT - 1] = 0.0
        imsE.append({"x2h": resC[c]["x2_o"], "q1h": resC[c]["q1_o"],
                     "k2h": resC[c]["k2_o"], "gch": resD[c]["gc_o"],
                     "o1ih": resD[c]["o1_o"], "sah": resD[c]["sa_o"],
                     "dsh": dsall, "pmh": pm, "x2nh": x2n1, "emh": em,
                     "maskh": maskU})
    resE = run_bass_kernel_spmd(ncE, imsE, core_ids=list(range(8))).results

    dmall = np.zeros((128, 8 * 512), np.float32)
    for c in range(8):
        dmall[:, c * 512:(c + 1) * 512] = resE[c]["ma_o"][NT - 1]
    ncF = build_LF()
    imsF = []
    for c in range(8):
        pm = np.zeros((128, 8), np.float32)
        for c2 in range(8):
            if c2 // 4 == c // 4 and c2 < c:
                pm[:, c2] = 1.0
        if c % 4 == 0:
            q1l = np.zeros((1, 1024), BF16NP)
            flag = np.array([[0.0]], np.float32)
        else:
            q1l = np.ascontiguousarray(resC[c - 1]["q1_o"][-1:])
            flag = np.array([[1.0]], np.float32)
        imsF.append({"x2h": resC[c]["x2_o"], "o1h": resE[c]["o1_o"],
                     "o2ih": resE[c]["o2i_o"], "mah": resE[c]["ma_o"],
                     "dmh": dmall, "pmh": pm, "q1h": resC[c]["q1_o"],
                     "q1lh": q1l, "flagh": flag,
                     "cpwh": w_moving(np.asarray(inputs["L1_attn_cp_w"])),
                     "cpbh": np.asarray(inputs["L1_attn_cp_b"]).reshape(1, -1).astype(np.float32),
                     "onesh": ones, "identh": ident})
    resF = run_bass_kernel_spmd(ncF, imsF, core_ids=list(range(8))).results

    am8 = np.zeros((1, 8), np.float32)
    for c in range(8):
        am8[0, c] = resF[c]["amax_o"][0, 0]
    ncG = build_LG()
    wuf = np.asarray(inputs["L1_ffn_w_up"]).astype(np.float32)
    wgf = np.asarray(inputs["L1_ffn_w_gate"]).astype(np.float32)
    imsG = [{"x3h": resF[c]["x3_o"], "am8h": am8, "wuh": wuf, "wgh": wgf,
             "identh": ident} for c in range(8)]
    resG = run_bass_kernel_spmd(ncG, imsG, core_ids=list(range(8))).results

    um8 = np.zeros((1, 8), np.float32)
    gm8 = np.zeros((1, 8), np.float32)
    for c in range(8):
        um8[0, c] = resG[c]["mx_o"][0, 0]
        gm8[0, c] = resG[c]["mx_o"][0, 1]
    ncH = build_LH()
    imsH = [{"uph": resG[c]["up_o"], "gth": resG[c]["gt_o"], "um8h": um8,
             "gm8h": gm8, "identh": ident} for c in range(8)]
    resH = run_bass_kernel_spmd(ncH, imsH, core_ids=list(range(8))).results

    m8 = np.zeros((1, 8), np.float32)
    for c in range(8):
        m8[0, c] = resH[c]["mx_o"][0, 0]
    ncI = build_LI()
    imsI = [{"uqh": resH[c]["uq_o"], "g2h": resH[c]["g2_o"], "m8h": m8,
             "identh": ident} for c in range(8)]
    resI = run_bass_kernel_spmd(ncI, imsI, core_ids=list(range(8))).results

    m8o = np.zeros((1, 8), np.float32)
    for c in range(8):
        m8o[0, c] = resI[c]["mx_o"][0, 0]
    ncJ = build_LJ()
    wdf = np.asarray(inputs["L1_ffn_w_down"]).astype(np.float32)
    imsJ = [{"oh": resI[c]["o_o"], "m8h": m8o, "x3h": resF[c]["x3_o"],
             "wdh": wdf, "identh": ident} for c in range(8)]
    resJ = run_bass_kernel_spmd(ncJ, imsJ, core_ids=list(range(8))).results

    out = np.empty((2, 8192, 1024), np.float32)
    for c in range(8):
        b, t0 = divmod(c, 4)
        out[b, t0 * TOK:(t0 + 1) * TOK] = resJ[c]["x4_o"]
    return out
